# version 10
# baseline (speedup 1.0000x reference)
import numpy as np
import concourse.bass as bass
import concourse.mybir as mybir
import bass_rust as _bass_rust
from concourse.bass_utils import run_bass_kernel_spmd
from concourse.library_config import all_libraries, standard as _std_lib
from concourse.bass_isa import ReduceOp

F32 = mybir.dt.float32
F16 = mybir.dt.float16
BF16 = mybir.dt.bfloat16
I32 = mybir.dt.int32
AF = mybir.ActivationFunctionType
ALU = mybir.AluOpType

T = 2048
D = 1024
DI = 2048
NST = 16
HF0 = 3072
HF1 = 4096
NH = 16
HS = 64
KVR = 128
HD = NH * HS
NCH = T // 512
DT_TILES = DI // 128
D_TILES = D // 128
H0_TILES = HF0 // 128
H1_TILES = HF1 // 128
LNEPS = 1e-5
EXP_SHIFT = 12.0

SB_BASE = 16576
OFF_A = SB_BASE
OFF_B = SB_BASE + 65536
OFF_C = SB_BASE + 131072
OFF_W = SB_BASE + 155648
OFF_M = SB_BASE + 200704
ARENA = 200704 + 6144
SB_LIMIT = SB_BASE + ARENA


class Eng:
    def __init__(self, name, stream, step=1):
        self.name = name
        self.stream = stream
        self.step = step
        self.count = 0
        self.sem = None
        self.last_waited = {}
        self.dirty = False

    phase = [None]

    def emit(self, fn, tok=False):
        if fn.__closure__:
            for nm, cell in zip(fn.__code__.co_freevars, fn.__closure__):
                try:
                    val = cell.cell_contents
                except ValueError:
                    continue
                assert not isinstance(val, (int, float, slice, str, range)), (
                    f"lambda closes over loop-variable {nm!r}={val!r}; "
                    f"bind it as a default arg")
        rec = ["op", fn, None, Eng.phase[0]]
        self.stream.append(rec)
        self.dirty = True
        if tok or self.step == 16:
            return self.token()
        return None

    def token(self):
        for rec in reversed(self.stream):
            if rec[0] == "op":
                break
        else:
            raise AssertionError("no op to token")
        if rec[2] is not None:
            self.dirty = False
            return rec[2]
        self.count += self.step
        rec[2] = (self, self.count)
        self.dirty = False
        return (self, self.count)

    def cur_q(self):
        return (self, self.count)

    def cur(self):
        if self.dirty:
            return self.token()
        return (self, self.count)

    def wait(self, tok):
        if tok is None:
            return
        prod, val = tok
        if prod is self or (prod.stream is self.stream and prod.step == self.step):
            return
        assert val <= prod.count, f"future token {prod.name}:{val}"
        if self.last_waited.get(prod.name, -1) >= val:
            return
        self.last_waited[prod.name] = val
        self.stream.append(["wait", (prod, val), None, Eng.phase[0]])


class Ctx:
    def __init__(self, nc):
        self.nc = nc
        self.streams = {"t": [], "v": [], "s": [], "g": [], "y": []}
        self.t = Eng("t", self.streams["t"])
        self.v = Eng("v", self.streams["v"])
        self.s = Eng("s", self.streams["s"])
        self.g = Eng("g", self.streams["g"])
        self.y = Eng("y", self.streams["y"], 16)
        self.sd = Eng("sd", self.streams["s"], 16)
        self.gd = Eng("gd", self.streams["g"], 16)
        self.all_engs = [self.t, self.v, self.s, self.g, self.y, self.sd,
                         self.gd]
        self.tn = {}
        self._nsb = 0

    def sb(self, name, shape, dtype, offset):
        nbytes = int(np.prod(shape[1:])) * mybir.dt.size(dtype)
        assert offset + nbytes <= SB_LIMIT, f"{name}: overflow {offset + nbytes}"
        h = self.nc.alloc_sbuf_tensor_at(name, shape, dtype, offset=offset)
        self.tn[name] = h
        return h

    def barrier(self):
        toks = [e.cur() for e in self.all_engs if e.count > 0 or e.dirty]
        for e in [self.t, self.v, self.s, self.g, self.y]:
            for tk in toks:
                e.wait(tk)

    def realize(self):
        nc = self.nc
        self._guards = []
        for e in self.all_engs:
            gd = nc.semaphore(f"sem_{e.name}")
            e.sem = gd.__enter__()
            self._guards.append(gd)
        bg = nc.Block()
        block = bg.__enter__()

        def play(stream):
            def body(eng_obj):
                curph, curid = None, None
                for rec in stream:
                    kind, payload, tok = rec[0], rec[1], rec[2]
                    ph = rec[3] if len(rec) > 3 else None
                    if ph != curph:
                        if curph is not None:
                            nc.leave_named_scope(curph, curid, False)
                        if ph is not None:
                            curid, _ = nc.enter_named_scope(ph, False)
                        curph = ph
                    if kind == "op":
                        inst = payload(eng_obj)
                        if tok is not None:
                            prod, _ = tok
                            inst.then_inc(prod.sem, prod.step)
                    else:
                        prod, val = payload
                        eng_obj.wait_ge(prod.sem, val)
                if curph is not None:
                    nc.leave_named_scope(curph, curid, False)
            return body

        block.tensor(play(self.streams["t"]))
        block.vector(play(self.streams["v"]))
        block.scalar(play(self.streams["s"]))
        block.gpsimd(play(self.streams["g"]))
        block.sync(play(self.streams["y"]))
        bg.__exit__(None, None, None)
        for gd in self._guards:
            gd.__exit__(None, None, None)


def _lower_extended(nc):
    m = {}
    for lib in all_libraries:
        for it in lib.instructions:
            m[it] = m.get(it, 0) | (1 << lib.index)
    _bass_rust.insert_library_loads(nc, m, len(all_libraries), _std_lib.index)
    mybir.codegen_inst_isa_subclasses(nc)


class WStream:

    def __init__(self, P, name, w_hbm, K, N, bw, off, pair=True, int_src=False):
        self.P = P
        c = P.c
        self.w = w_hbm
        self.K, self.N, self.bw = K, N, bw
        self.pair = pair
        self.int_src = int_src
        nb = 2
        self.nb = nb
        self.f32 = None
        sz_h = bw * 2
        cur = off
        self.hbuf = [c.sb(f"{name}_h{i}", [128, bw], BF16, cur + i * sz_h)
                     for i in range(nb)]
        cur += nb * sz_h
        if pair:
            self.lbuf = [c.sb(f"{name}_l{i}", [128, bw], BF16, cur + i * sz_h)
                         for i in range(nb)]
            cur += nb * sz_h
            self.f32 = c.sb(f"{name}_f", [128, bw], F32, cur)
            cur += bw * 4
        self.end = cur
        self.loaded = {}
        self.order = []
        self.seq = 0
        self.slot_busy = [None] * nb

    def load(self, ktile, moff, consumer_eng):
        key = (ktile, moff)
        if key in self.loaded:
            return self.loaded[key]
        P, c = self.P, self.P.c
        slot = self.seq % self.nb
        self.seq += 1
        prev = self.slot_busy[slot]
        k0, m0 = ktile * 128, moff
        src = self.w[k0:k0 + 128, m0:m0 + self.bw]
        if self.pair:
            def ld_f32(e, src=src):
                return e.dma_start(out=self.f32[:], in_=src)
            if prev is not None:
                c.y.wait(prev)
                c.gd.wait(prev)
            tk_f = c.y.emit(ld_f32, tok=True)
            tk_h = c.gd.emit(lambda e, s_=slot, src=src: e.dma_start(
                out=self.hbuf[s_][:], in_=src), tok=True)
            c.v.wait(tk_f)
            c.v.wait(tk_h)
            tk_l = c.v.emit(lambda e, s_=slot: e.tensor_sub(
                self.lbuf[s_][:], self.f32[:], self.hbuf[s_][:]), tok=True)
            res = (self.hbuf[slot], self.lbuf[slot] if self.pair else None, tk_l)
        else:
            if prev is not None:
                c.gd.wait(prev)
            tk_h = c.gd.emit(lambda e, s_=slot, src=src: e.dma_start(
                out=self.hbuf[s_][:], in_=src), tok=True)
            res = (self.hbuf[slot], None, tk_h)
        self.loaded = {key: res}
        self.slot_key = key
        return res

    def mark_done(self, tok):
        slot = (self.seq - 1) % self.nb
        self.slot_busy[slot] = tok


def build_program(upto="all"):
    nc = bass.Bass("TRN2", target_bir_lowering=False,
                   detect_race_conditions=False)
    nc.alloc_sbuf_tensor("arena_blk", [128, ARENA], mybir.dt.uint8)
    c = Ctx(nc)
    hbm = {}

    def hin(name, shape, dtype=F32):
        hbm[name] = nc.dram_tensor(name, shape, dtype, kind="ExternalInput").ap()

    hin("x", [D, T]); hin("ln0_scale", [D, 1]); hin("ln0_bias", [D, 1])
    hin("w_in", [D, 2 * DI]); hin("conv_w", [DI, 3]); hin("conv_b", [DI, 1])
    hin("w_dt", [DI, DI]); hin("b_dt", [DI, 1]); hin("a_log", [DI, NST])
    hin("d_param", [DI, 1]); hin("w_bc", [DI, 2 * NST]); hin("w_out0", [DI, D])
    hin("wg0", [D, HF0], I32); hin("wu0", [D, HF0], I32); hin("wd0", [HF0, D], I32)
    hin("w_kvd", [D, KVR]); hin("w_kvu", [KVR, 2 * HD]); hin("w_q", [D, HD])
    hin("w_out1", [HD, D]); hin("w1", [D, HF1]); hin("cv1_w", [HF1, 3])
    hin("cv1_b", [HF1, 1]); hin("w2", [HF1, D])

    x_out = nc.dram_tensor("x_out", [D, T], F32, kind="ExternalOutput").ap()

    scr = {}

    def hscr(name, shape, dtype=F32):
        scr[name] = nc.dram_tensor(name, shape, dtype).ap()

    hscr("s_x1", [D, T])
    hscr("s_xn2h", [D, T], BF16); hscr("s_xn2l", [D, T], BF16)
    hscr("s_xbh", [DI, T], BF16); hscr("s_xbl", [DI, T], BF16)
    hscr("s_zh", [DI, T], BF16); hscr("s_zl", [DI, T], BF16)
    hscr("s_dt", [DI, T])
    hscr("s_yvh", [DI, T], BF16); hscr("s_yvl", [DI, T], BF16)
    hscr("s_q", [HD, T]); hscr("s_vt", [T, HD]); hscr("s_y1", [HD, T])
    hscr("s_den", [NH, T])
    hscr("s_xq1", [D, T], BF16); hscr("s_sxi1", [1, T])
    hscr("s_h1h", [HF1, T], BF16); hscr("s_h1l", [HF1, T], BF16)
    hscr("s_sxi2", [1, T])
    hscr("s_wq1", [D, HF1], BF16); hscr("s_wq2", [HF1, D], BF16)
    hscr("s_dbg", [1, 4 * T])

    P = Prog(c, hbm, scr, x_out)
    P.build(upto)
    c.realize()
    _lower_extended(nc)
    return nc, P


class Prog:
    def __init__(self, c, hbm, scr, x_out):
        self.c = c
        self.hbm = hbm
        self.scr = scr
        self.x_out = x_out
        self.nc = c.nc
        self.ps = [self.nc.alloc_psum_tensor(f"ps{i}", [128, 512], F32)
                   for i in range(8)]

    def consts_setup(self):
        c = self.c
        h = self.hbm
        M = OFF_M

        def msb(name, shape, dtype=F32):
            nonlocal M
            t_ = c.sb(name, shape, dtype, M)
            M += int(np.prod(shape[1:])) * mybir.dt.size(dtype)
            M = (M + 31) & ~31
            return t_

        self.a_cols = msb("a_cols", [128, DT_TILES * NST])
        self.cw = [msb(f"cw{i}", [128, DT_TILES]) for i in range(3)]
        self.cb0 = msb("cb0", [128, DT_TILES])
        self.bdt = msb("bdt", [128, DT_TILES])
        self.dprm = msb("dprm", [128, DT_TILES])
        self.ln0s = msb("ln0s", [128, D_TILES])
        self.ln0b = msb("ln0b", [128, D_TILES])
        self.c1w = [msb(f"c1w{i}", [128, H1_TILES]) for i in range(3)]
        self.c1b = msb("c1b", [128, H1_TILES])
        self.ident = msb("ident", [128, 128])
        self.ones_col = msb("ones_col", [128, 1])
        self.iota_f = msb("iota_f", [128, 512])
        self.iota_p = msb("iota_p", [128, 1])
        self.cellA = msb("cellA", [1, 8])
        self.zcol = msb("zcol", [128, 1])
        self.s1col4 = msb("s1col4", [128, 6])
        self.state = msb("state", [128, DT_TILES * NST])
        assert M <= SB_LIMIT, f"M overflow {M}"

        y, g, v, s = c.y, c.g, c.v, c.s

        def col_load(dst, src_col, tiles):
            nc_ = src_col.shape[-1]
            return y.emit(lambda e, d=dst, s_=src_col, n=tiles, nc_=nc_:
                          e.dma_start(
                out=d[:, 0:n * nc_].rearrange("b (a c) -> b a c", c=nc_),
                in_=s_.rearrange("(a b) c -> b a c", b=128),
                allow_slow_non_contiguous=True), tok=True)

        tks = [col_load(self.a_cols, h["a_log"], DT_TILES)]
        for i in range(3):
            tks.append(col_load(self.cw[i], h["conv_w"][:, i:i + 1], DT_TILES))
            tks.append(col_load(self.c1w[i], h["cv1_w"][:, i:i + 1], H1_TILES))
        tks.append(col_load(self.cb0, h["conv_b"], DT_TILES))
        tks.append(col_load(self.bdt, h["b_dt"], DT_TILES))
        tks.append(col_load(self.dprm, h["d_param"], DT_TILES))
        tks.append(col_load(self.ln0s, h["ln0_scale"], D_TILES))
        tks.append(col_load(self.ln0b, h["ln0_bias"], D_TILES))
        tks.append(col_load(self.c1b, h["cv1_b"], H1_TILES))
        s.wait(c.y.cur())
        t_a = s.emit(lambda e: e.activation(
            out=self.a_cols[:], in_=self.a_cols[:], func=AF.Exp), tok=True)
        v.wait(t_a)
        v.emit(lambda e: e.tensor_scalar_mul(self.a_cols[:], self.a_cols[:], -1.0))
        gt = g.emit(lambda e: e.iota(self.iota_f[:], [[1, 512]],
                                     channel_multiplier=0,
                                     allow_small_or_imprecise_dtypes=True), tok=True)
        gt2 = g.emit(lambda e: e.iota(self.iota_p[:], [[1, 1]],
                                      channel_multiplier=1,
                                      allow_small_or_imprecise_dtypes=True), tok=True)
        v.wait(gt); v.wait(gt2)
        v.emit(lambda e: e.memset(self.ones_col[:], 1.0))
        v.emit(lambda e: e.tensor_scalar(
            self.ident[:], self.iota_f[:, 0:128], self.iota_p[:, 0:1], None,
            ALU.is_equal))
        v.emit(lambda e: e.memset(self.zcol[:], 0.0))
        c.barrier()

    def build_masks(self, masks):
        v = self.c.v
        for m in range(4):
            v.emit(lambda e, m=m: e.tensor_scalar(
                masks[:, 512 * m:512 * (m + 1)], self.iota_f[:],
                float(128 * m), self.iota_p[:, 0:1], ALU.subtract, ALU.is_lt))
            v.emit(lambda e, m=m: e.tensor_scalar(
                masks[:, 512 * m:512 * (m + 1)],
                masks[:, 512 * m:512 * (m + 1)], -1e30, -EXP_SHIFT,
                ALU.mult, ALU.add))

    def ln_rows(self, tiles_fn, n_tiles, dim, rows, sq_tmp, rowtmp):
        c = self.c
        t, v, s = c.t, c.v, c.s
        for ch in range(NCH):
            ps_s, ps_q = self.ps[4 + (ch % 2) * 2], self.ps[5 + (ch % 2) * 2]
            sl = slice(512 * ch, 512 * (ch + 1))
            for i in range(n_tiles):
                src = tiles_fn(i)
                t.emit(lambda e, src=src, sl=sl, i=i, n=n_tiles: e.matmul(
                    ps_s[0:1, :], self.ones_col[:], src[:, sl],
                    start=(i == 0), stop=(i == n - 1)))
                stk = s.emit(lambda e, src=src, sl=sl: e.activation(
                    out=sq_tmp[:], in_=src[:, sl], func=AF.Square), tok=True)
                t.wait(stk)
                ttk = t.emit(lambda e, i=i, n=n_tiles: e.matmul(
                    ps_q[0:1, :], self.ones_col[:], sq_tmp[:],
                    start=(i == 0), stop=(i == n - 1)), tok=True)
                s.wait(ttk)
            tk_mm = c.t.cur()
            v.wait(tk_mm)
            v.emit(lambda e, sl=sl, dim=dim: e.tensor_scalar_mul(
                rows[0:1, sl], ps_s[0:1, :], 1.0 / dim))
            v.emit(lambda e, sl=sl: e.tensor_mul(
                rowtmp[0:1, 0:512], rows[0:1, sl], rows[0:1, sl]))
            v.emit(lambda e, sl=sl, dim=dim: e.scalar_tensor_tensor(
                rowtmp[0:1, 0:512], ps_q[0:1, :], 1.0 / dim,
                rowtmp[0:1, 0:512], ALU.mult, ALU.subtract))
            tvv = v.emit(lambda e: e.tensor_scalar_add(
                rowtmp[0:1, 0:512], rowtmp[0:1, 0:512], LNEPS), tok=True)
            s.wait(tvv)
            tss = s.emit(lambda e: e.activation(
                out=rowtmp[0:1, 0:512], in_=rowtmp[0:1, 0:512],
                func=AF.Sqrt), tok=True)
            v.wait(tss)
            v.emit(lambda e, ch=ch: e.reciprocal(
                rows[0:1, T + 512 * ch:T + 512 * (ch + 1)],
                rowtmp[0:1, 0:512]))
            tvt = c.v.cur()
            t.wait(tvt)

    def split_pair(self, src_ap, h_ap, l_ap, tok=False):
        v = self.c.v
        v.emit(lambda e: e.tensor_copy(h_ap, src_ap))
        return v.emit(lambda e: e.scalar_tensor_tensor(
            l_ap, h_ap, -1.0, src_ap, ALU.mult, ALU.add), tok=tok)

    def ph_ln0_xn(self):
        c = self.c
        t, v, s, g, y, sd = c.t, c.v, c.s, c.g, c.y, c.sd
        self.xres = [c.sb(f"xres{i}", [128, T], F32, OFF_A + i * 8192)
                     for i in range(D_TILES)]
        self.xnh = [c.sb(f"xnh{i}", [128, T], BF16, OFF_B + i * 8192)
                    for i in range(D_TILES)]
        self.xnl = [c.sb(f"xnl{i}", [128, T], BF16, OFF_B + i * 8192 + 4096)
                    for i in range(D_TILES)]
        sq_tmp = c.sb("p1_sq", [128, 512], F32, OFF_C)
        rows = c.sb("p1_rows", [1, 2 * T], F32, OFF_C + 2048)
        bc0 = c.sb("p1_bc0", [128, 512], F32, OFF_C + 2048 + 16384)
        bc1 = c.sb("p1_bc1", [128, 512], F32, OFF_C + 2048 + 16384 + 2048)
        xn_t = c.sb("p1_xnt", [128, 512], F32, OFF_W)
        xn2_t = c.sb("p1_xn2t", [128, 512], F32, OFF_W + 2048)
        rowtmp = c.sb("p1_rowtmp", [1, 512], F32, OFF_W + 16384)
        dbgrow = c.sb("p1_dbgrow", [1, T], F32, OFF_W + 20480)
        h_t = c.sb("p1_ht", [128, 512], BF16, OFF_W + 4096)
        l_t = c.sb("p1_lt", [128, 512], BF16, OFF_W + 4096 + 1024)

        for i in range(D_TILES):
            y.emit(lambda e, i=i: e.dma_start(
                out=self.xres[i][:], in_=self.hbm["x"][128 * i:128 * (i + 1), :]),
                tok=True)
        c.barrier()
        self.ln_rows(lambda i: self.xres[i], D_TILES, D, rows, sq_tmp, rowtmp)
        c.barrier()
        y.emit(lambda e: e.dma_start(
            out=self.scr["s_dbg"][0:1, 0:2 * T], in_=rows[0:1, 0:2 * T]),
            tok=True)
        c.barrier()
        for ch in range(NCH):
            sl = slice(512 * ch, 512 * (ch + 1))
            g.wait(c.v.cur())
            tb0 = g.emit(lambda e, sl=sl: e.partition_broadcast(
                bc0[:], rows[0:1, sl]), tok=True)
            tb1 = g.emit(lambda e, ch=ch: e.partition_broadcast(
                bc1[:], rows[0:1, T + 512 * ch:T + 512 * (ch + 1)]), tok=True)
            v.wait(tb0); v.wait(tb1)
            ps_s, ps_q = self.ps[0], self.ps[1]
            for i in range(D_TILES):
                v.emit(lambda e, i=i, sl=sl: e.tensor_sub(
                    xn_t[:], self.xres[i][:, sl], bc0[:]))
                tvm = v.emit(lambda e: e.tensor_mul(xn_t[:], xn_t[:], bc1[:]),
                             tok=True)
                v.emit(lambda e, i=i, sl=sl: e.tensor_copy(
                    self.xnh[i][:, sl], xn_t[:]))
                tlo = v.emit(lambda e, i=i, sl=sl: e.scalar_tensor_tensor(
                    self.xnl[i][:, sl], self.xnh[i][:, sl], -1.0, xn_t[:],
                    ALU.mult, ALU.add), tok=True)
                t.wait(tvm)
                t.emit(lambda e, i=i: e.matmul(
                    ps_s[0:1, :], self.ones_col[:], xn_t[:],
                    start=(i == 0), stop=(i == D_TILES - 1)))
                s.wait(tvm)
                stk = s.emit(lambda e: e.activation(
                    out=sq_tmp[:], in_=xn_t[:], func=AF.Square), tok=True)
                t.wait(stk)
                ttk = t.emit(lambda e, i=i: e.matmul(
                    ps_q[0:1, :], self.ones_col[:], sq_tmp[:],
                    start=(i == 0), stop=(i == D_TILES - 1)), tok=True)
                s.wait(ttk)
                v.wait(ttk)
            tk_mm = c.t.cur()
            v.wait(tk_mm)
            v.emit(lambda e, sl=sl: e.tensor_scalar_mul(
                rows[0:1, sl], ps_s[0:1, :], 1.0 / D))
            v.emit(lambda e, sl=sl: e.tensor_mul(
                rowtmp[0:1, 0:512], rows[0:1, sl], rows[0:1, sl]))
            v.emit(lambda e: e.scalar_tensor_tensor(
                rowtmp[0:1, 0:512], ps_q[0:1, :], 1.0 / D,
                rowtmp[0:1, 0:512], ALU.mult, ALU.subtract))
            tvv = v.emit(lambda e: e.tensor_scalar_add(
                rowtmp[0:1, 0:512], rowtmp[0:1, 0:512], LNEPS), tok=True)
            s.wait(tvv)
            tss = s.emit(lambda e: e.activation(
                out=rowtmp[0:1, 0:512], in_=rowtmp[0:1, 0:512],
                func=AF.Sqrt), tok=True)
            v.wait(tss)
            tvr = v.emit(lambda e, ch=ch: e.reciprocal(
                rows[0:1, T + 512 * ch:T + 512 * (ch + 1)],
                rowtmp[0:1, 0:512]), tok=True)
            t.wait(tvr)
        c.barrier()
        for ch in range(NCH):
            sl = slice(512 * ch, 512 * (ch + 1))
            g.wait(c.v.cur())
            tb0 = g.emit(lambda e, sl=sl: e.partition_broadcast(
                bc0[:], rows[0:1, sl]), tok=True)
            tb1 = g.emit(lambda e, ch=ch: e.partition_broadcast(
                bc1[:], rows[0:1, T + 512 * ch:T + 512 * (ch + 1)]), tok=True)
            v.wait(tb0); v.wait(tb1)
            for i in range(D_TILES):
                v.emit(lambda e, i=i, sl=sl: e.tensor_add(
                    xn_t[:], self.xnh[i][:, sl], self.xnl[i][:, sl]))
                v.emit(lambda e: e.tensor_sub(xn_t[:], xn_t[:], bc0[:]))
                v.emit(lambda e: e.tensor_mul(xn_t[:], xn_t[:], bc1[:]))
                v.emit(lambda e, i=i: e.tensor_scalar(
                    xn2_t[:], xn_t[:], self.ln0s[:, i:i + 1],
                    self.ln0b[:, i:i + 1], ALU.mult, ALU.add))
                v.emit(lambda e: e.tensor_copy(h_t[:], xn2_t[:]))
                tsp = v.emit(lambda e: e.scalar_tensor_tensor(
                    l_t[:], h_t[:], -1.0, xn2_t[:], ALU.mult, ALU.add),
                    tok=True)
                sd.wait(tsp)
                sd.emit(lambda e, i=i, sl=sl: e.dma_start(
                    out=self.scr["s_xn2h"][128 * i:128 * (i + 1), sl],
                    in_=h_t[:]), tok=True)
                tdm = sd.emit(lambda e, i=i, sl=sl: e.dma_start(
                    out=self.scr["s_xn2l"][128 * i:128 * (i + 1), sl],
                    in_=l_t[:]), tok=True)
                v.wait(tdm)
        c.barrier()


    def ph_win(self):
        c = self.c
        t, v, s, g, y, sd, gd = c.t, c.v, c.s, c.g, c.y, c.sd, c.gd
        xnh = [c.sb(f"w0xnh{i}", [128, T], BF16, OFF_B + i * 8192)
               for i in range(D_TILES)]
        xnl = [c.sb(f"w0xnl{i}", [128, T], BF16, OFF_B + i * 8192 + 4096)
               for i in range(D_TILES)]
        whb = [c.sb(f"w0wh{i}", [128, 1024], BF16, OFF_W + i * 2048)
               for i in range(2)]
        wlb = [c.sb(f"w0wl{i}", [128, 1024], BF16, OFF_W + 4096 + i * 2048)
               for i in range(2)]
        wfb = [c.sb(f"w0wf{i}", [128, 1024], F32, OFF_W + 8192 + i * 4096)
               for i in range(2)]
        row = c.sb("w0row", [128, 2050], F32, OFF_W + 16384)
        cv1 = c.sb("w0cv", [128, T], F32, OFF_W + 24640)
        xh_r = c.sb("w0xh", [128, T], BF16, OFF_W + 32832)
        xl_r = c.sb("w0xl", [128, T], BF16, OFF_W + 36928)
        ztf = [c.sb(f"w0ztf{i}", [128, 512], F32, OFF_W + 16384 + i * 2048)
               for i in range(2)]
        zh_r = [c.sb(f"w0zh{i}", [128, 512], BF16, OFF_W + 20480 + i * 1024)
                for i in range(2)]
        zl_r = [c.sb(f"w0zl{i}", [128, 512], BF16, OFF_W + 22528 + i * 1024)
                for i in range(2)]

        for i in range(D_TILES):
            y.emit(lambda e, i=i: e.dma_start(
                out=xnh[i][:], in_=self.scr["s_xn2h"][128 * i:128 * (i + 1), :]),
                tok=True)
            y.emit(lambda e, i=i: e.dma_start(
                out=xnl[i][:], in_=self.scr["s_xn2l"][128 * i:128 * (i + 1), :]),
                tok=True)
        tmset = v.emit(lambda e: e.memset(row[:, 0:1], 0.0), tok=True)
        v.emit(lambda e: e.memset(row[:, 2049:2050], 0.0))
        c.barrier()

        wslot_tok = [None, None]
        row_free = None
        zfree = [None, None]
        for j in range(2 * DT_TILES):
            slot = j % 2
            y.wait(wslot_tok[slot])
            gd.wait(wslot_tok[slot])
            tfw = y.emit(lambda e, sl_=slot, j=j: e.dma_start(
                out=wfb[sl_][:].rearrange("p (k n) -> p k n", n=128),
                in_=self.hbm["w_in"].rearrange("(k p) n -> p k n", p=128)
                [:, :, 128 * j:128 * (j + 1)]), tok=True)
            thw = gd.emit(lambda e, sl_=slot, j=j: e.dma_start(
                out=whb[sl_][:].rearrange("p (k n) -> p k n", n=128),
                in_=self.hbm["w_in"].rearrange("(k p) n -> p k n", p=128)
                [:, :, 128 * j:128 * (j + 1)]), tok=True)
            g.wait(tfw)
            g.wait(thw)
            tlw = g.emit(lambda e, sl_=slot: e.tensor_sub(
                wlb[sl_][:], wfb[sl_][:], whb[sl_][:]), tok=True)
            t.wait(tlw)
            bank0 = (j % 2) * 4
            mm_tok = [None] * NCH
            for ch in range(NCH):
                sl = slice(512 * ch, 512 * (ch + 1))
                psb = self.ps[bank0 + ch]
                t.wait(self.win_ev[bank0 + ch])
                for k in range(D_TILES):
                    for wb, mv, st_, sp_ in ((whb, xnh, k == 0, False),
                                             (wlb, xnh, False, False),
                                             (whb, xnl, False, k == D_TILES - 1)):
                        t.emit(lambda e, k=k, sl=sl, sl_=slot, wb=wb, mv=mv,
                               st_=st_, sp_=sp_, psb=psb: e.matmul(
                            psb[:], wb[sl_][:, 128 * k:128 * (k + 1)],
                            mv[k][:, sl], start=st_, stop=sp_))
                mm_tok[ch] = c.t.cur()
            wslot_tok[slot] = mm_tok[NCH - 1]
            if j < DT_TILES:
                s.wait(row_free)
                for ch in range(NCH):
                    s.wait(mm_tok[ch])
                    s.emit(lambda e, ch=ch, bank0=bank0: e.activation(
                        out=row[:, 1 + 512 * ch:1 + 512 * (ch + 1)],
                        in_=self.ps[bank0 + ch][:], func=AF.Copy))
                tevc = c.s.cur()
                for ch in range(NCH):
                    self.win_ev[bank0 + ch] = tevc
                v.wait(tevc)
                v.emit(lambda e, j=j: e.tensor_scalar(
                    cv1[:], row[:, 1:2049], self.cw[1][:, j:j + 1],
                    self.cb0[:, j:j + 1], ALU.mult, ALU.add))
                v.emit(lambda e, j=j: e.scalar_tensor_tensor(
                    cv1[:], row[:, 0:2048], self.cw[0][:, j:j + 1], cv1[:],
                    ALU.mult, ALU.add))
                trow = v.emit(lambda e, j=j: e.scalar_tensor_tensor(
                    cv1[:], row[:, 2:2050], self.cw[2][:, j:j + 1], cv1[:],
                    ALU.mult, ALU.add), tok=True)
                row_free = trow
                s.wait(trow)
                tsil = s.emit(lambda e: e.activation(
                    out=cv1[:], in_=cv1[:], func=AF.Silu), tok=True)
                v.wait(tsil)
                v.emit(lambda e: e.tensor_copy(xh_r[:], cv1[:]))
                tsp = v.emit(lambda e: e.scalar_tensor_tensor(
                    xl_r[:], xh_r[:], -1.0, cv1[:], ALU.mult, ALU.add),
                    tok=True)
                sd.wait(tsp)
                sd.emit(lambda e, j=j: e.dma_start(
                    out=self.scr["s_xbh"][128 * j:128 * (j + 1), :],
                    in_=xh_r[:]), tok=True)
                tspl = sd.emit(lambda e, j=j: e.dma_start(
                    out=self.scr["s_xbl"][128 * j:128 * (j + 1), :],
                    in_=xl_r[:]), tok=True)
                v.wait(tspl)
                s.wait(tspl)
            else:
                jz = j - DT_TILES
                for ch in range(NCH):
                    pidx = ch % 2
                    psb = self.ps[bank0 + ch]
                    s.wait(mm_tok[ch])
                    s.wait(zfree[pidx])
                    tsz = s.emit(lambda e, psb=psb, pidx=pidx: e.activation(
                        out=ztf[pidx][:], in_=psb[:], func=AF.Silu), tok=True)
                    self.win_ev[bank0 + ch] = tsz
                    v.wait(tsz)
                    v.emit(lambda e, pidx=pidx: e.tensor_copy(
                        zh_r[pidx][:], ztf[pidx][:]))
                    tzl = v.emit(lambda e, pidx=pidx:
                                 e.scalar_tensor_tensor(
                        zl_r[pidx][:], zh_r[pidx][:], -1.0, ztf[pidx][:],
                        ALU.mult, ALU.add), tok=True)
                    sd.wait(tzl)
                    sd.emit(lambda e, jz=jz, ch=ch, pidx=pidx: e.dma_start(
                        out=self.scr["s_zh"][128 * jz:128 * (jz + 1),
                                             512 * ch:512 * (ch + 1)],
                        in_=zh_r[pidx][:]), tok=True)
                    tzs = sd.emit(lambda e, jz=jz, ch=ch, pidx=pidx: e.dma_start(
                        out=self.scr["s_zl"][128 * jz:128 * (jz + 1),
                                             512 * ch:512 * (ch + 1)],
                        in_=zl_r[pidx][:]), tok=True)
                    zfree[pidx] = tzs
        c.barrier()


    def ph_wdt_bc(self):
        c = self.c
        t, v, s, g, y, sd, gd = c.t, c.v, c.s, c.g, c.y, c.sd, c.gd
        xbh = [c.sb(f"d0xbh{i}", [128, T], BF16,
                    (OFF_A if i < 8 else OFF_B) + (i % 8) * 8192)
               for i in range(DT_TILES)]
        xbl = [c.sb(f"d0xbl{i}", [128, T], BF16,
                    (OFF_A if i < 8 else OFF_B) + (i % 8) * 8192 + 4096)
               for i in range(DT_TILES)]
        whb = [c.sb(f"d0wh{i}", [128, 2048], BF16, OFF_W + i * 4096)
               for i in range(2)]
        wlb = [c.sb(f"d0wl{i}", [128, 2048], BF16, OFF_W + 8192 + i * 4096)
               for i in range(2)]
        wfb = [c.sb(f"d0wf{i}", [128, 2048], F32, OFF_W + 16384 + i * 8192)
               for i in range(2)]
        dtr = [c.sb(f"d0dt{i}", [128, 512], F32, OFF_W + 32768 + i * 2048)
               for i in range(2)]
        bwh = c.sb("d0bwh", [128, 512], BF16, OFF_W + 36864)
        bwl = c.sb("d0bwl", [128, 512], BF16, OFF_W + 36864 + 1024)
        bwf = c.sb("d0bwf", [128, 512], F32, OFF_W + 38912)
        self.bc_rows = c.sb("bc_rows", [32, T], F32, OFF_C)

        for i in range(DT_TILES):
            y.emit(lambda e, i=i: e.dma_start(
                out=xbh[i][:], in_=self.scr["s_xbh"][128 * i:128 * (i + 1), :]),
                tok=True)
            y.emit(lambda e, i=i: e.dma_start(
                out=xbl[i][:], in_=self.scr["s_xbl"][128 * i:128 * (i + 1), :]),
                tok=True)
        self.xbh_d0, self.xbl_d0 = xbh, xbl
        c.barrier()
        wslot_tok = [None, None]
        dt_free = [None, None]
        for j in range(DT_TILES):
            slot = j % 2
            y.wait(wslot_tok[slot])
            gd.wait(wslot_tok[slot])
            tfw = y.emit(lambda e, sl_=slot, j=j: e.dma_start(
                out=wfb[sl_][:].rearrange("p (k n) -> p k n", n=128),
                in_=self.hbm["w_dt"].rearrange("(k p) n -> p k n", p=128)
                [:, :, 128 * j:128 * (j + 1)]), tok=True)
            thw = gd.emit(lambda e, sl_=slot, j=j: e.dma_start(
                out=whb[sl_][:].rearrange("p (k n) -> p k n", n=128),
                in_=self.hbm["w_dt"].rearrange("(k p) n -> p k n", p=128)
                [:, :, 128 * j:128 * (j + 1)]), tok=True)
            g.wait(tfw)
            g.wait(thw)
            tlw = g.emit(lambda e, sl_=slot: e.tensor_sub(
                wlb[sl_][:], wfb[sl_][:], whb[sl_][:]), tok=True)
            t.wait(tlw)
            bank0 = (j % 2) * 4
            for ch in range(NCH):
                sl = slice(512 * ch, 512 * (ch + 1))
                psb = self.ps[bank0 + ch]
                t.wait(self.wdt_ev[bank0 + ch])
                for k in range(DT_TILES):
                    for wb, mv, st_, sp_ in ((whb, xbh, k == 0, False),
                                             (wlb, xbh, False, False),
                                             (whb, xbl, False,
                                              k == DT_TILES - 1)):
                        t.emit(lambda e, k=k, sl=sl, sl_=slot, wb=wb, mv=mv,
                               st_=st_, sp_=sp_, psb=psb: e.matmul(
                            psb[:], wb[sl_][:, 128 * k:128 * (k + 1)],
                            mv[k][:, sl], start=st_, stop=sp_))
                tmm = c.t.cur()
                if ch == NCH - 1:
                    wslot_tok[slot] = tmm
                pidx = ch % 2
                s.wait(tmm)
                s.wait(dt_free[pidx])
                tse = s.emit(lambda e, psb=psb, j=j, pidx=pidx: e.activation(
                    out=dtr[pidx][:], in_=psb[:], func=AF.Exp,
                    bias=self.bdt[:, j:j + 1]), tok=True)
                self.wdt_ev[bank0 + ch] = tse
                v.wait(tse)
                tv1 = v.emit(lambda e, pidx=pidx: e.tensor_scalar_add(
                    dtr[pidx][:], dtr[pidx][:], 1.0), tok=True)
                s.wait(tv1)
                tso = s.emit(lambda e, pidx=pidx: e.activation(
                    out=dtr[pidx][:], in_=dtr[pidx][:], func=AF.Ln), tok=True)
                sd.wait(tso)
                tsd = sd.emit(lambda e, j=j, sl=sl, pidx=pidx: e.dma_start(
                    out=self.scr["s_dt"][128 * j:128 * (j + 1), sl],
                    in_=dtr[pidx][:]), tok=True)
                dt_free[pidx] = tsd
        tbw = y.emit(lambda e: e.dma_start(
            out=bwf[:].rearrange("p (k n) -> p k n", n=32),
            in_=self.hbm["w_bc"].rearrange("(k p) n -> p k n", p=128)),
            tok=True)
        tbh = gd.emit(lambda e: e.dma_start(
            out=bwh[:].rearrange("p (k n) -> p k n", n=32),
            in_=self.hbm["w_bc"].rearrange("(k p) n -> p k n", p=128)),
            tok=True)
        g.wait(tbw)
        g.wait(tbh)
        tbl = g.emit(lambda e: e.tensor_sub(bwl[:], bwf[:], bwh[:]), tok=True)
        t.wait(tbl)
        for ch in range(NCH):
            sl = slice(512 * ch, 512 * (ch + 1))
            psb = self.ps[ch % 2]
            t.wait(self.wdt_ev[ch % 2])
            t.wait(self.wdt_ev[4 + ch % 2])
            for k in range(DT_TILES):
                for wb, mv, st_, sp_ in ((bwh, xbh, k == 0, False),
                                         (bwl, xbh, False, False),
                                         (bwh, xbl, False, k == DT_TILES - 1)):
                    t.emit(lambda e, k=k, sl=sl, wb=wb, mv=mv, st_=st_,
                           sp_=sp_, psb=psb: e.matmul(
                        psb[0:32, :], wb[:, 32 * k:32 * (k + 1)],
                        mv[k][:, sl], start=st_, stop=sp_))
            tmm = c.t.cur()
            v.wait(tmm)
            tvc = v.emit(lambda e, sl=sl, psb=psb: e.tensor_copy(
                self.bc_rows[:, sl], psb[0:32, :]), tok=True)
            self.wdt_ev[ch % 2] = tvc
            t.wait(tvc)
        c.barrier()


    def ph_scan(self):
        c = self.c
        t, v, s, g, y, sd, gd = c.t, c.v, c.s, c.g, c.y, c.sd, c.gd
        dtt = [c.sb(f"scdt{i}", [128, 512], F32, OFF_A + i * 2048)
               for i in range(DT_TILES)]
        dtxt = [c.sb(f"scdx{i}", [128, 512], F32, OFF_A + 32768 + i * 2048)
                for i in range(DT_TILES)]
        ytl = [c.sb(f"scy{i}", [128, 512], F32, OFF_B + i * 2048)
               for i in range(DT_TILES)]
        xbh_c = [c.sb(f"scxh{i}", [128, 512], BF16, OFF_B + 32768 + i * 1024)
                 for i in range(DT_TILES)]
        xbl_c = [c.sb(f"scxl{i}", [128, 512], BF16, OFF_B + 49152 + i * 1024)
                 for i in range(DT_TILES)]
        ab = [c.sb(f"scab{i}", [128, 512], F32, OFF_W + i * 2048)
              for i in range(4)]
        ub = [c.sb(f"scub{i}", [128, 512], F32, OFF_W + 8192 + i * 2048)
              for i in range(4)]
        hb = [c.sb(f"schb{i}", [128, 512], F32, OFF_W + 16384 + i * 2048)
              for i in range(4)]
        wb_ = c.sb("scwb", [128, 512], F32, OFF_W + 24576)
        xrb = c.sb("scxr", [128, 512], F32, OFF_W + 26624)
        wb2_ = [c.sb(f"scwb2{i}", [128, 512], F32, OFF_W + 28672 + i * 2048)
                for i in range(2)]
        wbv = c.sb("scwbv", [128, 512], F32, OFF_W + 32768)
        bcb = [c.sb(f"scbc{i}", [128, 512], F32, OFF_W + 36864 + i * 2048)
               for i in range(2)]
        ccb = [c.sb(f"sccc{i}", [128, 512], F32, OFF_W + 40960 + i * 2048)
               for i in range(2)]
        t2b = c.sb("sct2", [128, 512], F32, OFF_W + 0)
        ztb = c.sb("sczt", [128, 512], F32, OFF_W + 2048)
        yvf = c.sb("scyv", [128, 512], F32, OFF_W + 4096)
        yvh = [c.sb(f"scyh{i}", [128, 512], BF16, OFF_W + 6144 + i * 1024)
               for i in range(2)]
        yvl = [c.sb(f"scyl{i}", [128, 512], BF16, OFF_W + 8192 + i * 1024)
               for i in range(2)]
        zhb = [c.sb(f"sczh{i}", [128, 512], BF16, OFF_W + 10240 + i * 1024)
               for i in range(2)]
        zlb = [c.sb(f"sczl{i}", [128, 512], BF16, OFF_W + 12288 + i * 1024)
               for i in range(2)]
        sel = c.sb("scsel", [32, 2 * NST * 128], F32, OFF_C + 8192)

        tio = g.emit(lambda e: e.iota(
            sel[:], [[1, 2 * NST], [0, 128]], channel_multiplier=0,
            allow_small_or_imprecise_dtypes=True), tok=True)
        v.wait(tio)
        v.emit(lambda e: e.tensor_scalar(
            sel[:], sel[:], self.iota_p[0:32, 0:1], None, ALU.is_equal))
        v.emit(lambda e: e.memset(self.state[:], 0.0))
        c.barrier()

        ab_free = [None] * 4
        u_free = [None] * 4
        hbg = [None] * 4
        ps_free = [None, None]
        bc_guard_g = [None, None]
        bc_guard_v = [None, None]
        yv_free = [None, None]
        z_free = [None, None]
        G_SET = (1, 3, 5, 7, 9, 11)
        pend_cp = []
        pend_g = []
        v_pend = None
        g_pend = None
        vcnt = 0
        gcnt = 0
        for ch in range(NCH):
            sl = slice(512 * ch, 512 * (ch + 1))
            y.wait(c.v.cur())
            sd.wait(c.v.cur())
            gd.wait(c.v.cur())
            ld_tok = []
            for i in range(DT_TILES):
                y.emit(lambda e, i=i, sl=sl: e.dma_start(
                    out=dtt[i][:], in_=self.scr["s_dt"][128 * i:128 * (i + 1), sl]))
                sd.emit(lambda e, i=i, sl=sl: e.dma_start(
                    out=xbh_c[i][:], in_=self.scr["s_xbh"][128 * i:128 * (i + 1), sl]))
                gd.emit(lambda e, i=i, sl=sl: e.dma_start(
                    out=xbl_c[i][:], in_=self.scr["s_xbl"][128 * i:128 * (i + 1), sl]))
                ld_tok.append((c.y.cur(), c.sd.cur(), c.gd.cur()))
            s.wait(c.v.cur()); s.wait(c.g.cur())
            g.wait(c.v.cur()); g.wait(c.sd.cur())
            for i in range(DT_TILES):
                for tk in ld_tok[i]:
                    g.wait(tk)
                g.emit(lambda e, i=i: e.tensor_add(
                    xrb[:], xbh_c[i][:], xbl_c[i][:]))
                g.emit(lambda e, i=i: e.tensor_mul(dtxt[i][:], dtt[i][:], xrb[:]))
            tdx = c.g.cur()
            for n in range(NST):
                par = n % 2
                psB, psC = self.ps[par], self.ps[2 + par]
                t.wait(ps_free[par])
                t.emit(lambda e, n=n, sl=sl, psB=psB: e.matmul(
                    psB[:], sel[:, 128 * n:128 * (n + 1)],
                    self.bc_rows[:, sl], start=True, stop=True))
                tps = t.emit(lambda e, n=n, sl=sl, psC=psC: e.matmul(
                    psC[:], sel[:, 128 * (NST + n):128 * (NST + n + 1)],
                    self.bc_rows[:, sl], start=True, stop=True), tok=True)
                s.wait(tps)
                s.wait(bc_guard_g[par]); s.wait(bc_guard_v[par])
                s.emit(lambda e, par=par, psB=psB: e.activation(
                    out=bcb[par][:], in_=psB[:], func=AF.Copy))
                tbe = s.emit(lambda e, par=par, psC=psC: e.activation(
                    out=ccb[par][:], in_=psC[:], func=AF.Copy), tok=True)
                ps_free[par] = tbe
                g.wait(tbe)
                if n == 0:
                    g.wait(tdx)
                for i in range(DT_TILES):
                    pai = i % 4
                    col = i * NST + n
                    if n == 0:
                        s.wait(ld_tok[i][0])
                    s.wait(ab_free[pai])
                    tab = s.emit(lambda e, i=i, col=col, pai=pai: e.activation(
                        out=ab[pai][:], in_=dtt[i][:], func=AF.Exp,
                        scale=self.a_cols[:, col:col + 1]), tok=True)
                    g.wait(u_free[pai])
                    tu = g.emit(lambda e, i=i, pai=pai, par=par: e.tensor_mul(
                        ub[pai][:], dtxt[i][:], bcb[par][:]), tok=True)
                    v.wait(tab)
                    v.wait(tu)
                    v.wait(hbg[pai])
                    tscn = v.emit(lambda e, pai=pai, col=col:
                                  e.tensor_tensor_scan(
                        hb[pai][:], ab[pai][:], ub[pai][:],
                        self.state[:, col:col + 1], ALU.mult, ALU.add),
                        tok=True)
                    ab_free[pai] = tscn
                    u_free[pai] = tscn
                    pend_cp.append((tscn, pai, col))
                    if len(pend_cp) > 3:
                        tsc_, pai_, col_ = pend_cp.pop(0)
                        s.wait(tsc_)
                        s.emit(lambda e, pai=pai_, col=col_: e.activation(
                            out=self.state[:, col:col + 1],
                            in_=hb[pai][:, 511:512], func=AF.Copy))
                    if i in G_SET:
                        pend_g.append((tscn, i, pai))
                    elif n == 0:
                        v.emit(lambda e, i=i, pai=pai, par=par: e.tensor_mul(
                            ytl[i][:], hb[pai][:], ccb[par][:]))
                    else:
                        sc = wb_ if vcnt % 2 == 0 else wbv
                        vcnt += 1
                        v.emit(lambda e, pai=pai, par=par, sc=sc:
                               e.tensor_mul(sc[:], hb[pai][:], ccb[par][:]))
                        if v_pend is not None:
                            j_, sc_ = v_pend
                            v.emit(lambda e, j=j_, sc=sc_: e.tensor_add(
                                ytl[j][:], ytl[j][:], sc[:]))
                        v_pend = (i, sc)
                    while pend_g and pend_g[0][1] <= i - 2:
                        tsc_, j_, paj_ = pend_g.pop(0)
                        g.wait(tsc_)
                        if n == 0:
                            thg = g.emit(lambda e, j=j_, pai=paj_, par=par:
                                         e.tensor_mul(
                                ytl[j][:], hb[pai][:], ccb[par][:]), tok=True)
                        else:
                            scg = wb2_[gcnt % 2]
                            gcnt += 1
                            thg = g.emit(lambda e, pai=paj_, par=par, sc=scg:
                                         e.tensor_mul(
                                sc[:], hb[pai][:], ccb[par][:]), tok=True)
                            if g_pend is not None:
                                jp_, scp_ = g_pend
                                g.emit(lambda e, j=jp_, sc=scp_: e.tensor_add(
                                    ytl[j][:], ytl[j][:], sc[:]))
                            g_pend = (j_, scg)
                        hbg[paj_] = thg
                bc_guard_g[par] = c.g.cur()
                bc_guard_v[par] = c.v.cur()
            if v_pend is not None:
                j_, sc_ = v_pend
                v.emit(lambda e, j=j_, sc=sc_: e.tensor_add(
                    ytl[j][:], ytl[j][:], sc[:]))
                v_pend = None
            if g_pend is not None:
                jp_, scp_ = g_pend
                g.emit(lambda e, j=jp_, sc=scp_: e.tensor_add(
                    ytl[j][:], ytl[j][:], sc[:]))
                g_pend = None
            assert not pend_g
            while pend_cp:
                tsc_, pai_, col_ = pend_cp.pop(0)
                s.wait(tsc_)
                s.emit(lambda e, pai=pai_, col=col_: e.activation(
                    out=self.state[:, col:col + 1],
                    in_=hb[pai][:, 511:512], func=AF.Copy))
            g.wait(c.v.cur())
            for i in range(DT_TILES):
                pz = i % 2
                sd.wait(z_free[pz])
                sd.emit(lambda e, i=i, sl=sl, pz=pz: e.dma_start(
                    out=zhb[pz][:], in_=self.scr["s_zh"][128 * i:128 * (i + 1), sl]))
                tzld = sd.emit(lambda e, i=i, sl=sl, pz=pz: e.dma_start(
                    out=zlb[pz][:], in_=self.scr["s_zl"][128 * i:128 * (i + 1), sl]),
                    tok=True)
                g.wait(tzld)
                tzr = g.emit(lambda e, pz=pz: e.tensor_add(
                    ztb[:], zhb[pz][:], zlb[pz][:]), tok=True)
                z_free[pz] = tzr
                txr = g.emit(lambda e, i=i: e.tensor_add(
                    xrb[:], xbh_c[i][:], xbl_c[i][:]), tok=True)
                v.wait(tzr)
                v.wait(txr)
                v.emit(lambda e, i=i: e.scalar_tensor_tensor(
                    t2b[:], xrb[:], self.dprm[:, i:i + 1], ytl[i][:],
                    ALU.mult, ALU.add))
                tyv2 = v.emit(lambda e: e.tensor_mul(yvf[:], t2b[:], ztb[:]),
                              tok=True)
                v.wait(yv_free[pz])
                v.emit(lambda e, pz=pz: e.tensor_copy(yvh[pz][:], yvf[:]))
                g.wait(c.v.cur())
                tsp = g.emit(lambda e, pz=pz: e.tensor_sub(
                    yvl[pz][:], yvf[:], yvh[pz][:]), tok=True)
                sd.wait(tsp)
                sd.emit(lambda e, i=i, sl=sl, pz=pz: e.dma_start(
                    out=self.scr["s_yvh"][128 * i:128 * (i + 1), sl],
                    in_=yvh[pz][:]))
                tyv = sd.emit(lambda e, i=i, sl=sl, pz=pz: e.dma_start(
                    out=self.scr["s_yvl"][128 * i:128 * (i + 1), sl],
                    in_=yvl[pz][:]), tok=True)
                yv_free[pz] = tyv
        c.barrier()


    def ph_wout0(self):
        c = self.c
        t, v, s, g, y, sd, gd = c.t, c.v, c.s, c.g, c.y, c.sd, c.gd
        yvh = [c.sb(f"o0yh{i}", [128, 1024], BF16, OFF_A + i * 4096)
               for i in range(DT_TILES)]
        yvl = [c.sb(f"o0yl{i}", [128, 1024], BF16, OFF_A + i * 4096 + 2048)
               for i in range(DT_TILES)]
        self.x1f = [c.sb(f"x1f{i}", [128, T], F32, OFF_B + i * 8192)
                    for i in range(D_TILES)]
        whb = [c.sb(f"o0wh{i}", [128, 2048], BF16, OFF_W + i * 4096)
               for i in range(2)]
        wlb = [c.sb(f"o0wl{i}", [128, 2048], BF16, OFF_W + 8192 + i * 4096)
               for i in range(2)]
        wfb = [c.sb(f"o0wf{i}", [128, 2048], F32, OFF_W + 16384 + i * 8192)
               for i in range(2)]

        for i in range(D_TILES):
            y.emit(lambda e, i=i: e.dma_start(
                out=self.x1f[i][:], in_=self.scr["s_x1"][128 * i:128 * (i + 1), :]))
        c.barrier()
        ev = [None] * 8
        for half in range(2):
            th0 = half * 1024
            hsl = slice(th0, th0 + 1024)
            tldy = None
            for i in range(DT_TILES):
                y.emit(lambda e, i=i, hsl=hsl: e.dma_start(
                    out=yvh[i][:], in_=self.scr["s_yvh"][128 * i:128 * (i + 1), hsl]))
                tldy = y.emit(lambda e, i=i, hsl=hsl: e.dma_start(
                    out=yvl[i][:], in_=self.scr["s_yvl"][128 * i:128 * (i + 1), hsl]),
                    tok=True)
            t.wait(tldy)
            wslot_tok = [None, None]
            for j in range(D_TILES):
                slot = j % 2
                y.wait(wslot_tok[slot])
                gd.wait(wslot_tok[slot])
                tfw = y.emit(lambda e, sl_=slot, j=j: e.dma_start(
                    out=wfb[sl_][:].rearrange("p (k n) -> p k n", n=128),
                    in_=self.hbm["w_out0"].rearrange("(k p) n -> p k n", p=128)
                    [:, :, 128 * j:128 * (j + 1)]), tok=True)
                thw = gd.emit(lambda e, sl_=slot, j=j: e.dma_start(
                    out=whb[sl_][:].rearrange("p (k n) -> p k n", n=128),
                    in_=self.hbm["w_out0"].rearrange("(k p) n -> p k n", p=128)
                    [:, :, 128 * j:128 * (j + 1)]), tok=True)
                g.wait(tfw)
                g.wait(thw)
                tlw = g.emit(lambda e, sl_=slot: e.tensor_sub(
                    wlb[sl_][:], wfb[sl_][:], whb[sl_][:]), tok=True)
                t.wait(tlw)
                bank0 = (j % 2) * 4
                for ch in range(2):
                    lsl = slice(512 * ch, 512 * (ch + 1))
                    gsl = slice(th0 + 512 * ch, th0 + 512 * (ch + 1))
                    psb = self.ps[bank0 + ch]
                    t.wait(ev[bank0 + ch])
                    for k in range(DT_TILES):
                        for wb, mv, st_, sp_ in ((whb, yvh, k == 0, False),
                                                 (wlb, yvh, False, False),
                                                 (whb, yvl, False,
                                                  k == DT_TILES - 1)):
                            t.emit(lambda e, k=k, lsl=lsl, sl_=slot, wb=wb,
                                   mv=mv, st_=st_, sp_=sp_, psb=psb: e.matmul(
                                psb[:], wb[sl_][:, 128 * k:128 * (k + 1)],
                                mv[k][:, lsl], start=st_, stop=sp_))
                    tmm = c.t.cur()
                    if ch == 1:
                        wslot_tok[slot] = tmm
                    v.wait(tmm)
                    tva = v.emit(lambda e, j=j, gsl=gsl, psb=psb: e.tensor_add(
                        self.x1f[j][:, gsl], self.x1f[j][:, gsl], psb[:]),
                        tok=True)
                    ev[bank0 + ch] = tva
            y.wait(c.t.cur())
        c.barrier()


    def ph_ln1(self):
        c = self.c
        t, v, s, g, y, sd, gd = c.t, c.v, c.s, c.g, c.y, c.sd, c.gd
        self.xn1h = [c.sb(f"l1xh{i}", [128, T], BF16, OFF_A + i * 8192)
                     for i in range(D_TILES)]
        self.xn1l = [c.sb(f"l1xl{i}", [128, T], BF16, OFF_A + i * 8192 + 4096)
                     for i in range(D_TILES)]
        rows = c.sb("l1rows", [1, 2 * T], F32, OFF_C)
        sq_tmp = c.sb("l1sq", [128, 512], F32, OFF_C + 16384)
        rowtmp = c.sb("l1rowt", [1, 512], F32, OFF_C + 18432)
        bc0 = c.sb("l1bc0", [128, 512], F32, OFF_W)
        bc1 = c.sb("l1bc1", [128, 512], F32, OFF_W + 2048)
        xn_t = c.sb("l1xnt", [128, 512], F32, OFF_W + 4096)
        macc = c.sb("l1mac", [128, 2048], F32, OFF_W + 6144)
        parb = c.sb("l1par", [128, 512], F32, OFF_W + 14336)
        xr_t = c.sb("l1xr", [128, 512], F32, OFF_W + 16384)
        qf_t = c.sb("l1qf", [128, 512], F16, OFF_W + 18432)
        qb_t = [c.sb(f"l1qb{i}", [128, 512], BF16, OFF_W + 19456 + i * 1024)
                for i in range(2)]
        rbc = c.sb("l1rbc", [128, 512], F32, OFF_W + 21504)
        rowb = c.sb("l1rowb", [1, 512], F32, OFF_W + 23552)

        self.ln_rows(lambda i: self.x1f[i], D_TILES, D, rows, sq_tmp, rowtmp)
        c.barrier()
        for ch in range(NCH):
            sl = slice(512 * ch, 512 * (ch + 1))
            g.wait(c.v.cur())
            tb0 = g.emit(lambda e, sl=sl: e.partition_broadcast(
                bc0[:], rows[0:1, sl]), tok=True)
            tb1 = g.emit(lambda e, ch=ch: e.partition_broadcast(
                bc1[:], rows[0:1, T + 512 * ch:T + 512 * (ch + 1)]), tok=True)
            v.wait(tb0); v.wait(tb1)
            for i in range(D_TILES):
                v.emit(lambda e, i=i, sl=sl: e.tensor_sub(
                    xn_t[:], self.x1f[i][:, sl], bc0[:]))
                v.emit(lambda e: e.tensor_mul(xn_t[:], xn_t[:], bc1[:]))
                v.emit(lambda e, i=i, sl=sl: e.tensor_copy(
                    self.xn1h[i][:, sl], xn_t[:]))
                tlo = v.emit(lambda e, i=i, sl=sl: e.scalar_tensor_tensor(
                    self.xn1l[i][:, sl], self.xn1h[i][:, sl], -1.0, xn_t[:],
                    ALU.mult, ALU.add), tok=True)
                v.emit(lambda e: e.scalar_tensor_tensor(
                    xr_t[:], xn_t[:], -1.0, xn_t[:], ALU.mult, ALU.max))
                tva = c.v.cur()
                g.wait(tva)
                tpr = g.emit(lambda e: e.partition_all_reduce(
                    parb[:], xr_t[:], channels=128,
                    reduce_op=ReduceOp.max), tok=True)
                v.wait(tpr)
                if i == 0:
                    tmx = v.emit(lambda e, sl=sl: e.tensor_copy(
                        macc[:, sl], parb[:]), tok=True)
                else:
                    tmx = v.emit(lambda e, sl=sl: e.tensor_max(
                        macc[:, sl], macc[:, sl], parb[:]), tok=True)
                g.wait(tmx)
        c.barrier()
        for ch in range(NCH):
            sl = slice(512 * ch, 512 * (ch + 1))
            v.emit(lambda e, sl=sl: e.tensor_scalar_max(
                rbc[:], macc[:, sl], 1e-5))
            tsx = v.emit(lambda e, sl=sl: e.tensor_scalar_mul(
                rowb[0:1, :], rbc[0:1, :], 1.0 / 127.0), tok=True)
            sd.wait(tsx)
            sd.emit(lambda e, sl=sl: e.dma_start(
                out=self.scr["s_sxi1"][0:1, sl], in_=rowb[0:1, :]), tok=True)
            v.emit(lambda e: e.reciprocal(rbc[:], rbc[:]))
            tsr = v.emit(lambda e: e.tensor_scalar_mul(rbc[:], rbc[:], 127.0),
                         tok=True)
            for i in range(D_TILES):
                pq = i % 2
                v.emit(lambda e, i=i, sl=sl: e.tensor_add(
                    xr_t[:], self.xn1h[i][:, sl], self.xn1l[i][:, sl]))
                v.emit(lambda e: e.tensor_mul(xr_t[:], xr_t[:], rbc[:]))
                v.emit(lambda e: e.tensor_scalar(
                    qf_t[:], xr_t[:], 1536.0, 1408.0, ALU.add, ALU.max))
                tqq = v.emit(lambda e, pq=pq: e.tensor_scalar(
                    qb_t[pq][:], qf_t[:], 1663.0, 1536.0, ALU.min,
                    ALU.subtract), tok=True)
                sd.wait(tqq)
                tqs = sd.emit(lambda e, i=i, sl=sl, pq=pq: e.dma_start(
                    out=self.scr["s_xq1"][128 * i:128 * (i + 1), sl],
                    in_=qb_t[pq][:]), tok=True)
                v.wait((tqs[0], max(0, tqs[1] - 16)))
        c.barrier()

    def ph_proj1(self):
        c = self.c
        t, v, s, g, y, sd, gd = c.t, c.v, c.s, c.g, c.y, c.sd, c.gd
        for i in range(D_TILES):
            sd.emit(lambda e, i=i: e.dma_start(
                out=self.scr["s_x1"][128 * i:128 * (i + 1), :],
                in_=self.x1f[i][:]), tok=True)
        self.kt = [c.sb(f"a1k{i}", [128, T], F32, OFF_B + i * 8192)
                   for i in range(D_TILES)]
        lath = c.sb("a1lath", [128, T], BF16, OFF_C)
        latl = c.sb("a1latl", [128, T], BF16, OFF_C + 4096)
        whb = [c.sb(f"a1wh{i}", [128, 1024], BF16, OFF_W + i * 2048)
               for i in range(2)]
        wlb = [c.sb(f"a1wl{i}", [128, 1024], BF16, OFF_W + 4096 + i * 2048)
               for i in range(2)]
        wfb = [c.sb(f"a1wf{i}", [128, 1024], F32, OFF_W + 8192 + i * 4096)
               for i in range(2)]
        kvh = c.sb("a1kvh", [128, 2048], BF16, OFF_W + 16384)
        kvl = c.sb("a1kvl", [128, 2048], BF16, OFF_W + 16384 + 4096)
        kvf = c.sb("a1kvf", [128, 2048], F32, OFF_W + 24576)
        etmp = [c.sb(f"a1et{i}", [128, 512], F32, OFF_W + 32768 + i * 2048)
                for i in range(2)]
        ttmp = [c.sb(f"a1tt{i}", [128, 128], F32, OFF_W + 36864 + i * 512)
                for i in range(2)]

        def stream_pair_mm(w_name, KT_, out_handler, ev, mtiles,
                           moving_h, moving_l):
            wslot_tok = [None, None]
            for j in range(mtiles):
                slot = j % 2
                y.wait(wslot_tok[slot])
                gd.wait(wslot_tok[slot])
                tfw = y.emit(lambda e, sl_=slot, j=j, w_name=w_name, KT_=KT_:
                             e.dma_start(
                    out=wfb[sl_][:, 0:KT_ * 128].rearrange(
                        "p (k n) -> p k n", n=128),
                    in_=self.hbm[w_name].rearrange("(k p) n -> p k n", p=128)
                    [:, :, 128 * j:128 * (j + 1)]), tok=True)
                thw = gd.emit(lambda e, sl_=slot, j=j, w_name=w_name, KT_=KT_:
                              e.dma_start(
                    out=whb[sl_][:, 0:KT_ * 128].rearrange(
                        "p (k n) -> p k n", n=128),
                    in_=self.hbm[w_name].rearrange("(k p) n -> p k n", p=128)
                    [:, :, 128 * j:128 * (j + 1)]), tok=True)
                g.wait(tfw)
                g.wait(thw)
                tlw = g.emit(lambda e, sl_=slot, KT_=KT_: e.tensor_sub(
                    wlb[sl_][:, 0:KT_ * 128], wfb[sl_][:, 0:KT_ * 128],
                    whb[sl_][:, 0:KT_ * 128]), tok=True)
                t.wait(tlw)
                bank0 = (j % 2) * 2
                for ch in range(NCH):
                    sl = slice(512 * ch, 512 * (ch + 1))
                    psb = self.ps[bank0 + ch % 2]
                    t.wait(ev[bank0 + ch % 2])
                    for k in range(KT_):
                        for wb, mv, st_, sp_ in (
                                (whb, moving_h, k == 0, False),
                                (wlb, moving_h, False, False),
                                (whb, moving_l, False, k == KT_ - 1)):
                            t.emit(lambda e, k=k, sl=sl, sl_=slot, wb=wb,
                                   mv=mv, st_=st_, sp_=sp_, psb=psb: e.matmul(
                                psb[:], wb[sl_][:, 128 * k:128 * (k + 1)],
                                mv[k][:, sl], start=st_, stop=sp_))
                    tmm = c.t.cur()
                    if ch == NCH - 1:
                        wslot_tok[slot] = tmm
                    ev[bank0 + ch % 2] = out_handler(j, ch, psb, tmm)

        ev = [None] * 4

        def lat_out(j, ch, psb, tmm):
            sl = slice(512 * ch, 512 * (ch + 1))
            v.wait(tmm)
            v.emit(lambda e, sl=sl, psb=psb: e.tensor_copy(lath[:, sl], psb[:]))
            return v.emit(lambda e, sl=sl, psb=psb: e.scalar_tensor_tensor(
                latl[:, sl], lath[:, sl], -1.0, psb[:], ALU.mult, ALU.add),
                tok=True)

        stream_pair_mm("w_kvd", D_TILES, lat_out, ev, 1,
                       self.xn1h, self.xn1l)
        c.barrier()
        lat_h = [lath.ap().rearrange("p n -> p n") if False else lath]
        ev = [None] * 4

        def kv_out(j, ch, psb, tmm):
            sl = slice(512 * ch, 512 * (ch + 1))
            if j < 8:
                v.wait(tmm)
                return v.emit(lambda e, j=j, sl=sl, psb=psb: e.tensor_copy(
                    self.kt[j][:, sl], psb[:]), tok=True)
            pe_ = ch % 2
            s.wait(tmm)
            s.wait(self.vt_free[pe_])
            tcp = s.emit(lambda e, pe_=pe_, psb=psb: e.activation(
                out=etmp[pe_][:], in_=psb[:], func=AF.Copy), tok=True)
            t.wait(tcp)
            last = None
            for b in range(4):
                pst = self.ps[4 + (b % 2)]
                t.wait(self.tt_free[b % 2])
                ttr = t.emit(lambda e, pe_=pe_, b=b, pst=pst: e.transpose(
                    pst[:, 0:128], etmp[pe_][:, 128 * b:128 * (b + 1)],
                    self.ident[:]), tok=True)
                v.wait(ttr)
                tvc = v.emit(lambda e, b=b, pst=pst: e.tensor_copy(
                    ttmp[b % 2][:], pst[:, 0:128]), tok=True)
                t.wait(tvc)
                sd.wait(tvc)
                tsd = sd.emit(lambda e, j=j, ch=ch, b=b: e.dma_start(
                    out=self.scr["s_vt"][512 * ch + 128 * b:
                                         512 * ch + 128 * (b + 1),
                                         128 * (j - 8):128 * (j - 7)],
                    in_=ttmp[b % 2][:]), tok=True)
                self.tt_free[b % 2] = tsd
                last = tsd
            self.vt_free[pe_] = last
            return tcp

        self.vt_free = [None, None]
        self.tt_free = [None, None]
        latp_h = [lath]
        latp_l = [latl]
        stream_pair_mm("w_kvu", 1, kv_out, ev, 16, latp_h, latp_l)
        c.barrier()
        ev = [None] * 4
        qb = [c.sb(f"a1qb{i}", [128, 512], F32, OFF_W + 37888 + i * 2048)
              for i in range(2)]
        self.q_free = [None, None]

        def q_out(j, ch, psb, tmm):
            pq = ch % 2
            v.wait(tmm)
            v.wait(self.q_free[pq])
            tqc = v.emit(lambda e, pq=pq, psb=psb: e.tensor_copy(
                qb[pq][:], psb[:]), tok=True)
            sd.wait(tqc)
            tqs = sd.emit(lambda e, j=j, ch=ch, pq=pq: e.dma_start(
                out=self.scr["s_q"][128 * j:128 * (j + 1),
                                    512 * ch:512 * (ch + 1)],
                in_=qb[pq][:]), tok=True)
            self.q_free[pq] = tqs
            return tqc

        stream_pair_mm("w_q", D_TILES, q_out, ev, D_TILES,
                       self.xn1h, self.xn1l)
        c.barrier()


    def ph_attn(self):
        c = self.c
        t, v, s, g, y, sd, gd = c.t, c.v, c.s, c.g, c.y, c.sd, c.gd
        scA = [c.sb(f"atA{i}", [128, 512], F32, OFF_A + i * 2048)
               for i in range(16)]
        scB = [c.sb(f"atB{i}", [128, 512], F32, OFF_A + 32768 + i * 2048)
               for i in range(16)]
        masks = c.sb("atmask", [128, 4 * 512], F32, OFF_C)
        vt_sl = [c.sb(f"atv{i}", [128, 130], F32, OFF_C + 8192 + i * 544)
                 for i in range(16)]
        qblk = [c.sb(f"atq{i}", [128, 512], F32, OFF_C + 17024 + i * 2048)
                for i in range(2)]
        y1b = [c.sb(f"aty{i}", [64, 512], F32, OFF_W + i * 2048)
               for i in range(2)]
        dnb = [c.sb(f"atd{i}", [1, 512], F32, OFF_W + 4096 + i * 2048)
               for i in range(2)]

        self.build_masks(masks)
        c.barrier()
        q_free = [None, None]
        vt_free = None
        y1_free = [None, None]
        av_done = None
        for hp in range(D_TILES):
            y.wait(vt_free)
            tvl = None
            for ti in range(16):
                tvl = y.emit(lambda e, ti=ti, hp=hp: e.dma_start(
                    out=vt_sl[ti][:, 0:130].rearrange(
                        "p (g n) -> p g n", n=65)[:, :, 0:64],
                    in_=self.scr["s_vt"][128 * ti:128 * (ti + 1),
                                         128 * hp:128 * (hp + 1)].rearrange(
                        "p (g n) -> p g n", n=64)), tok=True)
            v.wait(tvl)
            for ti in range(16):
                v.emit(lambda e, ti=ti: e.memset(vt_sl[ti][:, 64:65], 1.0))
                v.emit(lambda e, ti=ti: e.memset(vt_sl[ti][:, 129:130], 1.0))
            tones = c.v.cur()
            t.wait(tones)
            for tqb in range(NCH):
                tq0 = 512 * tqb
                ntk = 4 * (tqb + 1)
                pq = tqb % 2
                y.wait(q_free[pq])
                tql = y.emit(lambda e, hp=hp, tq0=tq0, pq=pq: e.dma_start(
                    out=qblk[pq][:],
                    in_=self.scr["s_q"][128 * hp:128 * (hp + 1),
                                        tq0:tq0 + 512]), tok=True)
                t.wait(tql)
                v.wait(av_done)
                exp_tok = [None] * 16
                evac_tok = [None, None]
                for ti in range(ntk):
                    par = ti % 2
                    psA, psB = self.ps[par], self.ps[2 + par]
                    t.wait(evac_tok[par])
                    kt_ = self.kt[hp]
                    for (tp, lsl, qsl, ps_, osl) in (
                            ((0, 0), slice(128 * ti, 128 * ti + 64),
                             slice(0, 64), psA, slice(0, 64)),
                            ((0, 64), slice(128 * ti + 64, 128 * ti + 128),
                             slice(0, 64), psA, slice(64, 128)),
                            ((64, 0), slice(128 * ti, 128 * ti + 64),
                             slice(64, 128), psB, slice(0, 64)),
                            ((64, 64), slice(128 * ti + 64, 128 * ti + 128),
                             slice(64, 128), psB, slice(64, 128))):
                        t.emit(lambda e, tp=tp, lsl=lsl, qsl=qsl, ps_=ps_,
                               osl=osl, kt_=kt_, pq=pq: e.matmul(
                            ps_[osl, :], kt_[qsl, lsl], qblk[pq][qsl, :],
                            start=True, stop=True, tile_position=tp))
                    tmm = c.t.cur()
                    v.wait(tmm)
                    diag = ti - 4 * tqb
                    for sc_, ps_ in ((scA, psA), (scB, psB)):
                        if 0 <= diag:
                            v.emit(lambda e, ti=ti, diag=diag, sc_=sc_,
                                   ps_=ps_: e.scalar_tensor_tensor(
                                sc_[ti][:], ps_[:], 0.125,
                                masks[:, 512 * diag:512 * (diag + 1)],
                                ALU.mult, ALU.add))
                        else:
                            v.emit(lambda e, ti=ti, sc_=sc_, ps_=ps_:
                                   e.tensor_scalar(
                                sc_[ti][:], ps_[:], 0.125, -EXP_SHIFT,
                                ALU.mult, ALU.add))
                    tev = c.v.cur()
                    evac_tok[par] = tev
                    s.wait(tev)
                    s.emit(lambda e, ti=ti: e.activation(
                        out=scA[ti][:], in_=scA[ti][:], func=AF.Exp))
                    exp_tok[ti] = s.emit(lambda e, ti=ti: e.activation(
                        out=scB[ti][:], in_=scB[ti][:], func=AF.Exp), tok=True)
                for hd, sc_ in ((0, scA), (1, scB)):
                    py = self.ps[4 + hd]
                    off = 65 * hd
                    for ti in range(ntk):
                        t.wait(exp_tok[ti])
                        t.emit(lambda e, ti=ti, py=py, off=off, sc_=sc_,
                               ntk=ntk: e.matmul(
                            py[0:65, :], vt_sl[ti][:, off:off + 65],
                            sc_[ti][:], start=(ti == 0),
                            stop=(ti == ntk - 1)))
                    tav = c.t.cur()
                    v.wait(tav)
                    v.wait(y1_free[hd])
                    v.emit(lambda e, hd=hd, py=py: e.tensor_copy(
                        y1b[hd][:], py[0:64, :]))
                    tcy = v.emit(lambda e, hd=hd, py=py: e.tensor_copy(
                        dnb[hd][0:1, :], py[64:65, :]), tok=True)
                    sd.wait(tcy)
                    sd.emit(lambda e, hp=hp, hd=hd, tq0=tq0: e.dma_start(
                        out=self.scr["s_y1"][128 * hp + 64 * hd:
                                             128 * hp + 64 * (hd + 1),
                                             tq0:tq0 + 512],
                        in_=y1b[hd][:]), tok=True)
                    tds = sd.emit(lambda e, hp=hp, hd=hd, tq0=tq0: e.dma_start(
                        out=self.scr["s_den"][2 * hp + hd:2 * hp + hd + 1,
                                              tq0:tq0 + 512],
                        in_=dnb[hd][0:1, :]), tok=True)
                    y1_free[hd] = tds
                av_done = c.t.cur()
                q_free[pq] = av_done
            vt_free = av_done
        c.barrier()


    def ph_wout1(self):
        c = self.c
        t, v, s, g, y, sd, gd = c.t, c.v, c.s, c.g, c.y, c.sd, c.gd
        denb = c.sb("o1den", [16, T], F32, OFF_C)
        sel2 = c.sb("o1sel", [16, 8 * 128], F32, OFF_C + 8192)
        y1t = [c.sb(f"o1yt{i}", [128, 512], F32, OFF_C + 12288 + i * 2048)
               for i in range(2)]
        y1h = [c.sb(f"o1yh{i}", [128, T], BF16, OFF_A + i * 8192)
               for i in range(D_TILES)]
        y1l = [c.sb(f"o1yl{i}", [128, T], BF16, OFF_A + i * 8192 + 4096)
               for i in range(D_TILES)]
        self.x2 = [c.sb(f"x2t{i}", [128, T], F32, OFF_B + i * 8192)
                   for i in range(D_TILES)]
        whb = [c.sb(f"o1wh{i}", [128, 1024], BF16, OFF_W + i * 2048)
               for i in range(2)]
        wlb = [c.sb(f"o1wl{i}", [128, 1024], BF16, OFF_W + 4096 + i * 2048)
               for i in range(2)]
        wfb = [c.sb(f"o1wf{i}", [128, 1024], F32, OFF_W + 8192 + i * 4096)
               for i in range(2)]

        tdn = y.emit(lambda e: e.dma_start(
            out=denb[:], in_=self.scr["s_den"][:, :]), tok=True)
        for i in range(D_TILES):
            y.emit(lambda e, i=i: e.dma_start(
                out=self.x2[i][:], in_=self.scr["s_x1"][128 * i:128 * (i + 1), :]))
        v.wait(tdn)
        v.emit(lambda e: e.reciprocal(denb[:], denb[:]))
        tio = g.emit(lambda e: e.iota(
            sel2[:], [[2, 8], [1, 2], [0, 64]], channel_multiplier=0,
            allow_small_or_imprecise_dtypes=True), tok=True)
        v.wait(tio)
        v.emit(lambda e: e.tensor_scalar(
            sel2[:], sel2[:], self.iota_p[0:16, 0:1], None, ALU.is_equal))
        c.barrier()
        rb_free = [None, None]
        for j in range(D_TILES):
            for ch in range(NCH):
                sl = slice(512 * ch, 512 * (ch + 1))
                pq = ch % 2
                y.wait(rb_free[pq])
                tyl = y.emit(lambda e, j=j, sl=sl, pq=pq: e.dma_start(
                    out=y1t[pq][:],
                    in_=self.scr["s_y1"][128 * j:128 * (j + 1), sl]), tok=True)
                psb = self.ps[pq]
                t.wait(rb_free[pq])
                trb = t.emit(lambda e, j=j, sl=sl, psb=psb: e.matmul(
                    psb[:], sel2[:, 128 * j:128 * (j + 1)], denb[:, sl],
                    start=True, stop=True), tok=True)
                v.wait(tyl)
                v.wait(trb)
                v.emit(lambda e, pq=pq, psb=psb: e.tensor_mul(
                    y1t[pq][:], y1t[pq][:], psb[:]))
                v.emit(lambda e, j=j, sl=sl, pq=pq: e.tensor_copy(
                    y1h[j][:, sl], y1t[pq][:]))
                trl = v.emit(lambda e, j=j, sl=sl, pq=pq: e.scalar_tensor_tensor(
                    y1l[j][:, sl], y1h[j][:, sl], -1.0, y1t[pq][:],
                    ALU.mult, ALU.add), tok=True)
                rb_free[pq] = trl
        c.barrier()
        wslot_tok = [None, None]
        ev = [None] * 8
        for j in range(D_TILES):
            slot = j % 2
            y.wait(wslot_tok[slot])
            gd.wait(wslot_tok[slot])
            tfw = y.emit(lambda e, sl_=slot, j=j: e.dma_start(
                out=wfb[sl_][:].rearrange("p (k n) -> p k n", n=128),
                in_=self.hbm["w_out1"].rearrange("(k p) n -> p k n", p=128)
                [:, :, 128 * j:128 * (j + 1)]), tok=True)
            thw = gd.emit(lambda e, sl_=slot, j=j: e.dma_start(
                out=whb[sl_][:].rearrange("p (k n) -> p k n", n=128),
                in_=self.hbm["w_out1"].rearrange("(k p) n -> p k n", p=128)
                [:, :, 128 * j:128 * (j + 1)]), tok=True)
            g.wait(tfw)
            g.wait(thw)
            tlw = g.emit(lambda e, sl_=slot: e.tensor_sub(
                wlb[sl_][:], wfb[sl_][:], whb[sl_][:]), tok=True)
            t.wait(tlw)
            bank0 = (j % 2) * 4
            for ch in range(NCH):
                sl = slice(512 * ch, 512 * (ch + 1))
                psb = self.ps[bank0 + ch]
                t.wait(ev[bank0 + ch])
                for k in range(D_TILES):
                    for wb, mv, st_, sp_ in ((whb, y1h, k == 0, False),
                                             (wlb, y1h, False, False),
                                             (whb, y1l, False,
                                              k == D_TILES - 1)):
                        t.emit(lambda e, k=k, sl=sl, sl_=slot, wb=wb, mv=mv,
                               st_=st_, sp_=sp_, psb=psb: e.matmul(
                            psb[:], wb[sl_][:, 128 * k:128 * (k + 1)],
                            mv[k][:, sl], start=st_, stop=sp_))
                tmm = c.t.cur()
                if ch == NCH - 1:
                    wslot_tok[slot] = tmm
                v.wait(tmm)
                tva = v.emit(lambda e, j=j, sl=sl, psb=psb: e.tensor_add(
                    self.x2[j][:, sl], self.x2[j][:, sl], psb[:]), tok=True)
                ev[bank0 + ch] = tva
        c.barrier()

    def ph_ffn1(self):
        c = self.c
        t, v, s, g, y, sd, gd = c.t, c.v, c.s, c.g, c.y, c.sd, c.gd
        wband = [c.sb(f"f1wb{i}", [128, 1024], F32, OFF_W + i * 4096)
                 for i in range(2)]
        tmpb = [c.sb(f"f1tmp{i}", [128, 1024], F32, OFF_W + 8192 + i * 4096)
                for i in range(2)]
        t1b = [c.sb(f"f1t1{i}", [128, 1024], F32, OFF_W + 16384 + i * 4096)
               for i in range(2)]
        wqb = [c.sb(f"f1wq{i}", [128, 1024], BF16, OFF_W + 24576 + i * 2048)
               for i in range(2)]
        accb = c.sb("f1acc", [128, 1], F32, OFF_W + 30784)
        redb = c.sb("f1red", [128, 32], F32, OFF_W + 30816)
        redo = c.sb("f1redo", [128, 32], F32, OFF_W + 30944)
        s1col = self.s1col4
        cells = self.cellA

        def wquant(w_name, KT_, N_, scr_name, cell_i):
            bw = min(N_, 1024)
            nh = N_ // bw
            tacc = None
            ld_tok = {}
            nb = 0
            for k in range(KT_):
                for h2 in range(nh):
                    sl_ = nb % 2
                    y.wait(ld_tok.get(sl_))
                    tw = y.emit(lambda e, k=k, h2=h2, w_name=w_name, bw=bw,
                                sl_=sl_: e.dma_start(
                        out=wband[sl_][:, 0:bw],
                        in_=self.hbm[w_name][128 * k:128 * (k + 1),
                                             bw * h2:bw * (h2 + 1)]), tok=True)
                    v.wait(tw)
                    tacc = v.emit(lambda e, bw=bw, sl_=sl_, nb=nb:
                                  e.reduce_sum(
                        redb[:, nb:nb + 1], wband[sl_][:, 0:bw],
                        mybir.AxisListType.X, apply_absolute_value=True),
                        tok=True)
                    ld_tok[sl_] = tacc
                    nb += 1
            s.wait(tacc)
            tsum = s.emit(lambda e, nb=nb: e.activation(
                out=redo[:, 0:nb], in_=redb[:, 0:nb], func=AF.Copy,
                accum_out=accb[:, 0:1]), tok=True)
            t.wait(tsum)
            tps = t.emit(lambda e: e.matmul(
                self.ps[0][0:1, 0:1], self.ones_col[:], accb[:],
                start=True, stop=True), tok=True)
            v.wait(tps)
            tms = v.emit(lambda e, KT_=KT_, N_=N_, cell_i=cell_i:
                         e.tensor_scalar(
                cells[0:1, cell_i:cell_i + 1], self.ps[0][0:1, 0:1],
                1.0 / (KT_ * 128 * N_), 1e-5, ALU.mult, ALU.max), tok=True)
            g.wait(tms)
            tbm = g.emit(lambda e, cell_i=cell_i: e.partition_broadcast(
                s1col[:, cell_i:cell_i + 1],
                cells[0:1, cell_i:cell_i + 1]), tok=True)
            v.wait(tbm)
            scrj = 4 + cell_i // 2
            tcl = v.emit(lambda e, cell_i=cell_i, scrj=scrj: e.reciprocal(
                s1col[:, scrj:scrj + 1],
                s1col[:, cell_i:cell_i + 1]), tok=True)
            g.wait(tcl)
            tbc = g.emit(lambda e, cell_i=cell_i, scrj=scrj: e.tensor_copy(
                s1col[:, cell_i + 1:cell_i + 2],
                s1col[:, scrj:scrj + 1]), tok=True)
            v.wait(tbc)
            t.wait(tbc)
            ld_tok = {}
            nb = 0
            for k in range(KT_):
                for h2 in range(nh):
                    sl_ = nb % 2
                    y.wait(ld_tok.get(sl_))
                    y.wait(ld_tok.get((sl_, "g")))
                    tw = y.emit(lambda e, k=k, h2=h2, w_name=w_name, bw=bw,
                                sl_=sl_: e.dma_start(
                        out=wband[sl_][:, 0:bw],
                        in_=self.hbm[w_name][128 * k:128 * (k + 1),
                                             bw * h2:bw * (h2 + 1)]), tok=True)
                    v.wait(tw)
                    v.wait(ld_tok.get((sl_, "g")))
                    v.emit(lambda e, bw=bw, cell_i=cell_i, sl_=sl_:
                           e.tensor_scalar(
                        tmpb[sl_][:, 0:bw], wband[sl_][:, 0:bw],
                        s1col[:, cell_i + 1:cell_i + 2], None, ALU.mult))
                    v.emit(lambda e, bw=bw, sl_=sl_: e.tensor_single_scalar(
                        t1b[sl_][:, 0:bw], tmpb[sl_][:, 0:bw], 0.5, ALU.is_ge))
                    tle = v.emit(lambda e, bw=bw, sl_=sl_: e.tensor_single_scalar(
                        wband[sl_][:, 0:bw], tmpb[sl_][:, 0:bw], -0.5,
                        ALU.is_le), tok=True)
                    ld_tok[sl_] = tle
                    g.wait(tle)
                    g.wait(ld_tok.get((sl_, "sd")))
                    tqw = g.emit(lambda e, bw=bw, sl_=sl_: e.tensor_sub(
                        wqb[sl_][:, 0:bw], t1b[sl_][:, 0:bw],
                        wband[sl_][:, 0:bw]), tok=True)
                    ld_tok[(sl_, "g")] = tqw
                    sd.wait(tqw)
                    tsp = sd.emit(lambda e, k=k, h2=h2, bw=bw, sl_=sl_,
                                  scr_name=scr_name: e.dma_start(
                        out=self.scr[scr_name][128 * k:128 * (k + 1),
                                               bw * h2:bw * (h2 + 1)],
                        in_=wqb[sl_][:, 0:bw]), tok=True)
                    ld_tok[(sl_, "sd")] = tsp
                    nb += 1
            c.barrier()

        wquant("w1", D_TILES, HF1, "s_wq1", 0)
        if getattr(self, "hw_probe", None) == "cellhw":
            tk1 = gd.emit(lambda e: e.dma_start(
                out=self.x_out[0:1, 0:8], in_=cells[0:1, 0:8]), tok=True)
            tk3 = gd.emit(lambda e: e.dma_start(
                out=self.x_out[128:256, 0:4], in_=s1col[:, 0:4]), tok=True)
            g.stream.append(["wait", (gd, tk3[1]), None])
            return
        wquant("w2", H1_TILES, D, "s_wq2", 2)
        if getattr(self, "hw_probe", None) == "wq1hw":
            tks = None
            for i in range(D_TILES):
                tks = gd.emit(lambda e, i=i: e.dma_start(
                    out=self.x_out[128 * i:128 * (i + 1), :],
                    in_=self.scr["s_wq1"][128 * i:128 * (i + 1), 0:T]), tok=True)
            g.stream.append(["wait", (gd, tks[1]), None])
            return

        xq1 = [c.sb(f"f1xq{i}", [128, T], BF16, OFF_A + i * 4096)
               for i in range(D_TILES)]
        macc2 = c.sb("f1mac", [128, T], F32, OFF_C)
        sxwb = [c.sb(f"f1sxw{i}", [128, 512], F32, OFF_C + 8192 + i * 2048)
                for i in range(NCH)]
        rowb = c.sb("f1row", [1, T], F32, OFF_C + 16384)
        h1row = c.sb("f1h1r", [128, 2050], F32, OFF_W)
        cvb = [c.sb(f"f1cv{i}", [128, T], F32, OFF_W + 8256 + i * 8192)
               for i in range(2)]
        hh_r = c.sb("f1hh", [128, T], BF16, OFF_W + 24640)
        hl_r = c.sb("f1hl", [128, T], BF16, OFF_W + 28736)
        parb = c.sb("f1par", [128, 512], F32, OFF_W + 32832)
        negb = c.sb("f1neg", [128, T], F32, OFF_C + 16384)
        qwh = [c.sb(f"f1qw{i}", [128, 1024], BF16, OFF_W + 34880 + i * 2048)
               for i in range(2)]

        for i in range(D_TILES):
            y.emit(lambda e, i=i: e.dma_start(
                out=xq1[i][:], in_=self.scr["s_xq1"][128 * i:128 * (i + 1), :]))
        tsx = y.emit(lambda e: e.dma_start(
            out=rowb[:], in_=self.scr["s_sxi1"][:, :]), tok=True)
        v.wait(tsx)
        tsw = v.emit(lambda e: e.tensor_scalar(
            rowb[:], rowb[:], cells[0:1, 0:1], None, ALU.mult), tok=True)
        g.wait(tsw)
        for ch in range(NCH):
            g.emit(lambda e, ch=ch: e.partition_broadcast(
                sxwb[ch][:], rowb[0:1, 512 * ch:512 * (ch + 1)]), tok=True)
        v.emit(lambda e: e.memset(h1row[:, 0:1], 0.0))
        v.emit(lambda e: e.memset(h1row[:, 2049:2050], 0.0))
        c.barrier()
        wslot_tok = [None, None]
        ev = [None] * 8
        row_free = None
        cv_free = [None, None]
        th1_free = None
        mx_free = None
        par_free = None
        for j in range(H1_TILES):
            slot = j % 2
            gd.wait(wslot_tok[slot])
            tqw = gd.emit(lambda e, sl_=slot, j=j: e.dma_start(
                out=qwh[sl_][:].rearrange("p (k n) -> p k n", n=128),
                in_=self.scr["s_wq1"].rearrange("(k p) n -> p k n", p=128)
                [:, :, 128 * j:128 * (j + 1)]), tok=True)
            t.wait(tqw)
            bank0 = (j % 2) * 4
            mm_tok = [None] * NCH
            for ch in range(NCH):
                sl = slice(512 * ch, 512 * (ch + 1))
                psb = self.ps[bank0 + ch]
                t.wait(ev[bank0 + ch])
                for k in range(D_TILES):
                    t.emit(lambda e, k=k, sl=sl, sl_=slot, psb=psb: e.matmul(
                        psb[:], qwh[sl_][:, 128 * k:128 * (k + 1)],
                        xq1[k][:, sl], start=(k == 0),
                        stop=(k == D_TILES - 1)))
                mm_tok[ch] = c.t.cur()
            wslot_tok[slot] = mm_tok[NCH - 1]
            v.wait(row_free)
            for ch in range(NCH):
                v.wait(mm_tok[ch])
                v.emit(lambda e, ch=ch, bank0=bank0: e.tensor_mul(
                    h1row[:, 1 + 512 * ch:1 + 512 * (ch + 1)],
                    self.ps[bank0 + ch][:], sxwb[ch][:]))
            tevc = c.v.cur()
            for ch in range(NCH):
                ev[bank0 + ch] = tevc
            pj = j % 2
            v.wait(cv_free[pj])
            v.emit(lambda e, j=j, pj=pj: e.tensor_scalar(
                cvb[pj][:], h1row[:, 1:2049], self.c1w[1][:, j:j + 1],
                self.c1b[:, j:j + 1], ALU.mult, ALU.add))
            v.emit(lambda e, j=j, pj=pj: e.scalar_tensor_tensor(
                cvb[pj][:], h1row[:, 0:2048], self.c1w[0][:, j:j + 1],
                cvb[pj][:], ALU.mult, ALU.add))
            trow = v.emit(lambda e, j=j, pj=pj: e.scalar_tensor_tensor(
                cvb[pj][:], h1row[:, 2:2050], self.c1w[2][:, j:j + 1],
                cvb[pj][:], ALU.mult, ALU.add), tok=True)
            row_free = trow
            s.wait(trow)
            tsil = s.emit(lambda e, pj=pj: e.activation(
                out=cvb[pj][:], in_=cvb[pj][:], func=AF.Silu), tok=True)
            v.wait(tsil)
            v.wait(th1_free)
            v.emit(lambda e, pj=pj: e.tensor_copy(hh_r[:], cvb[pj][:]))
            tcpy = c.v.cur()
            g.wait(tcpy)
            tsp = g.emit(lambda e, pj=pj: e.tensor_sub(
                hl_r[:], cvb[pj][:], hh_r[:]), tok=True)
            sd.wait(tsp)
            sd.emit(lambda e, j=j: e.dma_start(
                out=self.scr["s_h1h"][128 * j:128 * (j + 1), :], in_=hh_r[:]))
            th1 = sd.emit(lambda e, j=j: e.dma_start(
                out=self.scr["s_h1l"][128 * j:128 * (j + 1), :], in_=hl_r[:]),
                tok=True)
            th1_free = th1
            v.wait(par_free)
            tabs = v.emit(lambda e, pj=pj: e.scalar_tensor_tensor(
                negb[:], cvb[pj][:], -1.0, cvb[pj][:], ALU.mult, ALU.max),
                tok=True)
            cv_free[pj] = tabs
            g.wait(tabs)
            for ch in range(NCH):
                g.wait(mx_free)
                tpr = g.emit(lambda e, ch=ch: e.partition_all_reduce(
                    parb[:], negb[:, 512 * ch:512 * (ch + 1)], channels=128,
                    reduce_op=ReduceOp.max), tok=True)
                v.wait(tpr)
                if j == 0:
                    tmx = v.emit(lambda e, ch=ch: e.tensor_copy(
                        macc2[:, 512 * ch:512 * (ch + 1)], parb[:]), tok=True)
                else:
                    tmx = v.emit(lambda e, ch=ch: e.tensor_max(
                        macc2[:, 512 * ch:512 * (ch + 1)],
                        macc2[:, 512 * ch:512 * (ch + 1)], parb[:]), tok=True)
                mx_free = tmx
                par_free = tpr
        c.barrier()
        if getattr(self, "hw_probe", None) == "h1hw":
            tks = None
            for i in range(D_TILES):
                tks = gd.emit(lambda e, i=i: e.dma_start(
                    out=self.x_out[128 * i:128 * (i + 1), :],
                    in_=self.scr["s_h1h"][128 * i:128 * (i + 1), :]), tok=True)
            g.stream.append(["wait", (gd, tks[1]), None])
            return
        xq2 = [c.sb(f"f1x2q{i}", [128, 1024], BF16, OFF_A + i * 2048)
               for i in range(H1_TILES)]
        rbc2 = c.sb("f1rbc2", [128, T], F32, OFF_W)
        xrb2 = c.sb("f1xr2", [128, 1024], F32, OFF_W + 8192)
        w2b = [c.sb(f"f1w2b{i}", [128, H1_TILES * 128], BF16,
                    OFF_W + 16384 + i * 8192) for i in range(2)]
        sxw2 = [c.sb(f"f1sx2{i}", [128, 512], F32, OFF_W + 32768 + i * 2048)
                for i in range(NCH)]
        hqld = [c.sb(f"f1hq{i}", [128, 1024], BF16, OFF_C + i * 2048)
                for i in range(2)]
        qf2 = c.sb("f1qf2", [128, 1024], F16, OFF_C + 4096)
        rowb2 = c.sb("f1row2", [1, T], F32, OFF_C + 16384)

        for ch in range(NCH):
            sl = slice(512 * ch, 512 * (ch + 1))
            v.emit(lambda e, sl=sl: e.tensor_scalar_max(
                rbc2[:, sl], macc2[:, sl], 1e-5))
        v.emit(lambda e: e.tensor_scalar(
            rowb2[0:1, :], rbc2[0:1, :], 1.0 / 127.0,
            cells[0:1, 2:3], ALU.mult, ALU.mult))
        tsx2 = c.v.cur()
        g.wait(tsx2)
        for ch in range(NCH):
            g.emit(lambda e, ch=ch: e.partition_broadcast(
                sxw2[ch][:], rowb2[0:1, 512 * ch:512 * (ch + 1)]), tok=True)
        v.emit(lambda e: e.reciprocal(rbc2[:], rbc2[:]))
        v.emit(lambda e: e.tensor_scalar_mul(rbc2[:], rbc2[:], 127.0))
        c.barrier()
        for half in range(2):
            th0 = half * 1024
            hsl = slice(th0, th0 + 1024)
            h_free = None
            for j in range(H1_TILES):
                tld = y.emit(lambda e, j=j, hsl=hsl: e.dma_start(
                    out=hqld[0][:], in_=self.scr["s_h1h"]
                    [128 * j:128 * (j + 1), hsl]), tok=True)
                tld2 = y.emit(lambda e, j=j, hsl=hsl: e.dma_start(
                    out=hqld[1][:], in_=self.scr["s_h1l"]
                    [128 * j:128 * (j + 1), hsl]), tok=True)
                v.wait(tld2)
                v.emit(lambda e: e.tensor_add(xrb2[:], hqld[0][:], hqld[1][:]))
                tyw = c.v.cur()
                y.wait(tyw)
                v.emit(lambda e, th0=th0: e.tensor_mul(
                    xrb2[:], xrb2[:], rbc2[:, th0:th0 + 1024]))
                v.emit(lambda e: e.tensor_scalar(
                    qf2[:], xrb2[:], 1536.0, 1408.0, ALU.add, ALU.max))
                v.emit(lambda e, j=j: e.tensor_scalar(
                    xq2[j][:], qf2[:], 1663.0, 1536.0, ALU.min, ALU.subtract))
            tq2 = c.v.cur()
            t.wait(tq2)
            ev2 = [None] * 4
            wslot2 = [None, None]
            for d in range(D_TILES):
                slot = d % 2
                gd.wait(wslot2[slot])
                tqw = gd.emit(lambda e, sl_=slot, d=d: e.dma_start(
                    out=w2b[sl_][:].rearrange("p (k n) -> p k n", n=128),
                    in_=self.scr["s_wq2"].rearrange("(k p) n -> p k n", p=128)
                    [:, :, 128 * d:128 * (d + 1)]), tok=True)
                t.wait(tqw)
                for ch in range(2):
                    gsl = slice(th0 + 512 * ch, th0 + 512 * (ch + 1))
                    lsl = slice(512 * ch, 512 * (ch + 1))
                    bank = (d % 2) * 2 + ch
                    psb = self.ps[bank]
                    t.wait(ev2[bank])
                    for k in range(H1_TILES):
                        t.emit(lambda e, k=k, lsl=lsl, sl_=slot, psb=psb:
                               e.matmul(
                            psb[:], w2b[sl_][:, 128 * k:128 * (k + 1)],
                            xq2[k][:, lsl], start=(k == 0),
                            stop=(k == H1_TILES - 1)))
                    tmm = c.t.cur()
                    if ch == 1:
                        wslot2[slot] = tmm
                    v.wait(tmm)
                    v.emit(lambda e, ch=ch, hsl=hsl, psb=psb, half=half:
                           e.tensor_mul(
                        xrb2[:, 0:512], psb[:],
                        sxw2[2 * half + ch][:]))
                    tva = v.emit(lambda e, d=d, gsl=gsl: e.tensor_add(
                        self.x2[d][:, gsl], self.x2[d][:, gsl],
                        xrb2[:, 0:512]), tok=True)
                    ev2[bank] = tva
        c.barrier()

    def ph_store(self):
        c = self.c
        c.barrier()
        tks = None
        for i in range(D_TILES):
            tks = c.y.emit(lambda e, i=i: e.dma_start(
                out=self.x_out[128 * i:128 * (i + 1), :],
                in_=self.x2[i][:]), tok=True)
        c.y.stream.append(["wait", (c.y, tks[1]), None])

    def ph_ffn0(self):
        c = self.c
        t, v, s, g, y, sd, gd = c.t, c.v, c.s, c.g, c.y, c.sd, c.gd
        hh_t, hl_t = [], []
        for j in range(H0_TILES):
            if j < 16:
                off = OFF_A + j * 4096
            elif j < 22:
                off = OFF_C + (j - 16) * 4096
            else:
                off = OFF_W + 28672 + (j - 22) * 4096
            hh_t.append(c.sb(f"f0hh{j}", [128, 1024], BF16, off))
            hl_t.append(c.sb(f"f0hl{j}", [128, 1024], BF16, off + 2048))
        wg_b = [c.sb(f"f0wg{i}", [128, 1024], BF16, OFF_W + i * 2048)
                for i in range(2)]
        wu_b = [c.sb(f"f0wu{i}", [128, 1024], BF16, OFF_W + 4096 + i * 2048)
                for i in range(2)]
        wd_b = [c.sb(f"f0wd{i}", [128, HF0], BF16, OFF_W + 8192 + i * 6144)
                for i in range(2)]
        sg_t = [c.sb(f"f0sg{i}", [128, 512], F32, OFF_W + 20480 + i * 2048)
                for i in range(2)]
        hf_t = [c.sb(f"f0hf{i}", [128, 512], F32, OFF_W + 24576 + i * 2048)
                for i in range(2)]
        xt_t = [c.sb(f"f0xt{i}", [128, 512], F32, OFF_W + 36864 + i * 2048)
                for i in range(2)]
        x1_t = [c.sb(f"f0x1{i}", [128, 512], F32, OFF_W + 40960 + i * 2048)
                for i in range(2)]

        def wband_3d(whbm, rows, c0, cw):
            kt = rows // 128
            return whbm.rearrange("(j p) n -> p j n", p=128)[:, :, c0:c0 + cw]

        ev_tok = [None] * 8
        sg_free = [None, None]
        x_free = [None, None]
        for half in range(2):
            th0 = half * 1024
            slot_tok = [None, None]
            for j in range(H0_TILES):
                slot = j % 2
                gd.wait(slot_tok[slot])
                tkg = gd.emit(lambda e, sl_=slot, j=j: e.dma_start(
                    out=wg_b[sl_][:].rearrange("p (j n) -> p j n", n=128),
                    in_=wband_3d(self.hbm["wg0"], D, j * 128, 128)), tok=True)
                tku = gd.emit(lambda e, sl_=slot, j=j: e.dma_start(
                    out=wu_b[sl_][:].rearrange("p (j n) -> p j n", n=128),
                    in_=wband_3d(self.hbm["wu0"], D, j * 128, 128)), tok=True)
                t.wait(tku)
                bank0 = (j % 2) * 4
                for ch in range(2):
                    sl = slice(th0 + 512 * ch, th0 + 512 * (ch + 1))
                    psg, psu = self.ps[bank0 + ch], self.ps[bank0 + 2 + ch]
                    t.wait(ev_tok[bank0 + ch])
                    t.wait(ev_tok[bank0 + 2 + ch])
                    for k in range(D_TILES):
                        for pi, mv in ((0, self.xnh), (1, self.xnl)):
                            t.emit(lambda e, k=k, sl=sl, sl_=slot, mv=mv,
                                   psg=psg, pi=pi: e.matmul(
                                psg[:], wg_b[sl_][:, 128 * k:128 * (k + 1)],
                                mv[k][:, sl], start=(k == 0 and pi == 0),
                                stop=(k == D_TILES - 1 and pi == 1)))
                    for k in range(D_TILES):
                        for pi, mv in ((0, self.xnh), (1, self.xnl)):
                            t.emit(lambda e, k=k, sl=sl, sl_=slot, mv=mv,
                                   psu=psu, pi=pi: e.matmul(
                                psu[:], wu_b[sl_][:, 128 * k:128 * (k + 1)],
                                mv[k][:, sl], start=(k == 0 and pi == 0),
                                stop=(k == D_TILES - 1 and pi == 1)))
                    tmm = c.t.cur()
                    s.wait(tmm)
                    s.wait(sg_free[ch])
                    tsg = s.emit(lambda e, psg=psg, ch=ch: e.activation(
                        out=sg_t[ch][:], in_=psg[:], func=AF.Silu,
                        scale=1.0 / 127.0), tok=True)
                    v.wait(tsg)
                    v.emit(lambda e, psu=psu, ch=ch: e.scalar_tensor_tensor(
                        hf_t[ch][:], psu[:], 1.0 / 127.0, sg_t[ch][:],
                        ALU.mult, ALU.mult))
                    hsl = slice(512 * ch, 512 * (ch + 1))
                    v.emit(lambda e, j=j, hsl=hsl, ch=ch: e.tensor_copy(
                        hh_t[j][:, hsl], hf_t[ch][:]))
                    tkl = v.emit(lambda e, j=j, hsl=hsl, ch=ch:
                                 e.scalar_tensor_tensor(
                        hl_t[j][:, hsl], hh_t[j][:, hsl], -1.0, hf_t[ch][:],
                        ALU.mult, ALU.add), tok=True)
                    sg_free[ch] = tkl
                    ev_tok[bank0 + ch] = tkl
                    ev_tok[bank0 + 2 + ch] = tkl
                slot_tok[slot] = c.t.cur()
            c.barrier()
            slot_tok = [None, None]
            for d in range(D_TILES):
                slot = d % 2
                gd.wait(slot_tok[slot])
                tkd = gd.emit(lambda e, sl_=slot, d=d: e.dma_start(
                    out=wd_b[sl_][:].rearrange("p (j n) -> p j n", n=128),
                    in_=wband_3d(self.hbm["wd0"], HF0, d * 128, 128)), tok=True)
                t.wait(tkd)
                for ch in range(2):
                    sl = slice(th0 + 512 * ch, th0 + 512 * (ch + 1))
                    hsl = slice(512 * ch, 512 * (ch + 1))
                    bank = 4 + (d % 2) * 2 + ch
                    psd = self.ps[bank]
                    t.wait(ev_tok[bank])
                    y.wait(x_free[ch])
                    tx = y.emit(lambda e, d=d, sl=sl, ch=ch: e.dma_start(
                        out=xt_t[ch][:],
                        in_=self.hbm["x"][128 * d:128 * (d + 1), sl]),
                        tok=True)
                    for k in range(H0_TILES):
                        for pi, mvt in ((0, hh_t), (1, hl_t)):
                            t.emit(lambda e, k=k, hsl=hsl, sl_=slot, mvt=mvt,
                                   psd=psd, pi=pi: e.matmul(
                                psd[:], wd_b[sl_][:, 128 * k:128 * (k + 1)],
                                mvt[k][:, hsl], start=(k == 0 and pi == 0),
                                stop=(k == H0_TILES - 1 and pi == 1)))
                    tmm = c.t.cur()
                    v.wait(tmm)
                    v.wait(tx)
                    tv1 = v.emit(lambda e, psd=psd, ch=ch: e.scalar_tensor_tensor(
                        x1_t[ch][:], psd[:], 1.0 / 127.0, xt_t[ch][:], ALU.mult,
                        ALU.add), tok=True)
                    ev_tok[bank] = tv1
                    x_free[ch] = tv1
                    sd.wait(tv1)
                    tst = sd.emit(lambda e, d=d, sl=sl, ch=ch: e.dma_start(
                        out=self.scr["s_x1"][128 * d:128 * (d + 1), sl],
                        in_=x1_t[ch][:]), tok=True)
                    v.wait(tst)
                slot_tok[slot] = c.t.cur()
            if half == 0:
                vtk = c.t.cur()
                v.wait(vtk)
        c.barrier()

    def build(self, upto):
        import contextlib
        nc = self.nc

        def scoped(name, fn, *a):
            Eng.phase[0] = name
            out = fn(*a)
            Eng.phase[0] = None
            return out
        self._scoped = scoped
        self.consts_setup()
        scoped("ln0", self.ph_ln0_xn)
        if upto == "xn":
            self.finalize_dev()
            return
        scoped('ffn0', self.ph_ffn0)
        if upto == "ffn0":
            self.finalize_dev()
            return
        self.win_ev = [None] * 8
        scoped('win', self.ph_win)
        if upto == "win":
            self.finalize_dev()
            return
        self.wdt_ev = [None] * 8
        scoped('wdt', self.ph_wdt_bc)
        if upto == "wdt":
            self.finalize_dev()
            return
        scoped('scan', self.ph_scan)
        if upto == "scan":
            self.finalize_dev()
            return
        scoped('wout0', self.ph_wout0)
        if upto == "l0":
            self.finalize_dev()
            return
        if upto not in ("l0hw",):
            pass
        if upto == "l0hw":
            c = self.c
            c.barrier()
            tks = None
            for i in range(D_TILES):
                tks = c.y.emit(lambda e, i=i: e.dma_start(
                    out=self.x_out[128 * i:128 * (i + 1), :],
                    in_=self.x1f[i][:]), tok=True)
            c.y.stream.append(["wait", (c.y, tks[1]), None])
            return
        scoped('ln1', self.ph_ln1)
        if upto == "ln1":
            self.finalize_dev()
            return
        scoped('proj1', self.ph_proj1)
        if upto == "proj1":
            self.finalize_dev()
            return
        scoped('attn', self.ph_attn)
        if upto == "attn":
            self.finalize_dev()
            return
        scoped('wout1', self.ph_wout1)
        if upto == "wout1":
            self.finalize_dev()
            return
        if upto == "wout1hw":
            c = self.c
            c.barrier()
            tks = None
            for i in range(D_TILES):
                tks = c.y.emit(lambda e, i=i: e.dma_start(
                    out=self.x_out[128 * i:128 * (i + 1), :],
                    in_=self.x2[i][:]), tok=True)
            c.y.stream.append(["wait", (c.y, tks[1]), None])
            return
        self.hw_probe = upto if upto.endswith("hw") else None
        scoped('ffn1', self.ph_ffn1)
        if upto in ("wq1hw", "h1hw", "cellhw"):
            return
        scoped('store', self.ph_store)

    def finalize_dev(self):
        c = self.c
        c.barrier()
        tk = c.y.emit(lambda e: e.dma_start(
            out=self.x_out[0:128, 0:128], in_=self.ident[:]), tok=True)
        c.y.stream.append(["wait", (c.y, tk[1]), None])


def prep_core_inputs(inputs, core):
    a = {k: np.asarray(v) for k, v in inputs.items()}
    tp = np.ascontiguousarray
    im = {
        "x": tp(a["L0_attn_x"][core].T),
        "ln0_scale": tp(a["L0_attn_norm_scale"].reshape(D, 1)),
        "ln0_bias": tp(a["L0_attn_norm_bias"].reshape(D, 1)),
        "w_in": tp(a["L0_attn_W_in"]),
        "conv_w": tp(a["L0_attn_conv_w"]),
        "conv_b": tp(a["L0_attn_conv_b"].reshape(DI, 1)),
        "w_dt": tp(a["L0_attn_W_dt"]),
        "b_dt": tp(a["L0_attn_b_dt"].reshape(DI, 1)),
        "a_log": tp(a["L0_attn_A_log"]),
        "d_param": tp(a["L0_attn_D_param"].reshape(DI, 1)),
        "w_bc": tp(np.concatenate([a["L0_attn_W_B"], a["L0_attn_W_C"]],
                                  axis=1)),
        "w_out0": tp(a["L0_attn_W_out"]),
        "wg0": tp(a["L0_ffn_w_gate"].T),
        "wu0": tp(a["L0_ffn_w_up"].T),
        "wd0": tp(a["L0_ffn_w_down"].T),
        "w_kvd": tp(a["L1_attn_w_kv_down"]),
        "w_kvu": tp(a["L1_attn_w_kv_up"]),
        "w_q": tp(a["L1_attn_w_q"]),
        "w_out1": tp(a["L1_attn_w_out"]),
        "w1": tp(a["L1_ffn_w1"].T),
        "cv1_w": tp(a["L1_ffn_conv_w"].reshape(HF1, 3)),
        "cv1_b": tp(a["L1_ffn_conv_b"].reshape(HF1, 1)),
        "w2": tp(a["L1_ffn_w2"].T),
    }
    return im


_NC_CACHE = {}


def kernel(**inputs):
    if "all" not in _NC_CACHE:
        _NC_CACHE["all"] = build_program("all")
    nc, _ = _NC_CACHE["all"]
    in_maps = [prep_core_inputs(inputs, b) for b in range(8)]
    res = run_bass_kernel_spmd(nc, in_maps, core_ids=list(range(8)))
    out = np.stack([res.results[b]["x_out"].T for b in range(8)], axis=0)
    return np.ascontiguousarray(out.astype(np.float32))


# version 11
# speedup vs baseline: 1.0509x; 1.0509x over previous
import numpy as np
import concourse.bass as bass
import concourse.mybir as mybir
import bass_rust as _bass_rust
from concourse.bass_utils import run_bass_kernel_spmd
from concourse.library_config import all_libraries, standard as _std_lib
from concourse.bass_isa import ReduceOp

F32 = mybir.dt.float32
F16 = mybir.dt.float16
BF16 = mybir.dt.bfloat16
I32 = mybir.dt.int32
AF = mybir.ActivationFunctionType
ALU = mybir.AluOpType

T = 2048
D = 1024
DI = 2048
NST = 16
HF0 = 3072
HF1 = 4096
NH = 16
HS = 64
KVR = 128
HD = NH * HS
NCH = T // 512
DT_TILES = DI // 128
D_TILES = D // 128
H0_TILES = HF0 // 128
H1_TILES = HF1 // 128
LNEPS = 1e-5
EXP_SHIFT = 12.0

SB_BASE = 16576
OFF_A = SB_BASE
OFF_B = SB_BASE + 65536
OFF_C = SB_BASE + 131072
OFF_W = SB_BASE + 155648
OFF_M = SB_BASE + 200704
ARENA = 200704 + 6144
SB_LIMIT = SB_BASE + ARENA


class Eng:
    def __init__(self, name, stream, step=1):
        self.name = name
        self.stream = stream
        self.step = step
        self.count = 0
        self.sem = None
        self.last_waited = {}
        self.dirty = False

    phase = [None]

    def emit(self, fn, tok=False):
        if fn.__closure__:
            for nm, cell in zip(fn.__code__.co_freevars, fn.__closure__):
                try:
                    val = cell.cell_contents
                except ValueError:
                    continue
                assert not isinstance(val, (int, float, slice, str, range)), (
                    f"lambda closes over loop-variable {nm!r}={val!r}; "
                    f"bind it as a default arg")
        rec = ["op", fn, None, Eng.phase[0]]
        self.stream.append(rec)
        self.dirty = True
        if tok or self.step == 16:
            return self.token()
        return None

    def token(self):
        for rec in reversed(self.stream):
            if rec[0] == "op":
                break
        else:
            raise AssertionError("no op to token")
        if rec[2] is not None:
            self.dirty = False
            return rec[2]
        self.count += self.step
        rec[2] = (self, self.count)
        self.dirty = False
        return (self, self.count)

    def cur_q(self):
        return (self, self.count)

    def cur(self):
        if self.dirty:
            return self.token()
        return (self, self.count)

    def wait(self, tok):
        if tok is None:
            return
        prod, val = tok
        if prod is self or (prod.stream is self.stream and prod.step == self.step):
            return
        assert val <= prod.count, f"future token {prod.name}:{val}"
        if self.last_waited.get(prod.name, -1) >= val:
            return
        self.last_waited[prod.name] = val
        self.stream.append(["wait", (prod, val), None, Eng.phase[0]])


class Ctx:
    def __init__(self, nc):
        self.nc = nc
        self.streams = {"t": [], "v": [], "s": [], "g": [], "y": []}
        self.t = Eng("t", self.streams["t"])
        self.v = Eng("v", self.streams["v"])
        self.s = Eng("s", self.streams["s"])
        self.g = Eng("g", self.streams["g"])
        self.y = Eng("y", self.streams["y"], 16)
        self.sd = Eng("sd", self.streams["s"], 16)
        self.gd = Eng("gd", self.streams["g"], 16)
        self.all_engs = [self.t, self.v, self.s, self.g, self.y, self.sd,
                         self.gd]
        self.tn = {}
        self._nsb = 0

    def sb(self, name, shape, dtype, offset):
        nbytes = int(np.prod(shape[1:])) * mybir.dt.size(dtype)
        assert offset + nbytes <= SB_LIMIT, f"{name}: overflow {offset + nbytes}"
        h = self.nc.alloc_sbuf_tensor_at(name, shape, dtype, offset=offset)
        self.tn[name] = h
        return h

    def barrier(self):
        toks = [e.cur() for e in self.all_engs if e.count > 0 or e.dirty]
        for e in [self.t, self.v, self.s, self.g, self.y]:
            for tk in toks:
                e.wait(tk)

    def realize(self):
        nc = self.nc
        self._guards = []
        for e in self.all_engs:
            gd = nc.semaphore(f"sem_{e.name}")
            e.sem = gd.__enter__()
            self._guards.append(gd)
        bg = nc.Block()
        block = bg.__enter__()

        def play(stream):
            def body(eng_obj):
                curph, curid = None, None
                for rec in stream:
                    kind, payload, tok = rec[0], rec[1], rec[2]
                    ph = rec[3] if len(rec) > 3 else None
                    if ph != curph:
                        if curph is not None:
                            nc.leave_named_scope(curph, curid, False)
                        if ph is not None:
                            curid, _ = nc.enter_named_scope(ph, False)
                        curph = ph
                    if kind == "op":
                        inst = payload(eng_obj)
                        if tok is not None:
                            prod, _ = tok
                            inst.then_inc(prod.sem, prod.step)
                    else:
                        prod, val = payload
                        eng_obj.wait_ge(prod.sem, val)
                if curph is not None:
                    nc.leave_named_scope(curph, curid, False)
            return body

        block.tensor(play(self.streams["t"]))
        block.vector(play(self.streams["v"]))
        block.scalar(play(self.streams["s"]))
        block.gpsimd(play(self.streams["g"]))
        block.sync(play(self.streams["y"]))
        bg.__exit__(None, None, None)
        for gd in self._guards:
            gd.__exit__(None, None, None)


def _lower_extended(nc):
    m = {}
    for lib in all_libraries:
        for it in lib.instructions:
            m[it] = m.get(it, 0) | (1 << lib.index)
    _bass_rust.insert_library_loads(nc, m, len(all_libraries), _std_lib.index)
    mybir.codegen_inst_isa_subclasses(nc)


class WStream:

    def __init__(self, P, name, w_hbm, K, N, bw, off, pair=True, int_src=False):
        self.P = P
        c = P.c
        self.w = w_hbm
        self.K, self.N, self.bw = K, N, bw
        self.pair = pair
        self.int_src = int_src
        nb = 2
        self.nb = nb
        self.f32 = None
        sz_h = bw * 2
        cur = off
        self.hbuf = [c.sb(f"{name}_h{i}", [128, bw], BF16, cur + i * sz_h)
                     for i in range(nb)]
        cur += nb * sz_h
        if pair:
            self.lbuf = [c.sb(f"{name}_l{i}", [128, bw], BF16, cur + i * sz_h)
                         for i in range(nb)]
            cur += nb * sz_h
            self.f32 = c.sb(f"{name}_f", [128, bw], F32, cur)
            cur += bw * 4
        self.end = cur
        self.loaded = {}
        self.order = []
        self.seq = 0
        self.slot_busy = [None] * nb

    def load(self, ktile, moff, consumer_eng):
        key = (ktile, moff)
        if key in self.loaded:
            return self.loaded[key]
        P, c = self.P, self.P.c
        slot = self.seq % self.nb
        self.seq += 1
        prev = self.slot_busy[slot]
        k0, m0 = ktile * 128, moff
        src = self.w[k0:k0 + 128, m0:m0 + self.bw]
        if self.pair:
            def ld_f32(e, src=src):
                return e.dma_start(out=self.f32[:], in_=src)
            if prev is not None:
                c.y.wait(prev)
                c.gd.wait(prev)
            tk_f = c.y.emit(ld_f32, tok=True)
            tk_h = c.gd.emit(lambda e, s_=slot, src=src: e.dma_start(
                out=self.hbuf[s_][:], in_=src), tok=True)
            c.v.wait(tk_f)
            c.v.wait(tk_h)
            tk_l = c.v.emit(lambda e, s_=slot: e.tensor_sub(
                self.lbuf[s_][:], self.f32[:], self.hbuf[s_][:]), tok=True)
            res = (self.hbuf[slot], self.lbuf[slot] if self.pair else None, tk_l)
        else:
            if prev is not None:
                c.gd.wait(prev)
            tk_h = c.gd.emit(lambda e, s_=slot, src=src: e.dma_start(
                out=self.hbuf[s_][:], in_=src), tok=True)
            res = (self.hbuf[slot], None, tk_h)
        self.loaded = {key: res}
        self.slot_key = key
        return res

    def mark_done(self, tok):
        slot = (self.seq - 1) % self.nb
        self.slot_busy[slot] = tok


def build_program(upto="all"):
    nc = bass.Bass("TRN2", target_bir_lowering=False,
                   detect_race_conditions=False)
    nc.alloc_sbuf_tensor("arena_blk", [128, ARENA], mybir.dt.uint8)
    c = Ctx(nc)
    hbm = {}

    def hin(name, shape, dtype=F32):
        hbm[name] = nc.dram_tensor(name, shape, dtype, kind="ExternalInput").ap()

    hin("x", [D, T]); hin("ln0_scale", [D, 1]); hin("ln0_bias", [D, 1])
    hin("w_in", [D, 2 * DI]); hin("conv_w", [DI, 3]); hin("conv_b", [DI, 1])
    hin("w_dt", [DI, DI]); hin("b_dt", [DI, 1]); hin("a_log", [DI, NST])
    hin("d_param", [DI, 1]); hin("w_bc", [DI, 2 * NST]); hin("w_out0", [DI, D])
    hin("wg0", [D, HF0], I32); hin("wu0", [D, HF0], I32); hin("wd0", [HF0, D], I32)
    hin("w_kvd", [D, KVR]); hin("w_kvu", [KVR, 2 * HD]); hin("w_q", [D, HD])
    hin("w_out1", [HD, D]); hin("w1", [D, HF1]); hin("cv1_w", [HF1, 3])
    hin("cv1_b", [HF1, 1]); hin("w2", [HF1, D])

    x_out = nc.dram_tensor("x_out", [D, T], F32, kind="ExternalOutput").ap()

    scr = {}

    def hscr(name, shape, dtype=F32):
        scr[name] = nc.dram_tensor(name, shape, dtype).ap()

    hscr("s_x1", [D, T])
    hscr("s_xn2h", [D, T], BF16); hscr("s_xn2l", [D, T], BF16)
    hscr("s_xbh", [DI, T], BF16); hscr("s_xbl", [DI, T], BF16)
    hscr("s_zh", [DI, T], BF16); hscr("s_zl", [DI, T], BF16)
    hscr("s_dt", [DI, T])
    hscr("s_yvh", [DI, T], BF16); hscr("s_yvl", [DI, T], BF16)
    hscr("s_q", [HD, T]); hscr("s_vt", [T, HD]); hscr("s_y1", [HD, T])
    hscr("s_den", [NH, T])
    hscr("s_xq1", [D, T], BF16); hscr("s_sxi1", [1, T])
    hscr("s_h1h", [HF1, T], BF16); hscr("s_h1l", [HF1, T], BF16)
    hscr("s_sxi2", [1, T])
    hscr("s_wq1", [D, HF1], BF16); hscr("s_wq2", [HF1, D], BF16)
    hscr("s_dbg", [1, 4 * T])

    P = Prog(c, hbm, scr, x_out)
    P.build(upto)
    c.realize()
    _lower_extended(nc)
    return nc, P


class Prog:
    def __init__(self, c, hbm, scr, x_out):
        self.c = c
        self.hbm = hbm
        self.scr = scr
        self.x_out = x_out
        self.nc = c.nc
        self.ps = [self.nc.alloc_psum_tensor(f"ps{i}", [128, 512], F32)
                   for i in range(8)]

    def consts_setup(self):
        c = self.c
        h = self.hbm
        M = OFF_M

        def msb(name, shape, dtype=F32):
            nonlocal M
            t_ = c.sb(name, shape, dtype, M)
            M += int(np.prod(shape[1:])) * mybir.dt.size(dtype)
            M = (M + 31) & ~31
            return t_

        self.a_cols = msb("a_cols", [128, DT_TILES * NST])
        self.cw = [msb(f"cw{i}", [128, DT_TILES]) for i in range(3)]
        self.cb0 = msb("cb0", [128, DT_TILES])
        self.bdt = msb("bdt", [128, DT_TILES])
        self.dprm = msb("dprm", [128, DT_TILES])
        self.ln0s = msb("ln0s", [128, D_TILES])
        self.ln0b = msb("ln0b", [128, D_TILES])
        self.c1w = [msb(f"c1w{i}", [128, H1_TILES]) for i in range(3)]
        self.c1b = msb("c1b", [128, H1_TILES])
        self.ident = msb("ident", [128, 128])
        self.ones_col = msb("ones_col", [128, 1])
        self.iota_f = msb("iota_f", [128, 512])
        self.iota_p = msb("iota_p", [128, 1])
        self.cellA = msb("cellA", [1, 8])
        self.zcol = msb("zcol", [128, 1])
        self.s1col4 = msb("s1col4", [128, 6])
        self.state = msb("state", [128, DT_TILES * NST])
        assert M <= SB_LIMIT, f"M overflow {M}"

        y, g, v, s = c.y, c.g, c.v, c.s

        def col_load(dst, src_col, tiles):
            nc_ = src_col.shape[-1]
            return y.emit(lambda e, d=dst, s_=src_col, n=tiles, nc_=nc_:
                          e.dma_start(
                out=d[:, 0:n * nc_].rearrange("b (a c) -> b a c", c=nc_),
                in_=s_.rearrange("(a b) c -> b a c", b=128),
                allow_slow_non_contiguous=True), tok=True)

        tks = [col_load(self.a_cols, h["a_log"], DT_TILES)]
        for i in range(3):
            tks.append(col_load(self.cw[i], h["conv_w"][:, i:i + 1], DT_TILES))
            tks.append(col_load(self.c1w[i], h["cv1_w"][:, i:i + 1], H1_TILES))
        tks.append(col_load(self.cb0, h["conv_b"], DT_TILES))
        tks.append(col_load(self.bdt, h["b_dt"], DT_TILES))
        tks.append(col_load(self.dprm, h["d_param"], DT_TILES))
        tks.append(col_load(self.ln0s, h["ln0_scale"], D_TILES))
        tks.append(col_load(self.ln0b, h["ln0_bias"], D_TILES))
        tks.append(col_load(self.c1b, h["cv1_b"], H1_TILES))
        s.wait(c.y.cur())
        t_a = s.emit(lambda e: e.activation(
            out=self.a_cols[:], in_=self.a_cols[:], func=AF.Exp), tok=True)
        v.wait(t_a)
        v.emit(lambda e: e.tensor_scalar_mul(self.a_cols[:], self.a_cols[:], -1.0))
        gt = g.emit(lambda e: e.iota(self.iota_f[:], [[1, 512]],
                                     channel_multiplier=0,
                                     allow_small_or_imprecise_dtypes=True), tok=True)
        gt2 = g.emit(lambda e: e.iota(self.iota_p[:], [[1, 1]],
                                      channel_multiplier=1,
                                      allow_small_or_imprecise_dtypes=True), tok=True)
        v.wait(gt); v.wait(gt2)
        v.emit(lambda e: e.memset(self.ones_col[:], 1.0))
        v.emit(lambda e: e.tensor_scalar(
            self.ident[:], self.iota_f[:, 0:128], self.iota_p[:, 0:1], None,
            ALU.is_equal))
        v.emit(lambda e: e.memset(self.zcol[:], 0.0))
        c.barrier()

    def build_masks(self, masks):
        v = self.c.v
        for m in range(4):
            v.emit(lambda e, m=m: e.tensor_scalar(
                masks[:, 512 * m:512 * (m + 1)], self.iota_f[:],
                float(128 * m), self.iota_p[:, 0:1], ALU.subtract, ALU.is_lt))
            v.emit(lambda e, m=m: e.tensor_scalar(
                masks[:, 512 * m:512 * (m + 1)],
                masks[:, 512 * m:512 * (m + 1)], -1e30, -EXP_SHIFT,
                ALU.mult, ALU.add))

    def ln_rows(self, tiles_fn, n_tiles, dim, rows, sq_tmp, rowtmp):
        c = self.c
        t, v, s = c.t, c.v, c.s
        for ch in range(NCH):
            ps_s, ps_q = self.ps[4 + (ch % 2) * 2], self.ps[5 + (ch % 2) * 2]
            sl = slice(512 * ch, 512 * (ch + 1))
            for i in range(n_tiles):
                src = tiles_fn(i)
                t.emit(lambda e, src=src, sl=sl, i=i, n=n_tiles: e.matmul(
                    ps_s[0:1, :], self.ones_col[:], src[:, sl],
                    start=(i == 0), stop=(i == n - 1)))
                stk = s.emit(lambda e, src=src, sl=sl: e.activation(
                    out=sq_tmp[:], in_=src[:, sl], func=AF.Square), tok=True)
                t.wait(stk)
                ttk = t.emit(lambda e, i=i, n=n_tiles: e.matmul(
                    ps_q[0:1, :], self.ones_col[:], sq_tmp[:],
                    start=(i == 0), stop=(i == n - 1)), tok=True)
                s.wait(ttk)
            tk_mm = c.t.cur()
            v.wait(tk_mm)
            v.emit(lambda e, sl=sl, dim=dim: e.tensor_scalar_mul(
                rows[0:1, sl], ps_s[0:1, :], 1.0 / dim))
            v.emit(lambda e, sl=sl: e.tensor_mul(
                rowtmp[0:1, 0:512], rows[0:1, sl], rows[0:1, sl]))
            v.emit(lambda e, sl=sl, dim=dim: e.scalar_tensor_tensor(
                rowtmp[0:1, 0:512], ps_q[0:1, :], 1.0 / dim,
                rowtmp[0:1, 0:512], ALU.mult, ALU.subtract))
            tvv = v.emit(lambda e: e.tensor_scalar_add(
                rowtmp[0:1, 0:512], rowtmp[0:1, 0:512], LNEPS), tok=True)
            s.wait(tvv)
            tss = s.emit(lambda e: e.activation(
                out=rowtmp[0:1, 0:512], in_=rowtmp[0:1, 0:512],
                func=AF.Sqrt), tok=True)
            v.wait(tss)
            v.emit(lambda e, ch=ch: e.reciprocal(
                rows[0:1, T + 512 * ch:T + 512 * (ch + 1)],
                rowtmp[0:1, 0:512]))
            tvt = c.v.cur()
            t.wait(tvt)

    def split_pair(self, src_ap, h_ap, l_ap, tok=False):
        v = self.c.v
        v.emit(lambda e: e.tensor_copy(h_ap, src_ap))
        return v.emit(lambda e: e.scalar_tensor_tensor(
            l_ap, h_ap, -1.0, src_ap, ALU.mult, ALU.add), tok=tok)

    def ph_ln0_xn(self):
        c = self.c
        t, v, s, g, y, sd = c.t, c.v, c.s, c.g, c.y, c.sd
        self.xres = [c.sb(f"xres{i}", [128, T], F32, OFF_A + i * 8192)
                     for i in range(D_TILES)]
        self.xnh = [c.sb(f"xnh{i}", [128, T], BF16, OFF_B + i * 8192)
                    for i in range(D_TILES)]
        self.xnl = [c.sb(f"xnl{i}", [128, T], BF16, OFF_B + i * 8192 + 4096)
                    for i in range(D_TILES)]
        sq_tmp = c.sb("p1_sq", [128, 512], F32, OFF_C)
        rows = c.sb("p1_rows", [1, 2 * T], F32, OFF_C + 2048)
        bc0 = c.sb("p1_bc0", [128, 512], F32, OFF_C + 2048 + 16384)
        bc1 = c.sb("p1_bc1", [128, 512], F32, OFF_C + 2048 + 16384 + 2048)
        xn_t = c.sb("p1_xnt", [128, 512], F32, OFF_W)
        xn2_t = c.sb("p1_xn2t", [128, 512], F32, OFF_W + 2048)
        rowtmp = c.sb("p1_rowtmp", [1, 512], F32, OFF_W + 16384)
        dbgrow = c.sb("p1_dbgrow", [1, T], F32, OFF_W + 20480)
        h_t = c.sb("p1_ht", [128, 512], BF16, OFF_W + 4096)
        l_t = c.sb("p1_lt", [128, 512], BF16, OFF_W + 4096 + 1024)

        for i in range(D_TILES):
            y.emit(lambda e, i=i: e.dma_start(
                out=self.xres[i][:], in_=self.hbm["x"][128 * i:128 * (i + 1), :]),
                tok=True)
        c.barrier()
        self.ln_rows(lambda i: self.xres[i], D_TILES, D, rows, sq_tmp, rowtmp)
        c.barrier()
        y.emit(lambda e: e.dma_start(
            out=self.scr["s_dbg"][0:1, 0:2 * T], in_=rows[0:1, 0:2 * T]),
            tok=True)
        c.barrier()
        for ch in range(NCH):
            sl = slice(512 * ch, 512 * (ch + 1))
            g.wait(c.v.cur())
            tb0 = g.emit(lambda e, sl=sl: e.partition_broadcast(
                bc0[:], rows[0:1, sl]), tok=True)
            tb1 = g.emit(lambda e, ch=ch: e.partition_broadcast(
                bc1[:], rows[0:1, T + 512 * ch:T + 512 * (ch + 1)]), tok=True)
            v.wait(tb0); v.wait(tb1)
            ps_s, ps_q = self.ps[0], self.ps[1]
            for i in range(D_TILES):
                v.emit(lambda e, i=i, sl=sl: e.tensor_sub(
                    xn_t[:], self.xres[i][:, sl], bc0[:]))
                tvm = v.emit(lambda e: e.tensor_mul(xn_t[:], xn_t[:], bc1[:]),
                             tok=True)
                v.emit(lambda e, i=i, sl=sl: e.tensor_copy(
                    self.xnh[i][:, sl], xn_t[:]))
                tlo = v.emit(lambda e, i=i, sl=sl: e.scalar_tensor_tensor(
                    self.xnl[i][:, sl], self.xnh[i][:, sl], -1.0, xn_t[:],
                    ALU.mult, ALU.add), tok=True)
                t.wait(tvm)
                t.emit(lambda e, i=i: e.matmul(
                    ps_s[0:1, :], self.ones_col[:], xn_t[:],
                    start=(i == 0), stop=(i == D_TILES - 1)))
                s.wait(tvm)
                stk = s.emit(lambda e: e.activation(
                    out=sq_tmp[:], in_=xn_t[:], func=AF.Square), tok=True)
                t.wait(stk)
                ttk = t.emit(lambda e, i=i: e.matmul(
                    ps_q[0:1, :], self.ones_col[:], sq_tmp[:],
                    start=(i == 0), stop=(i == D_TILES - 1)), tok=True)
                s.wait(ttk)
                v.wait(ttk)
            tk_mm = c.t.cur()
            v.wait(tk_mm)
            v.emit(lambda e, sl=sl: e.tensor_scalar_mul(
                rows[0:1, sl], ps_s[0:1, :], 1.0 / D))
            v.emit(lambda e, sl=sl: e.tensor_mul(
                rowtmp[0:1, 0:512], rows[0:1, sl], rows[0:1, sl]))
            v.emit(lambda e: e.scalar_tensor_tensor(
                rowtmp[0:1, 0:512], ps_q[0:1, :], 1.0 / D,
                rowtmp[0:1, 0:512], ALU.mult, ALU.subtract))
            tvv = v.emit(lambda e: e.tensor_scalar_add(
                rowtmp[0:1, 0:512], rowtmp[0:1, 0:512], LNEPS), tok=True)
            s.wait(tvv)
            tss = s.emit(lambda e: e.activation(
                out=rowtmp[0:1, 0:512], in_=rowtmp[0:1, 0:512],
                func=AF.Sqrt), tok=True)
            v.wait(tss)
            tvr = v.emit(lambda e, ch=ch: e.reciprocal(
                rows[0:1, T + 512 * ch:T + 512 * (ch + 1)],
                rowtmp[0:1, 0:512]), tok=True)
            t.wait(tvr)
        c.barrier()
        for ch in range(NCH):
            sl = slice(512 * ch, 512 * (ch + 1))
            g.wait(c.v.cur())
            tb0 = g.emit(lambda e, sl=sl: e.partition_broadcast(
                bc0[:], rows[0:1, sl]), tok=True)
            tb1 = g.emit(lambda e, ch=ch: e.partition_broadcast(
                bc1[:], rows[0:1, T + 512 * ch:T + 512 * (ch + 1)]), tok=True)
            v.wait(tb0); v.wait(tb1)
            for i in range(D_TILES):
                v.emit(lambda e, i=i, sl=sl: e.tensor_add(
                    xn_t[:], self.xnh[i][:, sl], self.xnl[i][:, sl]))
                v.emit(lambda e: e.tensor_sub(xn_t[:], xn_t[:], bc0[:]))
                v.emit(lambda e: e.tensor_mul(xn_t[:], xn_t[:], bc1[:]))
                v.emit(lambda e, i=i: e.tensor_scalar(
                    xn2_t[:], xn_t[:], self.ln0s[:, i:i + 1],
                    self.ln0b[:, i:i + 1], ALU.mult, ALU.add))
                v.emit(lambda e: e.tensor_copy(h_t[:], xn2_t[:]))
                tsp = v.emit(lambda e: e.scalar_tensor_tensor(
                    l_t[:], h_t[:], -1.0, xn2_t[:], ALU.mult, ALU.add),
                    tok=True)
                sd.wait(tsp)
                sd.emit(lambda e, i=i, sl=sl: e.dma_start(
                    out=self.scr["s_xn2h"][128 * i:128 * (i + 1), sl],
                    in_=h_t[:]), tok=True)
                tdm = sd.emit(lambda e, i=i, sl=sl: e.dma_start(
                    out=self.scr["s_xn2l"][128 * i:128 * (i + 1), sl],
                    in_=l_t[:]), tok=True)
                v.wait(tdm)
        c.barrier()


    def ph_win(self):
        c = self.c
        t, v, s, g, y, sd, gd = c.t, c.v, c.s, c.g, c.y, c.sd, c.gd
        xnh = [c.sb(f"w0xnh{i}", [128, T], BF16, OFF_B + i * 8192)
               for i in range(D_TILES)]
        xnl = [c.sb(f"w0xnl{i}", [128, T], BF16, OFF_B + i * 8192 + 4096)
               for i in range(D_TILES)]
        whb = [c.sb(f"w0wh{i}", [128, 1024], BF16, OFF_W + i * 2048)
               for i in range(2)]
        wlb = [c.sb(f"w0wl{i}", [128, 1024], BF16, OFF_W + 4096 + i * 2048)
               for i in range(2)]
        wfb = [c.sb(f"w0wf{i}", [128, 1024], F32, OFF_W + 8192 + i * 4096)
               for i in range(2)]
        row = c.sb("w0row", [128, 2050], F32, OFF_W + 16384)
        cv1 = c.sb("w0cv", [128, T], F32, OFF_W + 24640)
        xh_r = c.sb("w0xh", [128, T], BF16, OFF_W + 32832)
        xl_r = c.sb("w0xl", [128, T], BF16, OFF_W + 36928)
        ztf = [c.sb(f"w0ztf{i}", [128, 512], F32, OFF_W + 16384 + i * 2048)
               for i in range(2)]
        zh_r = [c.sb(f"w0zh{i}", [128, 512], BF16, OFF_W + 20480 + i * 1024)
                for i in range(2)]
        zl_r = [c.sb(f"w0zl{i}", [128, 512], BF16, OFF_W + 22528 + i * 1024)
                for i in range(2)]

        for i in range(D_TILES):
            y.emit(lambda e, i=i: e.dma_start(
                out=xnh[i][:], in_=self.scr["s_xn2h"][128 * i:128 * (i + 1), :]),
                tok=True)
            y.emit(lambda e, i=i: e.dma_start(
                out=xnl[i][:], in_=self.scr["s_xn2l"][128 * i:128 * (i + 1), :]),
                tok=True)
        tmset = v.emit(lambda e: e.memset(row[:, 0:1], 0.0), tok=True)
        v.emit(lambda e: e.memset(row[:, 2049:2050], 0.0))
        c.barrier()

        wslot_tok = [None, None]
        row_free = None
        zfree = [None, None]
        for j in range(2 * DT_TILES):
            slot = j % 2
            y.wait(wslot_tok[slot])
            gd.wait(wslot_tok[slot])
            tfw = y.emit(lambda e, sl_=slot, j=j: e.dma_start(
                out=wfb[sl_][:].rearrange("p (k n) -> p k n", n=128),
                in_=self.hbm["w_in"].rearrange("(k p) n -> p k n", p=128)
                [:, :, 128 * j:128 * (j + 1)]), tok=True)
            thw = gd.emit(lambda e, sl_=slot, j=j: e.dma_start(
                out=whb[sl_][:].rearrange("p (k n) -> p k n", n=128),
                in_=self.hbm["w_in"].rearrange("(k p) n -> p k n", p=128)
                [:, :, 128 * j:128 * (j + 1)]), tok=True)
            g.wait(tfw)
            g.wait(thw)
            tlw = g.emit(lambda e, sl_=slot: e.tensor_sub(
                wlb[sl_][:], wfb[sl_][:], whb[sl_][:]), tok=True)
            t.wait(tlw)
            bank0 = (j % 2) * 4
            mm_tok = [None] * NCH
            for ch in range(NCH):
                sl = slice(512 * ch, 512 * (ch + 1))
                psb = self.ps[bank0 + ch]
                t.wait(self.win_ev[bank0 + ch])
                for k in range(D_TILES):
                    for wb, mv, st_, sp_ in ((whb, xnh, k == 0, False),
                                             (wlb, xnh, False, False),
                                             (whb, xnl, False, k == D_TILES - 1)):
                        t.emit(lambda e, k=k, sl=sl, sl_=slot, wb=wb, mv=mv,
                               st_=st_, sp_=sp_, psb=psb: e.matmul(
                            psb[:], wb[sl_][:, 128 * k:128 * (k + 1)],
                            mv[k][:, sl], start=st_, stop=sp_))
                mm_tok[ch] = c.t.cur()
            wslot_tok[slot] = mm_tok[NCH - 1]
            if j < DT_TILES:
                s.wait(row_free)
                for ch in range(NCH):
                    s.wait(mm_tok[ch])
                    s.emit(lambda e, ch=ch, bank0=bank0: e.activation(
                        out=row[:, 1 + 512 * ch:1 + 512 * (ch + 1)],
                        in_=self.ps[bank0 + ch][:], func=AF.Copy))
                tevc = c.s.cur()
                for ch in range(NCH):
                    self.win_ev[bank0 + ch] = tevc
                v.wait(tevc)
                v.emit(lambda e, j=j: e.tensor_scalar(
                    cv1[:], row[:, 1:2049], self.cw[1][:, j:j + 1],
                    self.cb0[:, j:j + 1], ALU.mult, ALU.add))
                v.emit(lambda e, j=j: e.scalar_tensor_tensor(
                    cv1[:], row[:, 0:2048], self.cw[0][:, j:j + 1], cv1[:],
                    ALU.mult, ALU.add))
                trow = v.emit(lambda e, j=j: e.scalar_tensor_tensor(
                    cv1[:], row[:, 2:2050], self.cw[2][:, j:j + 1], cv1[:],
                    ALU.mult, ALU.add), tok=True)
                row_free = trow
                s.wait(trow)
                tsil = s.emit(lambda e: e.activation(
                    out=cv1[:], in_=cv1[:], func=AF.Silu), tok=True)
                v.wait(tsil)
                v.emit(lambda e: e.tensor_copy(xh_r[:], cv1[:]))
                tsp = v.emit(lambda e: e.scalar_tensor_tensor(
                    xl_r[:], xh_r[:], -1.0, cv1[:], ALU.mult, ALU.add),
                    tok=True)
                sd.wait(tsp)
                sd.emit(lambda e, j=j: e.dma_start(
                    out=self.scr["s_xbh"][128 * j:128 * (j + 1), :],
                    in_=xh_r[:]), tok=True)
                tspl = sd.emit(lambda e, j=j: e.dma_start(
                    out=self.scr["s_xbl"][128 * j:128 * (j + 1), :],
                    in_=xl_r[:]), tok=True)
                v.wait(tspl)
                s.wait(tspl)
            else:
                jz = j - DT_TILES
                for ch in range(NCH):
                    pidx = ch % 2
                    psb = self.ps[bank0 + ch]
                    s.wait(mm_tok[ch])
                    s.wait(zfree[pidx])
                    tsz = s.emit(lambda e, psb=psb, pidx=pidx: e.activation(
                        out=ztf[pidx][:], in_=psb[:], func=AF.Silu), tok=True)
                    self.win_ev[bank0 + ch] = tsz
                    v.wait(tsz)
                    v.emit(lambda e, pidx=pidx: e.tensor_copy(
                        zh_r[pidx][:], ztf[pidx][:]))
                    tzl = v.emit(lambda e, pidx=pidx:
                                 e.scalar_tensor_tensor(
                        zl_r[pidx][:], zh_r[pidx][:], -1.0, ztf[pidx][:],
                        ALU.mult, ALU.add), tok=True)
                    sd.wait(tzl)
                    sd.emit(lambda e, jz=jz, ch=ch, pidx=pidx: e.dma_start(
                        out=self.scr["s_zh"][128 * jz:128 * (jz + 1),
                                             512 * ch:512 * (ch + 1)],
                        in_=zh_r[pidx][:]), tok=True)
                    tzs = sd.emit(lambda e, jz=jz, ch=ch, pidx=pidx: e.dma_start(
                        out=self.scr["s_zl"][128 * jz:128 * (jz + 1),
                                             512 * ch:512 * (ch + 1)],
                        in_=zl_r[pidx][:]), tok=True)
                    zfree[pidx] = tzs
        c.barrier()


    def ph_wdt_bc(self):
        c = self.c
        t, v, s, g, y, sd, gd = c.t, c.v, c.s, c.g, c.y, c.sd, c.gd
        xbh = [c.sb(f"d0xbh{i}", [128, T], BF16,
                    (OFF_A if i < 8 else OFF_B) + (i % 8) * 8192)
               for i in range(DT_TILES)]
        xbl = [c.sb(f"d0xbl{i}", [128, T], BF16,
                    (OFF_A if i < 8 else OFF_B) + (i % 8) * 8192 + 4096)
               for i in range(DT_TILES)]
        whb = [c.sb(f"d0wh{i}", [128, 2048], BF16, OFF_W + i * 4096)
               for i in range(2)]
        wlb = [c.sb(f"d0wl{i}", [128, 2048], BF16, OFF_W + 8192 + i * 4096)
               for i in range(2)]
        wfb = [c.sb(f"d0wf{i}", [128, 2048], F32, OFF_W + 16384 + i * 8192)
               for i in range(2)]
        dtr = [c.sb(f"d0dt{i}", [128, 512], F32, OFF_W + 32768 + i * 2048)
               for i in range(2)]
        bwh = c.sb("d0bwh", [128, 512], BF16, OFF_W + 36864)
        bwl = c.sb("d0bwl", [128, 512], BF16, OFF_W + 36864 + 1024)
        bwf = c.sb("d0bwf", [128, 512], F32, OFF_W + 38912)
        self.bc_rows = c.sb("bc_rows", [32, T], F32, OFF_C)

        for i in range(DT_TILES):
            y.emit(lambda e, i=i: e.dma_start(
                out=xbh[i][:], in_=self.scr["s_xbh"][128 * i:128 * (i + 1), :]),
                tok=True)
            y.emit(lambda e, i=i: e.dma_start(
                out=xbl[i][:], in_=self.scr["s_xbl"][128 * i:128 * (i + 1), :]),
                tok=True)
        self.xbh_d0, self.xbl_d0 = xbh, xbl
        c.barrier()
        wslot_tok = [None, None]
        dt_free = [None, None]
        for j in range(DT_TILES):
            slot = j % 2
            y.wait(wslot_tok[slot])
            gd.wait(wslot_tok[slot])
            tfw = y.emit(lambda e, sl_=slot, j=j: e.dma_start(
                out=wfb[sl_][:].rearrange("p (k n) -> p k n", n=128),
                in_=self.hbm["w_dt"].rearrange("(k p) n -> p k n", p=128)
                [:, :, 128 * j:128 * (j + 1)]), tok=True)
            thw = gd.emit(lambda e, sl_=slot, j=j: e.dma_start(
                out=whb[sl_][:].rearrange("p (k n) -> p k n", n=128),
                in_=self.hbm["w_dt"].rearrange("(k p) n -> p k n", p=128)
                [:, :, 128 * j:128 * (j + 1)]), tok=True)
            g.wait(tfw)
            g.wait(thw)
            tlw = g.emit(lambda e, sl_=slot: e.tensor_sub(
                wlb[sl_][:], wfb[sl_][:], whb[sl_][:]), tok=True)
            t.wait(tlw)
            bank0 = (j % 2) * 4
            for ch in range(NCH):
                sl = slice(512 * ch, 512 * (ch + 1))
                psb = self.ps[bank0 + ch]
                t.wait(self.wdt_ev[bank0 + ch])
                for k in range(DT_TILES):
                    for wb, mv, st_, sp_ in ((whb, xbh, k == 0, False),
                                             (wlb, xbh, False, False),
                                             (whb, xbl, False,
                                              k == DT_TILES - 1)):
                        t.emit(lambda e, k=k, sl=sl, sl_=slot, wb=wb, mv=mv,
                               st_=st_, sp_=sp_, psb=psb: e.matmul(
                            psb[:], wb[sl_][:, 128 * k:128 * (k + 1)],
                            mv[k][:, sl], start=st_, stop=sp_))
                tmm = c.t.cur()
                if ch == NCH - 1:
                    wslot_tok[slot] = tmm
                pidx = ch % 2
                s.wait(tmm)
                s.wait(dt_free[pidx])
                tse = s.emit(lambda e, psb=psb, j=j, pidx=pidx: e.activation(
                    out=dtr[pidx][:], in_=psb[:], func=AF.Exp,
                    bias=self.bdt[:, j:j + 1]), tok=True)
                self.wdt_ev[bank0 + ch] = tse
                v.wait(tse)
                tv1 = v.emit(lambda e, pidx=pidx: e.tensor_scalar_add(
                    dtr[pidx][:], dtr[pidx][:], 1.0), tok=True)
                s.wait(tv1)
                tso = s.emit(lambda e, pidx=pidx: e.activation(
                    out=dtr[pidx][:], in_=dtr[pidx][:], func=AF.Ln), tok=True)
                sd.wait(tso)
                tsd = sd.emit(lambda e, j=j, sl=sl, pidx=pidx: e.dma_start(
                    out=self.scr["s_dt"][128 * j:128 * (j + 1), sl],
                    in_=dtr[pidx][:]), tok=True)
                dt_free[pidx] = tsd
        tbw = y.emit(lambda e: e.dma_start(
            out=bwf[:].rearrange("p (k n) -> p k n", n=32),
            in_=self.hbm["w_bc"].rearrange("(k p) n -> p k n", p=128)),
            tok=True)
        tbh = gd.emit(lambda e: e.dma_start(
            out=bwh[:].rearrange("p (k n) -> p k n", n=32),
            in_=self.hbm["w_bc"].rearrange("(k p) n -> p k n", p=128)),
            tok=True)
        g.wait(tbw)
        g.wait(tbh)
        tbl = g.emit(lambda e: e.tensor_sub(bwl[:], bwf[:], bwh[:]), tok=True)
        t.wait(tbl)
        for ch in range(NCH):
            sl = slice(512 * ch, 512 * (ch + 1))
            psb = self.ps[ch % 2]
            t.wait(self.wdt_ev[ch % 2])
            t.wait(self.wdt_ev[4 + ch % 2])
            for k in range(DT_TILES):
                for wb, mv, st_, sp_ in ((bwh, xbh, k == 0, False),
                                         (bwl, xbh, False, False),
                                         (bwh, xbl, False, k == DT_TILES - 1)):
                    t.emit(lambda e, k=k, sl=sl, wb=wb, mv=mv, st_=st_,
                           sp_=sp_, psb=psb: e.matmul(
                        psb[0:32, :], wb[:, 32 * k:32 * (k + 1)],
                        mv[k][:, sl], start=st_, stop=sp_))
            tmm = c.t.cur()
            v.wait(tmm)
            tvc = v.emit(lambda e, sl=sl, psb=psb: e.tensor_copy(
                self.bc_rows[:, sl], psb[0:32, :]), tok=True)
            self.wdt_ev[ch % 2] = tvc
            t.wait(tvc)
        c.barrier()


    def ph_scan(self):
        c = self.c
        t, v, s, g, y, sd, gd = c.t, c.v, c.s, c.g, c.y, c.sd, c.gd
        dtt = [c.sb(f"scdt{i}", [128, 512], F32, OFF_A + i * 2048)
               for i in range(DT_TILES)]
        dtxt = [c.sb(f"scdx{i}", [128, 512], F32, OFF_A + 32768 + i * 2048)
                for i in range(DT_TILES)]
        ytl = [c.sb(f"scy{i}", [128, 512], F32, OFF_B + i * 2048)
               for i in range(DT_TILES)]
        xbh_c = [c.sb(f"scxh{i}", [128, 512], BF16, OFF_B + 32768 + i * 1024)
                 for i in range(DT_TILES)]
        xbl_c = [c.sb(f"scxl{i}", [128, 512], BF16, OFF_B + 49152 + i * 1024)
                 for i in range(DT_TILES)]
        ab = [c.sb(f"scab{i}", [128, 512], F32, OFF_W + i * 2048)
              for i in range(4)]
        ub = [c.sb(f"scub{i}", [128, 512], F32, OFF_W + 8192 + i * 2048)
              for i in range(4)]
        hb = [c.sb(f"schb{i}", [128, 512], F32, OFF_W + 16384 + i * 2048)
              for i in range(4)]
        wb_ = c.sb("scwb", [128, 512], F32, OFF_W + 24576)
        xrb = c.sb("scxr", [128, 512], F32, OFF_W + 26624)
        wb2_ = [c.sb(f"scwb2{i}", [128, 512], F32, OFF_W + 28672 + i * 2048)
                for i in range(2)]
        wbv = c.sb("scwbv", [128, 512], F32, OFF_W + 32768)
        bcb = [c.sb(f"scbc{i}", [128, 512], F32, OFF_W + 36864 + i * 2048)
               for i in range(2)]
        ccb = [c.sb(f"sccc{i}", [128, 512], F32, OFF_W + 40960 + i * 2048)
               for i in range(2)]
        t2b = c.sb("sct2", [128, 512], F32, OFF_W + 0)
        ztb = c.sb("sczt", [128, 512], F32, OFF_W + 2048)
        yvf = c.sb("scyv", [128, 512], F32, OFF_W + 4096)
        yvh = [c.sb(f"scyh{i}", [128, 512], BF16, OFF_W + 6144 + i * 1024)
               for i in range(2)]
        yvl = [c.sb(f"scyl{i}", [128, 512], BF16, OFF_W + 8192 + i * 1024)
               for i in range(2)]
        zhb = [c.sb(f"sczh{i}", [128, 512], BF16, OFF_W + 10240 + i * 1024)
               for i in range(2)]
        zlb = [c.sb(f"sczl{i}", [128, 512], BF16, OFF_W + 12288 + i * 1024)
               for i in range(2)]
        sel = c.sb("scsel", [32, 2 * NST * 128], F32, OFF_C + 8192)

        tio = g.emit(lambda e: e.iota(
            sel[:], [[1, 2 * NST], [0, 128]], channel_multiplier=0,
            allow_small_or_imprecise_dtypes=True), tok=True)
        v.wait(tio)
        v.emit(lambda e: e.tensor_scalar(
            sel[:], sel[:], self.iota_p[0:32, 0:1], None, ALU.is_equal))
        v.emit(lambda e: e.memset(self.state[:], 0.0))
        c.barrier()

        ab_free = [None] * 4
        u_free = [None] * 4
        hbg = [None] * 4
        ps_free = [None, None]
        bc_guard_g = [None, None]
        bc_guard_v = [None, None]
        yv_free = [None, None]
        z_free = [None, None]
        G_SET = (1, 3, 5, 7, 9, 11)
        pend_cp = []
        pend_g = []
        v_pend = None
        g_pend = None
        vcnt = 0
        gcnt = 0
        for ch in range(NCH):
            sl = slice(512 * ch, 512 * (ch + 1))
            y.wait(c.v.cur())
            sd.wait(c.v.cur())
            gd.wait(c.v.cur())
            ld_tok = []
            for i in range(DT_TILES):
                y.emit(lambda e, i=i, sl=sl: e.dma_start(
                    out=dtt[i][:], in_=self.scr["s_dt"][128 * i:128 * (i + 1), sl]))
                sd.emit(lambda e, i=i, sl=sl: e.dma_start(
                    out=xbh_c[i][:], in_=self.scr["s_xbh"][128 * i:128 * (i + 1), sl]))
                gd.emit(lambda e, i=i, sl=sl: e.dma_start(
                    out=xbl_c[i][:], in_=self.scr["s_xbl"][128 * i:128 * (i + 1), sl]))
                ld_tok.append((c.y.cur(), c.sd.cur(), c.gd.cur()))
            s.wait(c.v.cur()); s.wait(c.g.cur())
            g.wait(c.v.cur()); g.wait(c.sd.cur())
            for i in range(DT_TILES):
                for tk in ld_tok[i]:
                    g.wait(tk)
                g.emit(lambda e, i=i: e.tensor_add(
                    xrb[:], xbh_c[i][:], xbl_c[i][:]))
                g.emit(lambda e, i=i: e.tensor_mul(dtxt[i][:], dtt[i][:], xrb[:]))
            tdx = c.g.cur()
            for n in range(NST):
                par = n % 2
                psB, psC = self.ps[par], self.ps[2 + par]
                t.wait(ps_free[par])
                t.emit(lambda e, n=n, sl=sl, psB=psB: e.matmul(
                    psB[:], sel[:, 128 * n:128 * (n + 1)],
                    self.bc_rows[:, sl], start=True, stop=True))
                tps = t.emit(lambda e, n=n, sl=sl, psC=psC: e.matmul(
                    psC[:], sel[:, 128 * (NST + n):128 * (NST + n + 1)],
                    self.bc_rows[:, sl], start=True, stop=True), tok=True)
                s.wait(tps)
                s.wait(bc_guard_g[par]); s.wait(bc_guard_v[par])
                s.emit(lambda e, par=par, psB=psB: e.activation(
                    out=bcb[par][:], in_=psB[:], func=AF.Copy))
                tbe = s.emit(lambda e, par=par, psC=psC: e.activation(
                    out=ccb[par][:], in_=psC[:], func=AF.Copy), tok=True)
                ps_free[par] = tbe
                g.wait(tbe)
                if n == 0:
                    g.wait(tdx)
                for i in range(DT_TILES):
                    pai = i % 4
                    col = i * NST + n
                    if n == 0:
                        s.wait(ld_tok[i][0])
                    s.wait(ab_free[pai])
                    tab = s.emit(lambda e, i=i, col=col, pai=pai: e.activation(
                        out=ab[pai][:], in_=dtt[i][:], func=AF.Exp,
                        scale=self.a_cols[:, col:col + 1]), tok=True)
                    g.wait(u_free[pai])
                    tu = g.emit(lambda e, i=i, pai=pai, par=par: e.tensor_mul(
                        ub[pai][:], dtxt[i][:], bcb[par][:]), tok=True)
                    v.wait(tab)
                    v.wait(tu)
                    v.wait(hbg[pai])
                    tscn = v.emit(lambda e, pai=pai, col=col:
                                  e.tensor_tensor_scan(
                        hb[pai][:], ab[pai][:], ub[pai][:],
                        self.state[:, col:col + 1], ALU.mult, ALU.add),
                        tok=True)
                    ab_free[pai] = tscn
                    u_free[pai] = tscn
                    pend_cp.append((tscn, pai, col))
                    if len(pend_cp) > 3:
                        tsc_, pai_, col_ = pend_cp.pop(0)
                        s.wait(tsc_)
                        s.emit(lambda e, pai=pai_, col=col_: e.activation(
                            out=self.state[:, col:col + 1],
                            in_=hb[pai][:, 511:512], func=AF.Copy))
                    if i in G_SET:
                        pend_g.append((tscn, i, pai))
                    elif n == 0:
                        v.emit(lambda e, i=i, pai=pai, par=par: e.tensor_mul(
                            ytl[i][:], hb[pai][:], ccb[par][:]))
                    else:
                        sc = wb_ if vcnt % 2 == 0 else wbv
                        vcnt += 1
                        v.emit(lambda e, pai=pai, par=par, sc=sc:
                               e.tensor_mul(sc[:], hb[pai][:], ccb[par][:]))
                        if v_pend is not None:
                            j_, sc_ = v_pend
                            v.emit(lambda e, j=j_, sc=sc_: e.tensor_add(
                                ytl[j][:], ytl[j][:], sc[:]))
                        v_pend = (i, sc)
                    while pend_g and pend_g[0][1] <= i - 2:
                        tsc_, j_, paj_ = pend_g.pop(0)
                        g.wait(tsc_)
                        if n == 0:
                            thg = g.emit(lambda e, j=j_, pai=paj_, par=par:
                                         e.tensor_mul(
                                ytl[j][:], hb[pai][:], ccb[par][:]), tok=True)
                        else:
                            scg = wb2_[gcnt % 2]
                            gcnt += 1
                            thg = g.emit(lambda e, pai=paj_, par=par, sc=scg:
                                         e.tensor_mul(
                                sc[:], hb[pai][:], ccb[par][:]), tok=True)
                            if g_pend is not None:
                                jp_, scp_ = g_pend
                                g.emit(lambda e, j=jp_, sc=scp_: e.tensor_add(
                                    ytl[j][:], ytl[j][:], sc[:]))
                            g_pend = (j_, scg)
                        hbg[paj_] = thg
                bc_guard_g[par] = c.g.cur()
                bc_guard_v[par] = c.v.cur()
            if v_pend is not None:
                j_, sc_ = v_pend
                v.emit(lambda e, j=j_, sc=sc_: e.tensor_add(
                    ytl[j][:], ytl[j][:], sc[:]))
                v_pend = None
            if g_pend is not None:
                jp_, scp_ = g_pend
                g.emit(lambda e, j=jp_, sc=scp_: e.tensor_add(
                    ytl[j][:], ytl[j][:], sc[:]))
                g_pend = None
            assert not pend_g
            while pend_cp:
                tsc_, pai_, col_ = pend_cp.pop(0)
                s.wait(tsc_)
                s.emit(lambda e, pai=pai_, col=col_: e.activation(
                    out=self.state[:, col:col + 1],
                    in_=hb[pai][:, 511:512], func=AF.Copy))
            g.wait(c.v.cur())
            for i in range(DT_TILES):
                pz = i % 2
                sd.wait(z_free[pz])
                sd.emit(lambda e, i=i, sl=sl, pz=pz: e.dma_start(
                    out=zhb[pz][:], in_=self.scr["s_zh"][128 * i:128 * (i + 1), sl]))
                tzld = sd.emit(lambda e, i=i, sl=sl, pz=pz: e.dma_start(
                    out=zlb[pz][:], in_=self.scr["s_zl"][128 * i:128 * (i + 1), sl]),
                    tok=True)
                g.wait(tzld)
                tzr = g.emit(lambda e, pz=pz: e.tensor_add(
                    ztb[:], zhb[pz][:], zlb[pz][:]), tok=True)
                z_free[pz] = tzr
                txr = g.emit(lambda e, i=i: e.tensor_add(
                    xrb[:], xbh_c[i][:], xbl_c[i][:]), tok=True)
                v.wait(tzr)
                v.wait(txr)
                v.emit(lambda e, i=i: e.scalar_tensor_tensor(
                    t2b[:], xrb[:], self.dprm[:, i:i + 1], ytl[i][:],
                    ALU.mult, ALU.add))
                tyv2 = v.emit(lambda e: e.tensor_mul(yvf[:], t2b[:], ztb[:]),
                              tok=True)
                v.wait(yv_free[pz])
                v.emit(lambda e, pz=pz: e.tensor_copy(yvh[pz][:], yvf[:]))
                g.wait(c.v.cur())
                tsp = g.emit(lambda e, pz=pz: e.tensor_sub(
                    yvl[pz][:], yvf[:], yvh[pz][:]), tok=True)
                sd.wait(tsp)
                sd.emit(lambda e, i=i, sl=sl, pz=pz: e.dma_start(
                    out=self.scr["s_yvh"][128 * i:128 * (i + 1), sl],
                    in_=yvh[pz][:]))
                tyv = sd.emit(lambda e, i=i, sl=sl, pz=pz: e.dma_start(
                    out=self.scr["s_yvl"][128 * i:128 * (i + 1), sl],
                    in_=yvl[pz][:]), tok=True)
                yv_free[pz] = tyv
        c.barrier()


    def ph_wout0(self):
        c = self.c
        t, v, s, g, y, sd, gd = c.t, c.v, c.s, c.g, c.y, c.sd, c.gd
        yvh = [c.sb(f"o0yh{i}", [128, 1024], BF16, OFF_A + i * 4096)
               for i in range(DT_TILES)]
        yvl = [c.sb(f"o0yl{i}", [128, 1024], BF16, OFF_A + i * 4096 + 2048)
               for i in range(DT_TILES)]
        self.x1f = [c.sb(f"x1f{i}", [128, T], F32, OFF_B + i * 8192)
                    for i in range(D_TILES)]
        whb = [c.sb(f"o0wh{i}", [128, 2048], BF16, OFF_W + i * 4096)
               for i in range(2)]
        wlb = [c.sb(f"o0wl{i}", [128, 2048], BF16, OFF_W + 8192 + i * 4096)
               for i in range(2)]
        wfb = [c.sb(f"o0wf{i}", [128, 2048], F32, OFF_W + 16384 + i * 8192)
               for i in range(2)]

        for i in range(D_TILES):
            y.emit(lambda e, i=i: e.dma_start(
                out=self.x1f[i][:], in_=self.scr["s_x1"][128 * i:128 * (i + 1), :]))
        c.barrier()
        ev = [None] * 8
        for half in range(2):
            th0 = half * 1024
            hsl = slice(th0, th0 + 1024)
            tldy = None
            for i in range(DT_TILES):
                y.emit(lambda e, i=i, hsl=hsl: e.dma_start(
                    out=yvh[i][:], in_=self.scr["s_yvh"][128 * i:128 * (i + 1), hsl]))
                tldy = y.emit(lambda e, i=i, hsl=hsl: e.dma_start(
                    out=yvl[i][:], in_=self.scr["s_yvl"][128 * i:128 * (i + 1), hsl]),
                    tok=True)
            t.wait(tldy)
            wslot_tok = [None, None]
            for j in range(D_TILES):
                slot = j % 2
                y.wait(wslot_tok[slot])
                gd.wait(wslot_tok[slot])
                tfw = y.emit(lambda e, sl_=slot, j=j: e.dma_start(
                    out=wfb[sl_][:].rearrange("p (k n) -> p k n", n=128),
                    in_=self.hbm["w_out0"].rearrange("(k p) n -> p k n", p=128)
                    [:, :, 128 * j:128 * (j + 1)]), tok=True)
                thw = gd.emit(lambda e, sl_=slot, j=j: e.dma_start(
                    out=whb[sl_][:].rearrange("p (k n) -> p k n", n=128),
                    in_=self.hbm["w_out0"].rearrange("(k p) n -> p k n", p=128)
                    [:, :, 128 * j:128 * (j + 1)]), tok=True)
                g.wait(tfw)
                g.wait(thw)
                tlw = g.emit(lambda e, sl_=slot: e.tensor_sub(
                    wlb[sl_][:], wfb[sl_][:], whb[sl_][:]), tok=True)
                t.wait(tlw)
                bank0 = (j % 2) * 4
                for ch in range(2):
                    lsl = slice(512 * ch, 512 * (ch + 1))
                    gsl = slice(th0 + 512 * ch, th0 + 512 * (ch + 1))
                    psb = self.ps[bank0 + ch]
                    t.wait(ev[bank0 + ch])
                    for k in range(DT_TILES):
                        for wb, mv, st_, sp_ in ((whb, yvh, k == 0, False),
                                                 (wlb, yvh, False, False),
                                                 (whb, yvl, False,
                                                  k == DT_TILES - 1)):
                            t.emit(lambda e, k=k, lsl=lsl, sl_=slot, wb=wb,
                                   mv=mv, st_=st_, sp_=sp_, psb=psb: e.matmul(
                                psb[:], wb[sl_][:, 128 * k:128 * (k + 1)],
                                mv[k][:, lsl], start=st_, stop=sp_))
                    tmm = c.t.cur()
                    if ch == 1:
                        wslot_tok[slot] = tmm
                    v.wait(tmm)
                    tva = v.emit(lambda e, j=j, gsl=gsl, psb=psb: e.tensor_add(
                        self.x1f[j][:, gsl], self.x1f[j][:, gsl], psb[:]),
                        tok=True)
                    ev[bank0 + ch] = tva
            y.wait(c.t.cur())
        c.barrier()


    def ph_ln1(self):
        c = self.c
        t, v, s, g, y, sd, gd = c.t, c.v, c.s, c.g, c.y, c.sd, c.gd
        self.xn1h = [c.sb(f"l1xh{i}", [128, T], BF16, OFF_A + i * 8192)
                     for i in range(D_TILES)]
        self.xn1l = [c.sb(f"l1xl{i}", [128, T], BF16, OFF_A + i * 8192 + 4096)
                     for i in range(D_TILES)]
        rows = c.sb("l1rows", [1, 2 * T], F32, OFF_C)
        sq_tmp = c.sb("l1sq", [128, 512], F32, OFF_C + 16384)
        rowtmp = c.sb("l1rowt", [1, 512], F32, OFF_C + 18432)
        bc0 = c.sb("l1bc0", [128, 512], F32, OFF_W)
        bc1 = c.sb("l1bc1", [128, 512], F32, OFF_W + 2048)
        xn_t = c.sb("l1xnt", [128, 512], F32, OFF_W + 4096)
        macc = c.sb("l1mac", [128, 2048], F32, OFF_W + 6144)
        parb = c.sb("l1par", [128, 512], F32, OFF_W + 14336)
        xr_t = c.sb("l1xr", [128, 512], F32, OFF_W + 16384)
        qf_t = c.sb("l1qf", [128, 512], F16, OFF_W + 18432)
        qb_t = [c.sb(f"l1qb{i}", [128, 512], BF16, OFF_W + 19456 + i * 1024)
                for i in range(2)]
        rbc = c.sb("l1rbc", [128, 512], F32, OFF_W + 21504)
        rowb = c.sb("l1rowb", [1, 512], F32, OFF_W + 23552)

        self.ln_rows(lambda i: self.x1f[i], D_TILES, D, rows, sq_tmp, rowtmp)
        c.barrier()
        for ch in range(NCH):
            sl = slice(512 * ch, 512 * (ch + 1))
            g.wait(c.v.cur())
            tb0 = g.emit(lambda e, sl=sl: e.partition_broadcast(
                bc0[:], rows[0:1, sl]), tok=True)
            tb1 = g.emit(lambda e, ch=ch: e.partition_broadcast(
                bc1[:], rows[0:1, T + 512 * ch:T + 512 * (ch + 1)]), tok=True)
            v.wait(tb0); v.wait(tb1)
            for i in range(D_TILES):
                v.emit(lambda e, i=i, sl=sl: e.tensor_sub(
                    xn_t[:], self.x1f[i][:, sl], bc0[:]))
                v.emit(lambda e: e.tensor_mul(xn_t[:], xn_t[:], bc1[:]))
                v.emit(lambda e, i=i, sl=sl: e.tensor_copy(
                    self.xn1h[i][:, sl], xn_t[:]))
                tlo = v.emit(lambda e, i=i, sl=sl: e.scalar_tensor_tensor(
                    self.xn1l[i][:, sl], self.xn1h[i][:, sl], -1.0, xn_t[:],
                    ALU.mult, ALU.add), tok=True)
                v.emit(lambda e: e.scalar_tensor_tensor(
                    xr_t[:], xn_t[:], -1.0, xn_t[:], ALU.mult, ALU.max))
                tva = c.v.cur()
                g.wait(tva)
                tpr = g.emit(lambda e: e.partition_all_reduce(
                    parb[:], xr_t[:], channels=128,
                    reduce_op=ReduceOp.max), tok=True)
                v.wait(tpr)
                if i == 0:
                    tmx = v.emit(lambda e, sl=sl: e.tensor_copy(
                        macc[:, sl], parb[:]), tok=True)
                else:
                    tmx = v.emit(lambda e, sl=sl: e.tensor_max(
                        macc[:, sl], macc[:, sl], parb[:]), tok=True)
                g.wait(tmx)
        c.barrier()
        for ch in range(NCH):
            sl = slice(512 * ch, 512 * (ch + 1))
            v.emit(lambda e, sl=sl: e.tensor_scalar_max(
                rbc[:], macc[:, sl], 1e-5))
            tsx = v.emit(lambda e, sl=sl: e.tensor_scalar_mul(
                rowb[0:1, :], rbc[0:1, :], 1.0 / 127.0), tok=True)
            sd.wait(tsx)
            sd.emit(lambda e, sl=sl: e.dma_start(
                out=self.scr["s_sxi1"][0:1, sl], in_=rowb[0:1, :]), tok=True)
            v.emit(lambda e: e.reciprocal(rbc[:], rbc[:]))
            tsr = v.emit(lambda e: e.tensor_scalar_mul(rbc[:], rbc[:], 127.0),
                         tok=True)
            for i in range(D_TILES):
                pq = i % 2
                v.emit(lambda e, i=i, sl=sl: e.tensor_add(
                    xr_t[:], self.xn1h[i][:, sl], self.xn1l[i][:, sl]))
                v.emit(lambda e: e.tensor_mul(xr_t[:], xr_t[:], rbc[:]))
                v.emit(lambda e: e.tensor_scalar(
                    qf_t[:], xr_t[:], 1536.0, 1408.0, ALU.add, ALU.max))
                tqq = v.emit(lambda e, pq=pq: e.tensor_scalar(
                    qb_t[pq][:], qf_t[:], 1663.0, 1536.0, ALU.min,
                    ALU.subtract), tok=True)
                sd.wait(tqq)
                tqs = sd.emit(lambda e, i=i, sl=sl, pq=pq: e.dma_start(
                    out=self.scr["s_xq1"][128 * i:128 * (i + 1), sl],
                    in_=qb_t[pq][:]), tok=True)
                v.wait((tqs[0], max(0, tqs[1] - 16)))
        c.barrier()

    def ph_proj1(self):
        c = self.c
        t, v, s, g, y, sd, gd = c.t, c.v, c.s, c.g, c.y, c.sd, c.gd
        for i in range(D_TILES):
            sd.emit(lambda e, i=i: e.dma_start(
                out=self.scr["s_x1"][128 * i:128 * (i + 1), :],
                in_=self.x1f[i][:]), tok=True)
        self.kt = [c.sb(f"a1k{i}", [128, T], F32, OFF_B + i * 8192)
                   for i in range(D_TILES)]
        lath = c.sb("a1lath", [128, T], BF16, OFF_C)
        latl = c.sb("a1latl", [128, T], BF16, OFF_C + 4096)
        whb = [c.sb(f"a1wh{i}", [128, 1024], BF16, OFF_W + i * 2048)
               for i in range(2)]
        wlb = [c.sb(f"a1wl{i}", [128, 1024], BF16, OFF_W + 4096 + i * 2048)
               for i in range(2)]
        wfb = [c.sb(f"a1wf{i}", [128, 1024], F32, OFF_W + 8192 + i * 4096)
               for i in range(2)]
        kvh = c.sb("a1kvh", [128, 2048], BF16, OFF_W + 16384)
        kvl = c.sb("a1kvl", [128, 2048], BF16, OFF_W + 16384 + 4096)
        kvf = c.sb("a1kvf", [128, 2048], F32, OFF_W + 24576)
        etmp = [c.sb(f"a1et{i}", [128, 512], F32, OFF_W + 32768 + i * 2048)
                for i in range(2)]
        ttmp = [c.sb(f"a1tt{i}", [128, 128], F32, OFF_W + 36864 + i * 512)
                for i in range(2)]

        def stream_pair_mm(w_name, KT_, out_handler, ev, mtiles,
                           moving_h, moving_l):
            wslot_tok = [None, None]
            for j in range(mtiles):
                slot = j % 2
                y.wait(wslot_tok[slot])
                gd.wait(wslot_tok[slot])
                tfw = y.emit(lambda e, sl_=slot, j=j, w_name=w_name, KT_=KT_:
                             e.dma_start(
                    out=wfb[sl_][:, 0:KT_ * 128].rearrange(
                        "p (k n) -> p k n", n=128),
                    in_=self.hbm[w_name].rearrange("(k p) n -> p k n", p=128)
                    [:, :, 128 * j:128 * (j + 1)]), tok=True)
                thw = gd.emit(lambda e, sl_=slot, j=j, w_name=w_name, KT_=KT_:
                              e.dma_start(
                    out=whb[sl_][:, 0:KT_ * 128].rearrange(
                        "p (k n) -> p k n", n=128),
                    in_=self.hbm[w_name].rearrange("(k p) n -> p k n", p=128)
                    [:, :, 128 * j:128 * (j + 1)]), tok=True)
                g.wait(tfw)
                g.wait(thw)
                tlw = g.emit(lambda e, sl_=slot, KT_=KT_: e.tensor_sub(
                    wlb[sl_][:, 0:KT_ * 128], wfb[sl_][:, 0:KT_ * 128],
                    whb[sl_][:, 0:KT_ * 128]), tok=True)
                t.wait(tlw)
                bank0 = (j % 2) * 2
                for ch in range(NCH):
                    sl = slice(512 * ch, 512 * (ch + 1))
                    psb = self.ps[bank0 + ch % 2]
                    t.wait(ev[bank0 + ch % 2])
                    for k in range(KT_):
                        for wb, mv, st_, sp_ in (
                                (whb, moving_h, k == 0, False),
                                (wlb, moving_h, False, False),
                                (whb, moving_l, False, k == KT_ - 1)):
                            t.emit(lambda e, k=k, sl=sl, sl_=slot, wb=wb,
                                   mv=mv, st_=st_, sp_=sp_, psb=psb: e.matmul(
                                psb[:], wb[sl_][:, 128 * k:128 * (k + 1)],
                                mv[k][:, sl], start=st_, stop=sp_))
                    tmm = c.t.cur()
                    if ch == NCH - 1:
                        wslot_tok[slot] = tmm
                    ev[bank0 + ch % 2] = out_handler(j, ch, psb, tmm)

        ev = [None] * 4

        def lat_out(j, ch, psb, tmm):
            sl = slice(512 * ch, 512 * (ch + 1))
            v.wait(tmm)
            v.emit(lambda e, sl=sl, psb=psb: e.tensor_copy(lath[:, sl], psb[:]))
            return v.emit(lambda e, sl=sl, psb=psb: e.scalar_tensor_tensor(
                latl[:, sl], lath[:, sl], -1.0, psb[:], ALU.mult, ALU.add),
                tok=True)

        stream_pair_mm("w_kvd", D_TILES, lat_out, ev, 1,
                       self.xn1h, self.xn1l)
        c.barrier()
        lat_h = [lath.ap().rearrange("p n -> p n") if False else lath]
        ev = [None] * 4

        def kv_out(j, ch, psb, tmm):
            sl = slice(512 * ch, 512 * (ch + 1))
            if j < 8:
                v.wait(tmm)
                return v.emit(lambda e, j=j, sl=sl, psb=psb: e.tensor_copy(
                    self.kt[j][:, sl], psb[:]), tok=True)
            pe_ = ch % 2
            s.wait(tmm)
            s.wait(self.vt_free[pe_])
            tcp = s.emit(lambda e, pe_=pe_, psb=psb: e.activation(
                out=etmp[pe_][:], in_=psb[:], func=AF.Copy), tok=True)
            t.wait(tcp)
            last = None
            for b in range(4):
                pst = self.ps[4 + (b % 2)]
                t.wait(self.tt_free[b % 2])
                ttr = t.emit(lambda e, pe_=pe_, b=b, pst=pst: e.transpose(
                    pst[:, 0:128], etmp[pe_][:, 128 * b:128 * (b + 1)],
                    self.ident[:]), tok=True)
                v.wait(ttr)
                tvc = v.emit(lambda e, b=b, pst=pst: e.tensor_copy(
                    ttmp[b % 2][:], pst[:, 0:128]), tok=True)
                t.wait(tvc)
                sd.wait(tvc)
                tsd = sd.emit(lambda e, j=j, ch=ch, b=b: e.dma_start(
                    out=self.scr["s_vt"][512 * ch + 128 * b:
                                         512 * ch + 128 * (b + 1),
                                         128 * (j - 8):128 * (j - 7)],
                    in_=ttmp[b % 2][:]), tok=True)
                self.tt_free[b % 2] = tsd
                last = tsd
            self.vt_free[pe_] = last
            return tcp

        self.vt_free = [None, None]
        self.tt_free = [None, None]
        latp_h = [lath]
        latp_l = [latl]
        stream_pair_mm("w_kvu", 1, kv_out, ev, 16, latp_h, latp_l)
        c.barrier()
        ev = [None] * 4
        qb = [c.sb(f"a1qb{i}", [128, 512], F32, OFF_W + 37888 + i * 2048)
              for i in range(2)]
        self.q_free = [None, None]

        def q_out(j, ch, psb, tmm):
            pq = ch % 2
            v.wait(tmm)
            v.wait(self.q_free[pq])
            tqc = v.emit(lambda e, pq=pq, psb=psb: e.tensor_copy(
                qb[pq][:], psb[:]), tok=True)
            sd.wait(tqc)
            tqs = sd.emit(lambda e, j=j, ch=ch, pq=pq: e.dma_start(
                out=self.scr["s_q"][128 * j:128 * (j + 1),
                                    512 * ch:512 * (ch + 1)],
                in_=qb[pq][:]), tok=True)
            self.q_free[pq] = tqs
            return tqc

        stream_pair_mm("w_q", D_TILES, q_out, ev, D_TILES,
                       self.xn1h, self.xn1l)
        c.barrier()


    def ph_attn(self):
        c = self.c
        t, v, s, g, y, sd, gd = c.t, c.v, c.s, c.g, c.y, c.sd, c.gd
        scA = [c.sb(f"atA{i}", [128, 512], F32, OFF_A + i * 2048)
               for i in range(16)]
        scB = [c.sb(f"atB{i}", [128, 512], F32, OFF_A + 32768 + i * 2048)
               for i in range(16)]
        masks = c.sb("atmask", [128, 4 * 512], F32, OFF_C)
        vt_sl = [c.sb(f"atv{i}", [128, 130], F32, OFF_C + 8192 + i * 544)
                 for i in range(16)]
        qblk = [c.sb(f"atq{i}", [128, 512], F32, OFF_C + 17024 + i * 2048)
                for i in range(2)]
        y1b = [c.sb(f"aty{i}", [64, 512], F32, OFF_W + i * 2048)
               for i in range(2)]
        dnb = [c.sb(f"atd{i}", [1, 512], F32, OFF_W + 4096 + i * 2048)
               for i in range(2)]

        self.build_masks(masks)
        c.barrier()
        q_free = [None, None]
        vt_free = None
        y1_free = [None, None]
        av_done = None
        for hp in range(D_TILES):
            y.wait(vt_free)
            tvl = None
            for ti in range(16):
                tvl = y.emit(lambda e, ti=ti, hp=hp: e.dma_start(
                    out=vt_sl[ti][:, 0:130].rearrange(
                        "p (g n) -> p g n", n=65)[:, :, 0:64],
                    in_=self.scr["s_vt"][128 * ti:128 * (ti + 1),
                                         128 * hp:128 * (hp + 1)].rearrange(
                        "p (g n) -> p g n", n=64)), tok=True)
            v.wait(tvl)
            for ti in range(16):
                v.emit(lambda e, ti=ti: e.memset(vt_sl[ti][:, 64:65], 1.0))
                v.emit(lambda e, ti=ti: e.memset(vt_sl[ti][:, 129:130], 1.0))
            tones = c.v.cur()
            t.wait(tones)
            for tqb in range(NCH):
                tq0 = 512 * tqb
                ntk = 4 * (tqb + 1)
                pq = tqb % 2
                y.wait(q_free[pq])
                tql = y.emit(lambda e, hp=hp, tq0=tq0, pq=pq: e.dma_start(
                    out=qblk[pq][:],
                    in_=self.scr["s_q"][128 * hp:128 * (hp + 1),
                                        tq0:tq0 + 512]), tok=True)
                t.wait(tql)
                v.wait(av_done)
                exp_tok = [None] * 16
                evac_tok = [None, None]
                for ti in range(ntk):
                    par = ti % 2
                    psA, psB = self.ps[par], self.ps[2 + par]
                    t.wait(evac_tok[par])
                    kt_ = self.kt[hp]
                    for (tp, lsl, qsl, ps_, osl) in (
                            ((0, 0), slice(128 * ti, 128 * ti + 64),
                             slice(0, 64), psA, slice(0, 64)),
                            ((0, 64), slice(128 * ti + 64, 128 * ti + 128),
                             slice(0, 64), psA, slice(64, 128)),
                            ((64, 0), slice(128 * ti, 128 * ti + 64),
                             slice(64, 128), psB, slice(0, 64)),
                            ((64, 64), slice(128 * ti + 64, 128 * ti + 128),
                             slice(64, 128), psB, slice(64, 128))):
                        t.emit(lambda e, tp=tp, lsl=lsl, qsl=qsl, ps_=ps_,
                               osl=osl, kt_=kt_, pq=pq: e.matmul(
                            ps_[osl, :], kt_[qsl, lsl], qblk[pq][qsl, :],
                            start=True, stop=True, tile_position=tp))
                    tmm = c.t.cur()
                    v.wait(tmm)
                    diag = ti - 4 * tqb
                    for sc_, ps_ in ((scA, psA), (scB, psB)):
                        if 0 <= diag:
                            v.emit(lambda e, ti=ti, diag=diag, sc_=sc_,
                                   ps_=ps_: e.scalar_tensor_tensor(
                                sc_[ti][:], ps_[:], 0.125,
                                masks[:, 512 * diag:512 * (diag + 1)],
                                ALU.mult, ALU.add))
                        else:
                            v.emit(lambda e, ti=ti, sc_=sc_, ps_=ps_:
                                   e.tensor_scalar(
                                sc_[ti][:], ps_[:], 0.125, -EXP_SHIFT,
                                ALU.mult, ALU.add))
                    tev = c.v.cur()
                    evac_tok[par] = tev
                    s.wait(tev)
                    s.emit(lambda e, ti=ti: e.activation(
                        out=scA[ti][:], in_=scA[ti][:], func=AF.Exp))
                    exp_tok[ti] = s.emit(lambda e, ti=ti: e.activation(
                        out=scB[ti][:], in_=scB[ti][:], func=AF.Exp), tok=True)
                for hd, sc_ in ((0, scA), (1, scB)):
                    py = self.ps[4 + hd]
                    off = 65 * hd
                    for ti in range(ntk):
                        t.wait(exp_tok[ti])
                        t.emit(lambda e, ti=ti, py=py, off=off, sc_=sc_,
                               ntk=ntk: e.matmul(
                            py[0:65, :], vt_sl[ti][:, off:off + 65],
                            sc_[ti][:], start=(ti == 0),
                            stop=(ti == ntk - 1)))
                    tav = c.t.cur()
                    v.wait(tav)
                    v.wait(y1_free[hd])
                    v.emit(lambda e, hd=hd, py=py: e.tensor_copy(
                        y1b[hd][:], py[0:64, :]))
                    tcy = v.emit(lambda e, hd=hd, py=py: e.tensor_copy(
                        dnb[hd][0:1, :], py[64:65, :]), tok=True)
                    sd.wait(tcy)
                    sd.emit(lambda e, hp=hp, hd=hd, tq0=tq0: e.dma_start(
                        out=self.scr["s_y1"][128 * hp + 64 * hd:
                                             128 * hp + 64 * (hd + 1),
                                             tq0:tq0 + 512],
                        in_=y1b[hd][:]), tok=True)
                    tds = sd.emit(lambda e, hp=hp, hd=hd, tq0=tq0: e.dma_start(
                        out=self.scr["s_den"][2 * hp + hd:2 * hp + hd + 1,
                                              tq0:tq0 + 512],
                        in_=dnb[hd][0:1, :]), tok=True)
                    y1_free[hd] = tds
                av_done = c.t.cur()
                q_free[pq] = av_done
            vt_free = av_done
        c.barrier()


    def ph_wout1(self):
        c = self.c
        t, v, s, g, y, sd, gd = c.t, c.v, c.s, c.g, c.y, c.sd, c.gd
        denb = c.sb("o1den", [16, T], F32, OFF_C)
        sel2 = c.sb("o1sel", [16, 8 * 128], F32, OFF_C + 8192)
        y1t = [c.sb(f"o1yt{i}", [128, 512], F32, OFF_C + 12288 + i * 2048)
               for i in range(2)]
        y1h = [c.sb(f"o1yh{i}", [128, T], BF16, OFF_A + i * 8192)
               for i in range(D_TILES)]
        y1l = [c.sb(f"o1yl{i}", [128, T], BF16, OFF_A + i * 8192 + 4096)
               for i in range(D_TILES)]
        self.x2 = [c.sb(f"x2t{i}", [128, T], F32, OFF_B + i * 8192)
                   for i in range(D_TILES)]
        whb = [c.sb(f"o1wh{i}", [128, 1024], BF16, OFF_W + i * 2048)
               for i in range(2)]
        wlb = [c.sb(f"o1wl{i}", [128, 1024], BF16, OFF_W + 4096 + i * 2048)
               for i in range(2)]
        wfb = [c.sb(f"o1wf{i}", [128, 1024], F32, OFF_W + 8192 + i * 4096)
               for i in range(2)]

        tdn = y.emit(lambda e: e.dma_start(
            out=denb[:], in_=self.scr["s_den"][:, :]), tok=True)
        for i in range(D_TILES):
            y.emit(lambda e, i=i: e.dma_start(
                out=self.x2[i][:], in_=self.scr["s_x1"][128 * i:128 * (i + 1), :]))
        v.wait(tdn)
        v.emit(lambda e: e.reciprocal(denb[:], denb[:]))
        tio = g.emit(lambda e: e.iota(
            sel2[:], [[2, 8], [1, 2], [0, 64]], channel_multiplier=0,
            allow_small_or_imprecise_dtypes=True), tok=True)
        v.wait(tio)
        v.emit(lambda e: e.tensor_scalar(
            sel2[:], sel2[:], self.iota_p[0:16, 0:1], None, ALU.is_equal))
        c.barrier()
        rb_free = [None, None]
        for j in range(D_TILES):
            for ch in range(NCH):
                sl = slice(512 * ch, 512 * (ch + 1))
                pq = ch % 2
                y.wait(rb_free[pq])
                tyl = y.emit(lambda e, j=j, sl=sl, pq=pq: e.dma_start(
                    out=y1t[pq][:],
                    in_=self.scr["s_y1"][128 * j:128 * (j + 1), sl]), tok=True)
                psb = self.ps[pq]
                t.wait(rb_free[pq])
                trb = t.emit(lambda e, j=j, sl=sl, psb=psb: e.matmul(
                    psb[:], sel2[:, 128 * j:128 * (j + 1)], denb[:, sl],
                    start=True, stop=True), tok=True)
                v.wait(tyl)
                v.wait(trb)
                v.emit(lambda e, pq=pq, psb=psb: e.tensor_mul(
                    y1t[pq][:], y1t[pq][:], psb[:]))
                v.emit(lambda e, j=j, sl=sl, pq=pq: e.tensor_copy(
                    y1h[j][:, sl], y1t[pq][:]))
                trl = v.emit(lambda e, j=j, sl=sl, pq=pq: e.scalar_tensor_tensor(
                    y1l[j][:, sl], y1h[j][:, sl], -1.0, y1t[pq][:],
                    ALU.mult, ALU.add), tok=True)
                rb_free[pq] = trl
        c.barrier()
        wslot_tok = [None, None]
        ev = [None] * 8
        for j in range(D_TILES):
            slot = j % 2
            y.wait(wslot_tok[slot])
            gd.wait(wslot_tok[slot])
            tfw = y.emit(lambda e, sl_=slot, j=j: e.dma_start(
                out=wfb[sl_][:].rearrange("p (k n) -> p k n", n=128),
                in_=self.hbm["w_out1"].rearrange("(k p) n -> p k n", p=128)
                [:, :, 128 * j:128 * (j + 1)]), tok=True)
            thw = gd.emit(lambda e, sl_=slot, j=j: e.dma_start(
                out=whb[sl_][:].rearrange("p (k n) -> p k n", n=128),
                in_=self.hbm["w_out1"].rearrange("(k p) n -> p k n", p=128)
                [:, :, 128 * j:128 * (j + 1)]), tok=True)
            g.wait(tfw)
            g.wait(thw)
            tlw = g.emit(lambda e, sl_=slot: e.tensor_sub(
                wlb[sl_][:], wfb[sl_][:], whb[sl_][:]), tok=True)
            t.wait(tlw)
            bank0 = (j % 2) * 4
            for ch in range(NCH):
                sl = slice(512 * ch, 512 * (ch + 1))
                psb = self.ps[bank0 + ch]
                t.wait(ev[bank0 + ch])
                for k in range(D_TILES):
                    for wb, mv, st_, sp_ in ((whb, y1h, k == 0, False),
                                             (wlb, y1h, False, False),
                                             (whb, y1l, False,
                                              k == D_TILES - 1)):
                        t.emit(lambda e, k=k, sl=sl, sl_=slot, wb=wb, mv=mv,
                               st_=st_, sp_=sp_, psb=psb: e.matmul(
                            psb[:], wb[sl_][:, 128 * k:128 * (k + 1)],
                            mv[k][:, sl], start=st_, stop=sp_))
                tmm = c.t.cur()
                if ch == NCH - 1:
                    wslot_tok[slot] = tmm
                v.wait(tmm)
                tva = v.emit(lambda e, j=j, sl=sl, psb=psb: e.tensor_add(
                    self.x2[j][:, sl], self.x2[j][:, sl], psb[:]), tok=True)
                ev[bank0 + ch] = tva
        c.barrier()

    def ph_ffn1(self):
        c = self.c
        t, v, s, g, y, sd, gd = c.t, c.v, c.s, c.g, c.y, c.sd, c.gd
        wband = [c.sb(f"f1wb{i}", [128, 1024], F32, OFF_W + i * 4096)
                 for i in range(2)]
        tmpb = [c.sb(f"f1tmp{i}", [128, 1024], F32, OFF_W + 8192 + i * 4096)
                for i in range(2)]
        t1b = [c.sb(f"f1t1{i}", [128, 1024], F32, OFF_W + 16384 + i * 4096)
               for i in range(2)]
        wqb = [c.sb(f"f1wq{i}", [128, 1024], BF16, OFF_W + 24576 + i * 2048)
               for i in range(2)]
        accb = c.sb("f1acc", [128, 1], F32, OFF_W + 30784)
        redb = c.sb("f1red", [128, 32], F32, OFF_W + 30816)
        redo = c.sb("f1redo", [128, 32], F32, OFF_W + 30944)
        s1col = self.s1col4
        cells = self.cellA

        def wquant(w_name, KT_, N_, scr_name, cell_i):
            bw = min(N_, 1024)
            nh = N_ // bw
            tacc = None
            ld_tok = {}
            nb = 0
            for k in range(KT_):
                for h2 in range(nh):
                    sl_ = nb % 2
                    y.wait(ld_tok.get(sl_))
                    tw = y.emit(lambda e, k=k, h2=h2, w_name=w_name, bw=bw,
                                sl_=sl_: e.dma_start(
                        out=wband[sl_][:, 0:bw],
                        in_=self.hbm[w_name][128 * k:128 * (k + 1),
                                             bw * h2:bw * (h2 + 1)]), tok=True)
                    v.wait(tw)
                    tacc = v.emit(lambda e, bw=bw, sl_=sl_, nb=nb:
                                  e.reduce_sum(
                        redb[:, nb:nb + 1], wband[sl_][:, 0:bw],
                        mybir.AxisListType.X, apply_absolute_value=True),
                        tok=True)
                    ld_tok[sl_] = tacc
                    nb += 1
            s.wait(tacc)
            tsum = s.emit(lambda e, nb=nb: e.activation(
                out=redo[:, 0:nb], in_=redb[:, 0:nb], func=AF.Copy,
                accum_out=accb[:, 0:1]), tok=True)
            t.wait(tsum)
            tps = t.emit(lambda e: e.matmul(
                self.ps[0][0:1, 0:1], self.ones_col[:], accb[:],
                start=True, stop=True), tok=True)
            v.wait(tps)
            tms = v.emit(lambda e, KT_=KT_, N_=N_, cell_i=cell_i:
                         e.tensor_scalar(
                cells[0:1, cell_i:cell_i + 1], self.ps[0][0:1, 0:1],
                1.0 / (KT_ * 128 * N_), 1e-5, ALU.mult, ALU.max), tok=True)
            g.wait(tms)
            tbm = g.emit(lambda e, cell_i=cell_i: e.partition_broadcast(
                s1col[:, cell_i:cell_i + 1],
                cells[0:1, cell_i:cell_i + 1]), tok=True)
            v.wait(tbm)
            scrj = 4 + cell_i // 2
            tcl = v.emit(lambda e, cell_i=cell_i, scrj=scrj: e.reciprocal(
                s1col[:, scrj:scrj + 1],
                s1col[:, cell_i:cell_i + 1]), tok=True)
            g.wait(tcl)
            tbc = g.emit(lambda e, cell_i=cell_i, scrj=scrj: e.tensor_copy(
                s1col[:, cell_i + 1:cell_i + 2],
                s1col[:, scrj:scrj + 1]), tok=True)
            v.wait(tbc)
            t.wait(tbc)
            ld_tok = {}
            nb = 0
            for k in range(KT_):
                for h2 in range(nh):
                    sl_ = nb % 2
                    y.wait(ld_tok.get(sl_))
                    y.wait(ld_tok.get((sl_, "g")))
                    tw = y.emit(lambda e, k=k, h2=h2, w_name=w_name, bw=bw,
                                sl_=sl_: e.dma_start(
                        out=wband[sl_][:, 0:bw],
                        in_=self.hbm[w_name][128 * k:128 * (k + 1),
                                             bw * h2:bw * (h2 + 1)]), tok=True)
                    v.wait(tw)
                    v.wait(ld_tok.get((sl_, "g")))
                    v.emit(lambda e, bw=bw, cell_i=cell_i, sl_=sl_:
                           e.tensor_scalar(
                        tmpb[sl_][:, 0:bw], wband[sl_][:, 0:bw],
                        s1col[:, cell_i + 1:cell_i + 2], None, ALU.mult))
                    v.emit(lambda e, bw=bw, sl_=sl_: e.tensor_single_scalar(
                        t1b[sl_][:, 0:bw], tmpb[sl_][:, 0:bw], 0.5, ALU.is_ge))
                    tle = v.emit(lambda e, bw=bw, sl_=sl_: e.tensor_single_scalar(
                        wband[sl_][:, 0:bw], tmpb[sl_][:, 0:bw], -0.5,
                        ALU.is_le), tok=True)
                    ld_tok[sl_] = tle
                    g.wait(tle)
                    g.wait(ld_tok.get((sl_, "sd")))
                    tqw = g.emit(lambda e, bw=bw, sl_=sl_: e.tensor_sub(
                        wqb[sl_][:, 0:bw], t1b[sl_][:, 0:bw],
                        wband[sl_][:, 0:bw]), tok=True)
                    ld_tok[(sl_, "g")] = tqw
                    sd.wait(tqw)
                    tsp = sd.emit(lambda e, k=k, h2=h2, bw=bw, sl_=sl_,
                                  scr_name=scr_name: e.dma_start(
                        out=self.scr[scr_name][128 * k:128 * (k + 1),
                                               bw * h2:bw * (h2 + 1)],
                        in_=wqb[sl_][:, 0:bw]), tok=True)
                    ld_tok[(sl_, "sd")] = tsp
                    nb += 1
            c.barrier()

        wquant("w1", D_TILES, HF1, "s_wq1", 0)
        if getattr(self, "hw_probe", None) == "cellhw":
            tk1 = gd.emit(lambda e: e.dma_start(
                out=self.x_out[0:1, 0:8], in_=cells[0:1, 0:8]), tok=True)
            tk3 = gd.emit(lambda e: e.dma_start(
                out=self.x_out[128:256, 0:4], in_=s1col[:, 0:4]), tok=True)
            g.stream.append(["wait", (gd, tk3[1]), None])
            return
        wquant("w2", H1_TILES, D, "s_wq2", 2)
        if getattr(self, "hw_probe", None) == "wq1hw":
            tks = None
            for i in range(D_TILES):
                tks = gd.emit(lambda e, i=i: e.dma_start(
                    out=self.x_out[128 * i:128 * (i + 1), :],
                    in_=self.scr["s_wq1"][128 * i:128 * (i + 1), 0:T]), tok=True)
            g.stream.append(["wait", (gd, tks[1]), None])
            return

        xq1 = [c.sb(f"f1xq{i}", [128, T], BF16, OFF_A + i * 4096)
               for i in range(D_TILES)]
        macc2 = c.sb("f1mac", [128, T], F32, OFF_C)
        sxwb = [c.sb(f"f1sxw{i}", [128, 512], F32, OFF_C + 8192 + i * 2048)
                for i in range(NCH)]
        rowb = c.sb("f1row", [1, T], F32, OFF_C + 16384)
        h1row = [c.sb(f"f1h1r{i}", [128, 2050], F32, OFF_A + 32768 + i * 8224)
                 for i in range(2)]
        cvb = [c.sb(f"f1cv{i}", [128, T], F32, OFF_W + i * 8192)
               for i in range(2)]
        hh_r = [c.sb(f"f1hh{i}", [128, T], BF16, OFF_W + 16384 + i * 4096)
                for i in range(2)]
        hl_r = [c.sb(f"f1hl{i}", [128, T], BF16, OFF_W + 24576 + i * 4096)
                for i in range(2)]
        parb = c.sb("f1par", [128, 512], F32, OFF_W + 32768)
        negb = c.sb("f1neg", [128, T], F32, OFF_C + 16384)
        qwh = [c.sb(f"f1qw{i}", [128, 1024], BF16, OFF_W + 34816 + i * 2048)
               for i in range(2)]

        for i in range(D_TILES):
            y.emit(lambda e, i=i: e.dma_start(
                out=xq1[i][:], in_=self.scr["s_xq1"][128 * i:128 * (i + 1), :]))
        tsx = y.emit(lambda e: e.dma_start(
            out=rowb[:], in_=self.scr["s_sxi1"][:, :]), tok=True)
        v.wait(tsx)
        tsw = v.emit(lambda e: e.tensor_scalar(
            rowb[:], rowb[:], cells[0:1, 0:1], None, ALU.mult), tok=True)
        g.wait(tsw)
        for ch in range(NCH):
            g.emit(lambda e, ch=ch: e.partition_broadcast(
                sxwb[ch][:], rowb[0:1, 512 * ch:512 * (ch + 1)]), tok=True)
        for sl_ in range(2):
            v.emit(lambda e, sl_=sl_: e.memset(h1row[sl_][:, 0:1], 0.0))
            v.emit(lambda e, sl_=sl_: e.memset(h1row[sl_][:, 2049:2050], 0.0))
        c.barrier()
        wslot_tok = [None, None]
        ev = [None] * 8
        cv_guard = [None, None]
        th1_free = [None, None]
        pend_post = []

        def h1_post(j, sl_, tsil):
            v.wait(tsil)
            v.wait(th1_free[sl_])
            v.emit(lambda e, sl_=sl_: e.tensor_copy(hh_r[sl_][:], cvb[sl_][:]))
            tcpy = c.v.cur()
            g.wait(tcpy)
            tsp = g.emit(lambda e, sl_=sl_: e.tensor_sub(
                hl_r[sl_][:], cvb[sl_][:], hh_r[sl_][:]), tok=True)
            cv_guard[sl_] = tsp
            sd.wait(tsp)
            sd.emit(lambda e, j=j, sl_=sl_: e.dma_start(
                out=self.scr["s_h1h"][128 * j:128 * (j + 1), :],
                in_=hh_r[sl_][:]))
            th1 = sd.emit(lambda e, j=j, sl_=sl_: e.dma_start(
                out=self.scr["s_h1l"][128 * j:128 * (j + 1), :],
                in_=hl_r[sl_][:]), tok=True)
            th1_free[sl_] = th1
            v.emit(lambda e, sl_=sl_: e.scalar_tensor_tensor(
                negb[:], cvb[sl_][:], -1.0, cvb[sl_][:], ALU.mult, ALU.max))
            if j == 0:
                v.emit(lambda e: e.tensor_copy(macc2[:], negb[:]))
            else:
                v.emit(lambda e: e.tensor_max(macc2[:], macc2[:], negb[:]))

        for j in range(H1_TILES):
            slot = j % 2
            gd.wait(wslot_tok[slot])
            tqw = gd.emit(lambda e, sl_=slot, j=j: e.dma_start(
                out=qwh[sl_][:].rearrange("p (k n) -> p k n", n=128),
                in_=self.scr["s_wq1"].rearrange("(k p) n -> p k n", p=128)
                [:, :, 128 * j:128 * (j + 1)]), tok=True)
            t.wait(tqw)
            bank0 = (j % 2) * 4
            mm_tok = [None] * NCH
            for ch in range(NCH):
                sl = slice(512 * ch, 512 * (ch + 1))
                psb = self.ps[bank0 + ch]
                t.wait(ev[bank0 + ch])
                for k in range(D_TILES):
                    t.emit(lambda e, k=k, sl=sl, sl_=slot, psb=psb: e.matmul(
                        psb[:], qwh[sl_][:, 128 * k:128 * (k + 1)],
                        xq1[k][:, sl], start=(k == 0),
                        stop=(k == D_TILES - 1)))
                mm_tok[ch] = c.t.cur()
            wslot_tok[slot] = mm_tok[NCH - 1]
            for ch in range(NCH):
                v.wait(mm_tok[ch])
                v.emit(lambda e, ch=ch, bank0=bank0, sl_=slot: e.tensor_mul(
                    h1row[sl_][:, 1 + 512 * ch:1 + 512 * (ch + 1)],
                    self.ps[bank0 + ch][:], sxwb[ch][:]))
            tevc = c.v.cur()
            for ch in range(NCH):
                ev[bank0 + ch] = tevc
            v.wait(cv_guard[slot])
            v.emit(lambda e, j=j, sl_=slot: e.tensor_scalar(
                cvb[sl_][:], h1row[sl_][:, 1:2049], self.c1w[1][:, j:j + 1],
                self.c1b[:, j:j + 1], ALU.mult, ALU.add))
            v.emit(lambda e, j=j, sl_=slot: e.scalar_tensor_tensor(
                cvb[sl_][:], h1row[sl_][:, 0:2048], self.c1w[0][:, j:j + 1],
                cvb[sl_][:], ALU.mult, ALU.add))
            trow = v.emit(lambda e, j=j, sl_=slot: e.scalar_tensor_tensor(
                cvb[sl_][:], h1row[sl_][:, 2:2050], self.c1w[2][:, j:j + 1],
                cvb[sl_][:], ALU.mult, ALU.add), tok=True)
            s.wait(trow)
            tsil = s.emit(lambda e, sl_=slot: e.activation(
                out=cvb[sl_][:], in_=cvb[sl_][:], func=AF.Silu), tok=True)
            pend_post.append((j, slot, tsil))
            if len(pend_post) > 1:
                h1_post(*pend_post.pop(0))
        while pend_post:
            h1_post(*pend_post.pop(0))
        tvm = c.v.cur()
        g.wait(tvm)
        for ch in range(NCH):
            tpr = g.emit(lambda e, ch=ch: e.partition_all_reduce(
                parb[:], macc2[:, 512 * ch:512 * (ch + 1)], channels=128,
                reduce_op=ReduceOp.max), tok=True)
            v.wait(tpr)
            tmx = v.emit(lambda e, ch=ch: e.tensor_copy(
                macc2[:, 512 * ch:512 * (ch + 1)], parb[:]), tok=True)
            g.wait(tmx)
        c.barrier()
        if getattr(self, "hw_probe", None) == "h1hw":
            tks = None
            for i in range(D_TILES):
                tks = gd.emit(lambda e, i=i: e.dma_start(
                    out=self.x_out[128 * i:128 * (i + 1), :],
                    in_=self.scr["s_h1h"][128 * i:128 * (i + 1), :]), tok=True)
            g.stream.append(["wait", (gd, tks[1]), None])
            return
        xq2 = [c.sb(f"f1x2q{i}", [128, 1024], BF16, OFF_A + i * 2048)
               for i in range(H1_TILES)]
        rbc2 = c.sb("f1rbc2", [128, T], F32, OFF_W)
        xrb2 = c.sb("f1xr2", [128, 1024], F32, OFF_W + 8192)
        w2b = [c.sb(f"f1w2b{i}", [128, H1_TILES * 128], BF16,
                    OFF_W + 16384 + i * 8192) for i in range(2)]
        sxw2 = [c.sb(f"f1sx2{i}", [128, 512], F32, OFF_W + 32768 + i * 2048)
                for i in range(NCH)]
        hqld = [c.sb(f"f1hq{i}", [128, 1024], BF16, OFF_C + i * 2048)
                for i in range(2)]
        qf2 = c.sb("f1qf2", [128, 1024], F16, OFF_C + 4096)
        rowb2 = c.sb("f1row2", [1, T], F32, OFF_C + 16384)

        for ch in range(NCH):
            sl = slice(512 * ch, 512 * (ch + 1))
            v.emit(lambda e, sl=sl: e.tensor_scalar_max(
                rbc2[:, sl], macc2[:, sl], 1e-5))
        v.emit(lambda e: e.tensor_scalar(
            rowb2[0:1, :], rbc2[0:1, :], 1.0 / 127.0,
            cells[0:1, 2:3], ALU.mult, ALU.mult))
        tsx2 = c.v.cur()
        g.wait(tsx2)
        for ch in range(NCH):
            g.emit(lambda e, ch=ch: e.partition_broadcast(
                sxw2[ch][:], rowb2[0:1, 512 * ch:512 * (ch + 1)]), tok=True)
        v.emit(lambda e: e.reciprocal(rbc2[:], rbc2[:]))
        v.emit(lambda e: e.tensor_scalar_mul(rbc2[:], rbc2[:], 127.0))
        c.barrier()
        for half in range(2):
            th0 = half * 1024
            hsl = slice(th0, th0 + 1024)
            h_free = None
            for j in range(H1_TILES):
                tld = y.emit(lambda e, j=j, hsl=hsl: e.dma_start(
                    out=hqld[0][:], in_=self.scr["s_h1h"]
                    [128 * j:128 * (j + 1), hsl]), tok=True)
                tld2 = y.emit(lambda e, j=j, hsl=hsl: e.dma_start(
                    out=hqld[1][:], in_=self.scr["s_h1l"]
                    [128 * j:128 * (j + 1), hsl]), tok=True)
                v.wait(tld2)
                v.emit(lambda e: e.tensor_add(xrb2[:], hqld[0][:], hqld[1][:]))
                tyw = c.v.cur()
                y.wait(tyw)
                v.emit(lambda e, th0=th0: e.tensor_mul(
                    xrb2[:], xrb2[:], rbc2[:, th0:th0 + 1024]))
                v.emit(lambda e: e.tensor_scalar(
                    qf2[:], xrb2[:], 1536.0, 1408.0, ALU.add, ALU.max))
                v.emit(lambda e, j=j: e.tensor_scalar(
                    xq2[j][:], qf2[:], 1663.0, 1536.0, ALU.min, ALU.subtract))
            tq2 = c.v.cur()
            t.wait(tq2)
            ev2 = [None] * 4
            wslot2 = [None, None]
            for d in range(D_TILES):
                slot = d % 2
                gd.wait(wslot2[slot])
                tqw = gd.emit(lambda e, sl_=slot, d=d: e.dma_start(
                    out=w2b[sl_][:].rearrange("p (k n) -> p k n", n=128),
                    in_=self.scr["s_wq2"].rearrange("(k p) n -> p k n", p=128)
                    [:, :, 128 * d:128 * (d + 1)]), tok=True)
                t.wait(tqw)
                for ch in range(2):
                    gsl = slice(th0 + 512 * ch, th0 + 512 * (ch + 1))
                    lsl = slice(512 * ch, 512 * (ch + 1))
                    bank = (d % 2) * 2 + ch
                    psb = self.ps[bank]
                    t.wait(ev2[bank])
                    for k in range(H1_TILES):
                        t.emit(lambda e, k=k, lsl=lsl, sl_=slot, psb=psb:
                               e.matmul(
                            psb[:], w2b[sl_][:, 128 * k:128 * (k + 1)],
                            xq2[k][:, lsl], start=(k == 0),
                            stop=(k == H1_TILES - 1)))
                    tmm = c.t.cur()
                    if ch == 1:
                        wslot2[slot] = tmm
                    v.wait(tmm)
                    v.emit(lambda e, ch=ch, hsl=hsl, psb=psb, half=half:
                           e.tensor_mul(
                        xrb2[:, 0:512], psb[:],
                        sxw2[2 * half + ch][:]))
                    tva = v.emit(lambda e, d=d, gsl=gsl: e.tensor_add(
                        self.x2[d][:, gsl], self.x2[d][:, gsl],
                        xrb2[:, 0:512]), tok=True)
                    ev2[bank] = tva
        c.barrier()

    def ph_store(self):
        c = self.c
        c.barrier()
        tks = None
        for i in range(D_TILES):
            tks = c.y.emit(lambda e, i=i: e.dma_start(
                out=self.x_out[128 * i:128 * (i + 1), :],
                in_=self.x2[i][:]), tok=True)
        c.y.stream.append(["wait", (c.y, tks[1]), None])

    def ph_ffn0(self):
        c = self.c
        t, v, s, g, y, sd, gd = c.t, c.v, c.s, c.g, c.y, c.sd, c.gd
        hh_t, hl_t = [], []
        for j in range(H0_TILES):
            if j < 16:
                off = OFF_A + j * 4096
            elif j < 22:
                off = OFF_C + (j - 16) * 4096
            else:
                off = OFF_W + 28672 + (j - 22) * 4096
            hh_t.append(c.sb(f"f0hh{j}", [128, 1024], BF16, off))
            hl_t.append(c.sb(f"f0hl{j}", [128, 1024], BF16, off + 2048))
        wg_b = [c.sb(f"f0wg{i}", [128, 1024], BF16, OFF_W + i * 2048)
                for i in range(2)]
        wu_b = [c.sb(f"f0wu{i}", [128, 1024], BF16, OFF_W + 4096 + i * 2048)
                for i in range(2)]
        wd_b = [c.sb(f"f0wd{i}", [128, HF0], BF16, OFF_W + 8192 + i * 6144)
                for i in range(2)]
        sg_t = [c.sb(f"f0sg{i}", [128, 512], F32, OFF_W + 20480 + i * 2048)
                for i in range(2)]
        hf_t = [c.sb(f"f0hf{i}", [128, 512], F32, OFF_W + 24576 + i * 2048)
                for i in range(2)]
        xt_t = [c.sb(f"f0xt{i}", [128, 512], F32, OFF_W + 36864 + i * 2048)
                for i in range(2)]
        x1_t = [c.sb(f"f0x1{i}", [128, 512], F32, OFF_W + 40960 + i * 2048)
                for i in range(2)]

        def wband_3d(whbm, rows, c0, cw):
            kt = rows // 128
            return whbm.rearrange("(j p) n -> p j n", p=128)[:, :, c0:c0 + cw]

        ev_tok = [None] * 8
        sg_free = [None, None]
        x_free = [None, None]
        for half in range(2):
            th0 = half * 1024
            slot_tok = [None, None]
            for j in range(H0_TILES):
                slot = j % 2
                gd.wait(slot_tok[slot])
                tkg = gd.emit(lambda e, sl_=slot, j=j: e.dma_start(
                    out=wg_b[sl_][:].rearrange("p (j n) -> p j n", n=128),
                    in_=wband_3d(self.hbm["wg0"], D, j * 128, 128)), tok=True)
                tku = gd.emit(lambda e, sl_=slot, j=j: e.dma_start(
                    out=wu_b[sl_][:].rearrange("p (j n) -> p j n", n=128),
                    in_=wband_3d(self.hbm["wu0"], D, j * 128, 128)), tok=True)
                t.wait(tku)
                bank0 = (j % 2) * 4
                for ch in range(2):
                    sl = slice(th0 + 512 * ch, th0 + 512 * (ch + 1))
                    psg, psu = self.ps[bank0 + ch], self.ps[bank0 + 2 + ch]
                    t.wait(ev_tok[bank0 + ch])
                    t.wait(ev_tok[bank0 + 2 + ch])
                    for k in range(D_TILES):
                        for pi, mv in ((0, self.xnh), (1, self.xnl)):
                            t.emit(lambda e, k=k, sl=sl, sl_=slot, mv=mv,
                                   psg=psg, pi=pi: e.matmul(
                                psg[:], wg_b[sl_][:, 128 * k:128 * (k + 1)],
                                mv[k][:, sl], start=(k == 0 and pi == 0),
                                stop=(k == D_TILES - 1 and pi == 1)))
                    for k in range(D_TILES):
                        for pi, mv in ((0, self.xnh), (1, self.xnl)):
                            t.emit(lambda e, k=k, sl=sl, sl_=slot, mv=mv,
                                   psu=psu, pi=pi: e.matmul(
                                psu[:], wu_b[sl_][:, 128 * k:128 * (k + 1)],
                                mv[k][:, sl], start=(k == 0 and pi == 0),
                                stop=(k == D_TILES - 1 and pi == 1)))
                    tmm = c.t.cur()
                    s.wait(tmm)
                    s.wait(sg_free[ch])
                    tsg = s.emit(lambda e, psg=psg, ch=ch: e.activation(
                        out=sg_t[ch][:], in_=psg[:], func=AF.Silu,
                        scale=1.0 / 127.0), tok=True)
                    v.wait(tsg)
                    v.emit(lambda e, psu=psu, ch=ch: e.scalar_tensor_tensor(
                        hf_t[ch][:], psu[:], 1.0 / 127.0, sg_t[ch][:],
                        ALU.mult, ALU.mult))
                    hsl = slice(512 * ch, 512 * (ch + 1))
                    v.emit(lambda e, j=j, hsl=hsl, ch=ch: e.tensor_copy(
                        hh_t[j][:, hsl], hf_t[ch][:]))
                    tkl = v.emit(lambda e, j=j, hsl=hsl, ch=ch:
                                 e.scalar_tensor_tensor(
                        hl_t[j][:, hsl], hh_t[j][:, hsl], -1.0, hf_t[ch][:],
                        ALU.mult, ALU.add), tok=True)
                    sg_free[ch] = tkl
                    ev_tok[bank0 + ch] = tkl
                    ev_tok[bank0 + 2 + ch] = tkl
                slot_tok[slot] = c.t.cur()
            c.barrier()
            slot_tok = [None, None]
            for d in range(D_TILES):
                slot = d % 2
                gd.wait(slot_tok[slot])
                tkd = gd.emit(lambda e, sl_=slot, d=d: e.dma_start(
                    out=wd_b[sl_][:].rearrange("p (j n) -> p j n", n=128),
                    in_=wband_3d(self.hbm["wd0"], HF0, d * 128, 128)), tok=True)
                t.wait(tkd)
                for ch in range(2):
                    sl = slice(th0 + 512 * ch, th0 + 512 * (ch + 1))
                    hsl = slice(512 * ch, 512 * (ch + 1))
                    bank = 4 + (d % 2) * 2 + ch
                    psd = self.ps[bank]
                    t.wait(ev_tok[bank])
                    y.wait(x_free[ch])
                    tx = y.emit(lambda e, d=d, sl=sl, ch=ch: e.dma_start(
                        out=xt_t[ch][:],
                        in_=self.hbm["x"][128 * d:128 * (d + 1), sl]),
                        tok=True)
                    for k in range(H0_TILES):
                        for pi, mvt in ((0, hh_t), (1, hl_t)):
                            t.emit(lambda e, k=k, hsl=hsl, sl_=slot, mvt=mvt,
                                   psd=psd, pi=pi: e.matmul(
                                psd[:], wd_b[sl_][:, 128 * k:128 * (k + 1)],
                                mvt[k][:, hsl], start=(k == 0 and pi == 0),
                                stop=(k == H0_TILES - 1 and pi == 1)))
                    tmm = c.t.cur()
                    v.wait(tmm)
                    v.wait(tx)
                    tv1 = v.emit(lambda e, psd=psd, ch=ch: e.scalar_tensor_tensor(
                        x1_t[ch][:], psd[:], 1.0 / 127.0, xt_t[ch][:], ALU.mult,
                        ALU.add), tok=True)
                    ev_tok[bank] = tv1
                    x_free[ch] = tv1
                    sd.wait(tv1)
                    tst = sd.emit(lambda e, d=d, sl=sl, ch=ch: e.dma_start(
                        out=self.scr["s_x1"][128 * d:128 * (d + 1), sl],
                        in_=x1_t[ch][:]), tok=True)
                    v.wait(tst)
                slot_tok[slot] = c.t.cur()
            if half == 0:
                vtk = c.t.cur()
                v.wait(vtk)
        c.barrier()

    def build(self, upto):
        import contextlib
        nc = self.nc

        def scoped(name, fn, *a):
            Eng.phase[0] = name
            out = fn(*a)
            Eng.phase[0] = None
            return out
        self._scoped = scoped
        self.consts_setup()
        scoped("ln0", self.ph_ln0_xn)
        if upto == "xn":
            self.finalize_dev()
            return
        scoped('ffn0', self.ph_ffn0)
        if upto == "ffn0":
            self.finalize_dev()
            return
        self.win_ev = [None] * 8
        scoped('win', self.ph_win)
        if upto == "win":
            self.finalize_dev()
            return
        self.wdt_ev = [None] * 8
        scoped('wdt', self.ph_wdt_bc)
        if upto == "wdt":
            self.finalize_dev()
            return
        scoped('scan', self.ph_scan)
        if upto == "scan":
            self.finalize_dev()
            return
        scoped('wout0', self.ph_wout0)
        if upto == "l0":
            self.finalize_dev()
            return
        if upto not in ("l0hw",):
            pass
        if upto == "l0hw":
            c = self.c
            c.barrier()
            tks = None
            for i in range(D_TILES):
                tks = c.y.emit(lambda e, i=i: e.dma_start(
                    out=self.x_out[128 * i:128 * (i + 1), :],
                    in_=self.x1f[i][:]), tok=True)
            c.y.stream.append(["wait", (c.y, tks[1]), None])
            return
        scoped('ln1', self.ph_ln1)
        if upto == "ln1":
            self.finalize_dev()
            return
        scoped('proj1', self.ph_proj1)
        if upto == "proj1":
            self.finalize_dev()
            return
        scoped('attn', self.ph_attn)
        if upto == "attn":
            self.finalize_dev()
            return
        scoped('wout1', self.ph_wout1)
        if upto == "wout1":
            self.finalize_dev()
            return
        if upto == "wout1hw":
            c = self.c
            c.barrier()
            tks = None
            for i in range(D_TILES):
                tks = c.y.emit(lambda e, i=i: e.dma_start(
                    out=self.x_out[128 * i:128 * (i + 1), :],
                    in_=self.x2[i][:]), tok=True)
            c.y.stream.append(["wait", (c.y, tks[1]), None])
            return
        self.hw_probe = upto if upto.endswith("hw") else None
        scoped('ffn1', self.ph_ffn1)
        if upto in ("wq1hw", "h1hw", "cellhw"):
            return
        scoped('store', self.ph_store)

    def finalize_dev(self):
        c = self.c
        c.barrier()
        tk = c.y.emit(lambda e: e.dma_start(
            out=self.x_out[0:128, 0:128], in_=self.ident[:]), tok=True)
        c.y.stream.append(["wait", (c.y, tk[1]), None])


def prep_core_inputs(inputs, core):
    a = {k: np.asarray(v) for k, v in inputs.items()}
    tp = np.ascontiguousarray
    im = {
        "x": tp(a["L0_attn_x"][core].T),
        "ln0_scale": tp(a["L0_attn_norm_scale"].reshape(D, 1)),
        "ln0_bias": tp(a["L0_attn_norm_bias"].reshape(D, 1)),
        "w_in": tp(a["L0_attn_W_in"]),
        "conv_w": tp(a["L0_attn_conv_w"]),
        "conv_b": tp(a["L0_attn_conv_b"].reshape(DI, 1)),
        "w_dt": tp(a["L0_attn_W_dt"]),
        "b_dt": tp(a["L0_attn_b_dt"].reshape(DI, 1)),
        "a_log": tp(a["L0_attn_A_log"]),
        "d_param": tp(a["L0_attn_D_param"].reshape(DI, 1)),
        "w_bc": tp(np.concatenate([a["L0_attn_W_B"], a["L0_attn_W_C"]],
                                  axis=1)),
        "w_out0": tp(a["L0_attn_W_out"]),
        "wg0": tp(a["L0_ffn_w_gate"].T),
        "wu0": tp(a["L0_ffn_w_up"].T),
        "wd0": tp(a["L0_ffn_w_down"].T),
        "w_kvd": tp(a["L1_attn_w_kv_down"]),
        "w_kvu": tp(a["L1_attn_w_kv_up"]),
        "w_q": tp(a["L1_attn_w_q"]),
        "w_out1": tp(a["L1_attn_w_out"]),
        "w1": tp(a["L1_ffn_w1"].T),
        "cv1_w": tp(a["L1_ffn_conv_w"].reshape(HF1, 3)),
        "cv1_b": tp(a["L1_ffn_conv_b"].reshape(HF1, 1)),
        "w2": tp(a["L1_ffn_w2"].T),
    }
    return im


_NC_CACHE = {}


def kernel(**inputs):
    if "all" not in _NC_CACHE:
        _NC_CACHE["all"] = build_program("all")
    nc, _ = _NC_CACHE["all"]
    in_maps = [prep_core_inputs(inputs, b) for b in range(8)]
    res = run_bass_kernel_spmd(nc, in_maps, core_ids=list(range(8)))
    out = np.stack([res.results[b]["x_out"].T for b in range(8)], axis=0)
    return np.ascontiguousarray(out.astype(np.float32))
